# version 09
# baseline (speedup 1.0000x reference)
import numpy as np
import ml_dtypes
import concourse.bass as bass
import concourse.mybir as mybir
from concourse.bass_utils import run_bass_kernel_spmd
from contextlib import ExitStack

F32 = mybir.dt.float32
BF16 = mybir.dt.bfloat16
I32 = mybir.dt.int32
U32 = mybir.dt.uint32
AF = mybir.ActivationFunctionType
ALU = mybir.AluOpType
AX = mybir.AxisListType

B, L, D, H, HD = 4, 4096, 1024, 16, 64
T = B * L
TPC = T // 8
DC = D // 128
F = 4096
E = 16
CAP = 1280
CPAD = 256
MSHIFT = 32.0
EPS = 1e-5
NEGBIG = -1e30

BF = ml_dtypes.bfloat16


class Sched:

    ENGINES = ("sync", "gpsimd", "scalar", "vector", "tensor")

    def __init__(self, nc, ctx):
        self.nc = nc
        self.ctx = ctx
        self.ops = {e: [] for e in self.ENGINES}
        self.cnt = {}

    def sem(self, name):
        h = self.ctx.enter_context(self.nc.semaphore(name))
        self.cnt[name] = [h, 0]
        return name

    def now(self, sem):
        return (self.cnt[sem][0], self.cnt[sem][1])

    def emit(self, eng, fn, waits=(), inc=None, inc_k=1):
        ws = []
        for w in waits:
            if isinstance(w, str):
                h, n = self.cnt[w]
                if n > 0:
                    ws.append((h, n))
            else:
                if w[1] > 0:
                    ws.append(w)
        pair = None
        if inc is not None:
            c = self.cnt[inc]
            c[1] += inc_k
            pair = (c[0], inc_k)
        self.ops[eng].append((fn, ws, pair))

    def dma(self, eng, out, in_, waits=(), inc=None):
        self.emit(eng, lambda e, o=out, i=in_: e.dma_start(out=o, in_=i),
                  waits=waits, inc=inc, inc_k=16)

    def build(self):
        block = self.ctx.enter_context(self.nc.Block())

        def section(eng_ops):
            def run(eng):
                for fn, ws, pair in eng_ops:
                    for h, n in ws:
                        eng.wait_ge(h, n)
                    inst = fn(eng)
                    if pair is not None:
                        assert inst is not None
                        inst.then_inc(pair[0], pair[1])
            return run

        block.sync(section(self.ops["sync"]))
        block.gpsimd(section(self.ops["gpsimd"]))
        block.scalar(section(self.ops["scalar"]))
        block.vector(section(self.ops["vector"]))
        block.tensor(section(self.ops["tensor"]))


def emit_ln(S, src, dst, nchunk, tw, cons, psum, pe, act, dve, wait_src,
            gamma=None, beta=None, dst_cast=None):
    ones128, ones1 = cons["ones128"], cons["ones1"]
    sq, st4, tmp = cons["sq"], cons["st4"], cons["tmp"]
    epsc = cons["epsc"]
    p_s, p_q, p_a, p_b = psum["s"], psum["q"], psum["a"], psum["b"]
    nf = 128 * nchunk
    for c in range(nchunk):
        S.emit("tensor", lambda e, c=c: e.matmul(p_s, lhsT=ones128[:], rhs=src(c),
                                                 start=(c == 0), stop=(c == nchunk - 1)),
               waits=[wait_src] if c == 0 else (), inc=pe if c == nchunk - 1 else None)
    qmarks = []
    for c in range(nchunk):
        w = [wait_src] if c == 0 else ([qmarks[c - 2]] if c >= 2 else [])
        S.emit("scalar", lambda e, c=c: e.activation(sq[c % 2][:, 0:tw], src(c), AF.Square),
               waits=w, inc=act)
        aw = S.now(act)
        S.emit("tensor", lambda e, c=c: e.matmul(p_q, lhsT=ones128[:], rhs=sq[c % 2][:, 0:tw],
                                                 start=(c == 0), stop=(c == nchunk - 1)),
               waits=[aw], inc=pe)
        qmarks.append(S.now(pe))
    stat_done = S.now(pe)
    nmu = st4[0:1, 0 * tw:1 * tw]
    ex2 = st4[0:1, 1 * tw:2 * tw]
    var = st4[0:1, 2 * tw:3 * tw]
    bb = st4[0:1, 3 * tw:4 * tw]
    S.emit("vector", lambda e: e.tensor_scalar_mul(nmu, p_s, -1.0 / nf), waits=[stat_done],
           inc=dve)
    S.emit("vector", lambda e: e.tensor_scalar_mul(ex2, p_q, 1.0 / nf), inc=dve)
    S.emit("vector", lambda e: e.tensor_tensor(var, nmu, nmu, ALU.mult),
           waits=[S.now(dve)], inc=dve)
    S.emit("vector", lambda e: e.tensor_tensor(var, ex2, var, ALU.subtract),
           waits=[S.now(dve)], inc=dve)
    dv = S.now(dve)
    S.emit("scalar", lambda e: e.activation(var, var, AF.Ln, bias=epsc, scale=1.0), waits=[dv],
           inc=act)
    S.emit("scalar", lambda e: e.activation(var, var, AF.Exp, bias=0.0, scale=-0.5),
           waits=[S.now(act)], inc=act)
    sa = S.now(act)
    S.emit("vector", lambda e: e.tensor_tensor(bb, nmu, var, ALU.mult), waits=[sa], inc=dve)
    dv2 = S.now(dve)
    S.emit("tensor", lambda e: e.matmul(p_a, lhsT=ones1[:], rhs=var, start=True, stop=True),
           waits=[dv2])
    S.emit("tensor", lambda e: e.matmul(p_b, lhsT=ones1[:], rhs=bb, start=True, stop=True),
           inc=pe)
    ab = S.now(pe)
    for c in range(nchunk):
        w = [ab] if c == 0 else ()
        if gamma is None:
            S.emit("vector", lambda e, c=c: e.tensor_tensor(dst(c), src(c), p_a, ALU.mult), waits=w)
            S.emit("vector", lambda e, c=c: e.tensor_tensor(dst(c), dst(c), p_b, ALU.add), inc=dve)
        else:
            w = list(w) + ([S.now(act)] if c >= 1 else [])
            S.emit("vector", lambda e, c=c: e.tensor_tensor(tmp[:, 0:tw], src(c), p_a, ALU.mult), waits=w)
            S.emit("vector", lambda e, c=c: e.tensor_tensor(tmp[:, 0:tw], tmp[:, 0:tw], p_b, ALU.add),
                   inc=dve)
            S.emit("vector", lambda e, c=c: e.tensor_scalar(dst_cast(c), tmp[:, 0:tw],
                                                            gamma(c), beta(c),
                                                            ALU.mult, ALU.add),
                   inc=dve)
    return S.now(dve)


def build_l2(tw=256, debug=False):
    nt = TPC // tw
    nc = bass.Bass("TRN2", target_bir_lowering=False)
    xT = nc.dram_tensor("xT", [D, TPC], F32, kind="ExternalInput").ap()
    ctxT = nc.dram_tensor("ctxT", [D, TPC], F32, kind="ExternalInput").ap()
    g0T = nc.dram_tensor("g0T", [128, DC], F32, kind="ExternalInput").ap()
    b0T = nc.dram_tensor("b0T", [128, DC], F32, kind="ExternalInput").ap()
    fc1q = nc.dram_tensor("fc1q", [D, F], I32, kind="ExternalInput").ap()
    sc1 = nc.dram_tensor("sc1", [DC, F], F32, kind="ExternalInput").ap()
    b1T = nc.dram_tensor("b1T", [128, F // 128], F32, kind="ExternalInput").ap()
    fc2q = nc.dram_tensor("fc2q", [F, D], I32, kind="ExternalInput").ap()
    sc2 = nc.dram_tensor("sc2", [F // 128, D], F32, kind="ExternalInput").ap()
    b2T = nc.dram_tensor("b2T", [128, DC], F32, kind="ExternalInput").ap()
    x2T = nc.dram_tensor("x2T", [D, TPC], F32, kind="ExternalOutput").ap()
    if debug:
        dbg_y1 = nc.dram_tensor("dbg_y1", [D, TPC], F32, kind="ExternalOutput").ap()
        dbg_x1 = nc.dram_tensor("dbg_x1", [D, TPC], F32, kind="ExternalOutput").ap()
        dbg_h = nc.dram_tensor("dbg_h", [D, TPC], BF16, kind="ExternalOutput").ap()
        dbg_y = nc.dram_tensor("dbg_y", [F, TPC], BF16, kind="ExternalOutput").ap()
        dbg_w1 = nc.dram_tensor("dbg_w1", [D, F], BF16, kind="ExternalOutput").ap()
        dbg_w2 = nc.dram_tensor("dbg_w2", [F, D], BF16, kind="ExternalOutput").ap()

    with ExitStack() as ctx:
        En = ctx.enter_context
        S = Sched(nc, ctx)
        w1 = En(nc.sbuf_tensor("w1", [128, DC, F], BF16))
        w2 = En(nc.sbuf_tensor("w2", [128, F // 128, D], BF16))
        qbuf = [En(nc.sbuf_tensor(f"qbuf{i}", [128, 512], F32)) for i in range(2)]
        scrow = [En(nc.sbuf_tensor(f"scrow{i}", [1, 512], F32)) for i in range(2)]
        ya = En(nc.sbuf_tensor("ya", [128, DC, tw], F32))
        x1 = En(nc.sbuf_tensor("x1", [128, DC, tw], F32))
        hb = En(nc.sbuf_tensor("hb", [128, DC, tw], BF16))
        yb = En(nc.sbuf_tensor("yb", [128, F // 128, tw], BF16))
        sq = [En(nc.sbuf_tensor(f"sq{i}", [128, tw], F32)) for i in range(2)]
        st4 = En(nc.sbuf_tensor("st4", [1, 4 * tw], F32))
        tmp = En(nc.sbuf_tensor("tmp", [128, tw], F32))
        fo = En(nc.sbuf_tensor("fo", [128, tw], F32))
        ones128 = En(nc.sbuf_tensor("ones128", [128, 1], F32))
        ones1 = En(nc.sbuf_tensor("ones1", [1, 128], F32))
        gg = En(nc.sbuf_tensor("gg", [128, DC], F32))
        bb0 = En(nc.sbuf_tensor("bb0", [128, DC], F32))
        bb1 = En(nc.sbuf_tensor("bb1", [128, F // 128], F32))
        hb2 = En(nc.sbuf_tensor("hb2", [128, DC], F32))
        p_ping = En(nc.psum_tensor("p_ping", [128, 512], F32))
        p_pong = En(nc.psum_tensor("p_pong", [128, 512], F32))
        p_s = En(nc.psum_tensor("p_s", [1, 512], F32))
        p_q = En(nc.psum_tensor("p_q", [1, 512], F32))
        p_y = [En(nc.psum_tensor(f"p_y{i}", [128, 512], F32)) for i in range(2)]
        p_z = [En(nc.psum_tensor(f"p_z{i}", [128, 512], F32)) for i in range(2)]

        ld = S.sem("ld")
        gld = S.sem("gld")
        pe = S.sem("pe")
        act = S.sem("act")
        dve = S.sem("dve")
        sto = S.sem("sto")

        epsc = En(nc.sbuf_tensor("epsc", [1, 1], F32))
        cons = dict(ones128=ones128, ones1=ones1, sq=sq, st4=st4, tmp=tmp,
                    epsc=epsc[0:1, 0:1])
        psd = dict(s=p_s[0:1, 0:tw], q=p_q[0:1, 0:tw],
                   a=p_ping[:, 0:tw], b=p_pong[:, 0:tw])

        S.emit("vector", lambda e: e.memset(ones128[:], 1.0), inc=dve)
        S.emit("vector", lambda e: e.memset(epsc[:], EPS), inc=dve)
        S.emit("vector", lambda e: e.memset(ones1[:], 1.0), inc=dve)
        S.dma("sync", gg[:], g0T, inc=ld)
        S.dma("sync", bb0[:], b0T, inc=ld)
        S.dma("sync", bb1[:], b1T, inc=ld)
        S.dma("sync", hb2[:], b2T, inc=ld)
        S.emit("vector", lambda e: e.tensor_scalar_mul(hb2[:], hb2[:], 0.5),
               waits=[S.now(ld)], inc=dve)
        constv = S.now(dve)

        dmarks = []

        def dequant(qdram, scdram, ngrp, ncol, wdst):
            nfs = ncol // 512
            for g in range(ngrp):
                for fs in range(nfs):
                    i = len(dmarks)
                    pb = i % 2
                    ppsum = p_ping if pb == 0 else p_pong
                    wprev = [dmarks[i - 2]] if i >= 2 else []
                    S.dma("gpsimd", qbuf[pb][:],
                          qdram[g * 128:(g + 1) * 128, fs * 512:(fs + 1) * 512],
                          waits=wprev, inc=gld)
                    S.dma("sync", scrow[pb][:],
                          scdram[g:g + 1, fs * 512:(fs + 1) * 512],
                          waits=wprev, inc=ld)
                    gl, ll = S.now(gld), S.now(ld)
                    S.emit("tensor",
                           lambda e, ppsum=ppsum, pb=pb: e.matmul(
                               ppsum[:], lhsT=ones1[:], rhs=scrow[pb][:],
                               start=True, stop=True),
                           waits=[ll, constv] + wprev, inc=pe)
                    pv = S.now(pe)
                    S.emit("vector",
                           lambda e, ppsum=ppsum, pb=pb, g=g, fs=fs: e.scalar_tensor_tensor(
                               wdst[:, g, fs * 512:(fs + 1) * 512],
                               qbuf[pb][:], -128.0, ppsum[:], ALU.add, ALU.mult),
                           waits=[pv, gl], inc=dve)
                    dmarks.append(S.now(dve))

        dequant(fc1q, sc1, DC, F, w1)
        dequant(fc2q, sc2, F // 128, D, w2)
        wready = S.now(dve)
        if debug:
            S.dma("sync", dbg_w1.rearrange("(g p) f -> p g f", p=128), w1[:],
                  waits=[wready], inc=sto)
            S.dma("sync", dbg_w2.rearrange("(g p) f -> p g f", p=128), w2[:],
                  waits=[wready], inc=sto)

        for it in range(nt):
            tsl = slice(it * tw, (it + 1) * tw)
            lw = [S.now(dve), S.now(sto)]
            S.dma("sync", ya[:], xT[:, tsl].rearrange("(c p) t -> p c t", p=128),
                  waits=lw, inc=ld)
            xv = S.now(ld)
            S.emit("gpsimd",
                   lambda e, tsl=tsl: e.dma_start(
                       out=ya[:], in_=ctxT[:, tsl].rearrange("(c p) t -> p c t", p=128),
                       accum_op=ALU.add),
                   waits=[xv], inc=gld, inc_k=16)
            y1v = S.now(gld)
            if debug:
                S.dma("sync", dbg_y1[:, tsl].rearrange("(c p) t -> p c t", p=128), ya[:],
                      waits=[y1v], inc=sto)
            emit_ln(S, lambda c: ya[:, c, :], lambda c: x1[:, c, :], DC, tw,
                    cons, psd, pe, act, dve, y1v)
            x1v = S.now(dve)
            if debug:
                S.dma("sync", dbg_x1[:, tsl].rearrange("(c p) t -> p c t", p=128), x1[:],
                      waits=[x1v], inc=sto)
            hv = emit_ln(S, lambda c: x1[:, c, :], lambda c: tmp[:, 0:tw], DC, tw,
                         cons, psd, pe, act, dve, x1v,
                         gamma=lambda c: gg[:, c:c + 1], beta=lambda c: bb0[:, c:c + 1],
                         dst_cast=lambda c: hb[:, c, :])
            if debug:
                S.dma("sync", dbg_h[:, tsl].rearrange("(c p) t -> p c t", p=128), hb[:],
                      waits=[hv], inc=sto)
            for fg in range(F // 128):
                pyb = p_y[fg % 2]
                wy = [S.now(act)] if fg >= 2 else [hv, wready]
                for g in range(DC):
                    S.emit("tensor",
                           lambda e, g=g, fg=fg, pyb=pyb: e.matmul(
                               pyb[:, 0:tw], lhsT=w1[:, g, fg * 128:(fg + 1) * 128],
                               rhs=hb[:, g, :], start=(g == 0), stop=(g == DC - 1)),
                           waits=wy if g == 0 else (),
                           inc=pe if g == DC - 1 else None)
                pv = S.now(pe)
                S.emit("scalar",
                       lambda e, fg=fg, pyb=pyb: e.activation(
                           yb[:, fg, :], pyb[:, 0:tw], AF.Silu,
                           bias=bb1[:, fg:fg + 1], scale=1.0),
                       waits=[pv], inc=act)
            yv = S.now(act)
            if debug:
                S.dma("sync", dbg_y[:, tsl].rearrange("(c p) t -> p c t", p=128), yb[:],
                      waits=[yv], inc=sto)
            for dc_ in range(DC):
                pz = p_z[dc_ % 2]
                wz = [S.now(dve)] if dc_ >= 2 else [yv]
                for fg in range(F // 128):
                    S.emit("tensor",
                           lambda e, dc_=dc_, fg=fg, pz=pz: e.matmul(
                               pz[:, 0:tw], lhsT=w2[:, fg, dc_ * 128:(dc_ + 1) * 128],
                               rhs=yb[:, fg, :], start=(fg == 0), stop=(fg == F // 128 - 1)),
                           waits=(wz + [yv]) if fg == 0 else (),
                           inc=pe if fg == F // 128 - 1 else None)
                zv = S.now(pe)
                S.emit("vector",
                       lambda e, dc_=dc_, pz=pz: e.tensor_scalar(
                           tmp[:, 0:tw], pz[:, 0:tw], 0.5, hb2[:, dc_:dc_ + 1],
                           ALU.mult, ALU.add),
                       waits=[zv], inc=dve)
                S.emit("vector",
                       lambda e, dc_=dc_: e.tensor_tensor(fo[:, 0:tw], tmp[:, 0:tw],
                                                          x1[:, dc_, :], ALU.add),
                       waits=[S.now(dve)], inc=dve)
                S.emit("vector",
                       lambda e, dc_=dc_: e.tensor_tensor(ya[:, dc_, :], fo[:, 0:tw],
                                                          x1[:, dc_, :], ALU.add),
                       waits=[S.now(dve)], inc=dve)
            y2v = S.now(dve)
            emit_ln(S, lambda c: ya[:, c, :], lambda c: x1[:, c, :], DC, tw,
                    cons, psd, pe, act, dve, y2v)
            ov = S.now(dve)
            S.dma("sync", x2T[:, tsl].rearrange("(c p) t -> p c t", p=128), x1[:],
                  waits=[ov], inc=sto)
        S.emit("sync", lambda e: e.nop(), waits=[S.now(sto)])
        S.build()
    return nc



def build_l1():
    nc = bass.Bass("TRN2", target_bir_lowering=False)
    xT = nc.dram_tensor("xT", [512, L], F32, kind="ExternalInput").ap()
    xtok = nc.dram_tensor("xtok", [L, 512], F32, kind="ExternalInput").ap()
    trimask = nc.dram_tensor("trimask", [128, 4 * 512], F32, kind="ExternalInput").ap()
    ctxT = nc.dram_tensor("ctxT", [512, L], F32, kind="ExternalOutput").ap()

    NQB = L // 512
    with ExitStack() as ctx:
        En = ctx.enter_context
        S = Sched(nc, ctx)
        xh = En(nc.sbuf_tensor("xh", [128, 4, L], BF16))
        vt = [En(nc.sbuf_tensor(f"vt{h}", [128, 32, 65], BF16)) for h in range(8)]
        tri = En(nc.sbuf_tensor("tri", [128, 4 * 512], BF16))
        pt = [En(nc.sbuf_tensor(f"pt{i}", [128, 512], BF16)) for i in range(4)]
        cobuf = En(nc.sbuf_tensor("cobuf", [128, 4, L], F32))
        ctmp = En(nc.sbuf_tensor("ctmp", [64, 512], F32))
        linv = En(nc.sbuf_tensor("linv", [1, 512], F32))
        ones1 = En(nc.sbuf_tensor("ones1", [1, 128], F32))
        mshift = En(nc.sbuf_tensor("mshift", [128, 1], F32))
        p_sA = [En(nc.psum_tensor(f"p_sA{i}", [128, 512], F32)) for i in range(2)]
        p_sB = [En(nc.psum_tensor(f"p_sB{i}", [128, 512], F32)) for i in range(2)]
        p_cA = En(nc.psum_tensor("p_cA", [65, 512], F32))
        p_cB = En(nc.psum_tensor("p_cB", [65, 512], F32))
        p_lb = En(nc.psum_tensor("p_lb", [64, 512], F32))

        ld = S.sem("ld")
        gld = S.sem("gld")
        pe = S.sem("pe")
        act = S.sem("act")
        dve = S.sem("dve")
        sto = S.sem("sto")

        S.emit("vector", lambda e: e.memset(ones1[:], 1.0), inc=dve)
        S.emit("vector", lambda e: e.memset(mshift[:], -MSHIFT), inc=dve)
        for h in range(8):
            S.emit("vector", lambda e, h=h: e.memset(vt[h][:], 1.0), inc=dve)
        memv = S.now(dve)
        S.dma("gpsimd", tri[:], trimask, inc=gld)
        S.dma("gpsimd", xh[:], xT.rearrange("(c p) t -> p c t", p=128), inc=gld)
        xhv = S.now(gld)
        for h in range(8):
            S.emit("gpsimd",
                   lambda e, h=h: e.dma_start(
                       out=vt[h][:, :, 0:64],
                       in_=xtok[:, 64 * h:64 * h + 64].rearrange("(c p) d -> p c d", p=128)),
                   waits=[memv], inc=gld, inc_k=16)
        vtv = S.now(gld)

        for hp in range(4):
            hA, hB = 2 * hp, 2 * hp + 1
            for qb in range(NQB):
                nkc = 4 * (qb + 1)
                evac_prev = S.now(dve)
                expmarks = {}
                pvmarks = {}
                for kc in range(nkc):
                    par = kc % 2
                    psA, psB = p_sA[par], p_sB[par]
                    wq = [xhv]
                    wq.append(expmarks[kc - 2] if kc >= 2 else S.now(act))
                    S.emit("tensor",
                           lambda e, hp=hp, qb=qb, kc=kc, psA=psA: e.matmul(
                               psA[:], lhsT=xh[0:64, hp, kc * 128:(kc + 1) * 128],
                               rhs=xh[0:64, hp, qb * 512:(qb + 1) * 512],
                               start=True, stop=True, tile_position=(0, 0)),
                           waits=wq)
                    S.emit("tensor",
                           lambda e, hp=hp, qb=qb, kc=kc, psB=psB: e.matmul(
                               psB[:], lhsT=xh[64:128, hp, kc * 128:(kc + 1) * 128],
                               rhs=xh[64:128, hp, qb * 512:(qb + 1) * 512],
                               start=True, stop=True, tile_position=(64, 0)),
                           inc=pe)
                    qkm = S.now(pe)
                    we = [qkm]
                    we.append(pvmarks[kc - 2] if kc >= 2 else S.now(pe))
                    S.emit("scalar",
                           lambda e, par=par, psA=psA: e.activation(
                               pt[2 * par + 0][:], psA[:], AF.Exp, bias=mshift[:, 0:1], scale=0.125),
                           waits=we + [memv], inc=act)
                    S.emit("scalar",
                           lambda e, par=par, psB=psB: e.activation(
                               pt[2 * par + 1][:], psB[:], AF.Exp, bias=mshift[:, 0:1], scale=0.125),
                           inc=act)
                    expmarks[kc] = S.now(act)
                    pvw = [expmarks[kc]]
                    if kc >= 4 * qb:
                        j = kc - 4 * qb
                        S.emit("vector",
                               lambda e, j=j, par=par: e.tensor_tensor(
                                   pt[2 * par + 0][:], pt[2 * par + 0][:],
                                   tri[:, j * 512:(j + 1) * 512], ALU.mult),
                               waits=[expmarks[kc], xhv])
                        S.emit("vector",
                               lambda e, j=j, par=par: e.tensor_tensor(
                                   pt[2 * par + 1][:], pt[2 * par + 1][:],
                                   tri[:, j * 512:(j + 1) * 512], ALU.mult),
                               inc=dve)
                        pvw = [S.now(dve)]
                    wv = pvw + ([vtv, evac_prev] if kc == 0 else [])
                    S.emit("tensor",
                           lambda e, hA=hA, kc=kc, par=par, nkc=nkc: e.matmul(
                               p_cA[:], lhsT=vt[hA][:, kc, :], rhs=pt[2 * par + 0][:],
                               start=(kc == 0), stop=(kc == nkc - 1)),
                           waits=wv)
                    S.emit("tensor",
                           lambda e, hB=hB, kc=kc, par=par, nkc=nkc: e.matmul(
                               p_cB[:], lhsT=vt[hB][:, kc, :], rhs=pt[2 * par + 1][:],
                               start=(kc == 0), stop=(kc == nkc - 1)),
                           inc=pe)
                    pvmarks[kc] = S.now(pe)
                ctxm = S.now(pe)
                for (pc, hloc) in ((p_cA, hA), (p_cB, hB)):
                    e0 = (hloc % 2) * 64
                    S.emit("vector", lambda e, pc=pc: e.reciprocal(linv[:], pc[64:65, :]),
                           waits=[ctxm, S.now(pe)], inc=dve)
                    lv = S.now(dve)
                    S.emit("tensor",
                           lambda e: e.matmul(p_lb[:], lhsT=ones1[0:1, 0:64], rhs=linv[:],
                                              start=True, stop=True),
                           waits=[lv], inc=pe)
                    lbv = S.now(pe)
                    S.emit("vector", lambda e, pc=pc: e.tensor_copy(ctmp[:], pc[0:64, :]),
                           waits=[lbv], inc=dve)
                    S.emit("vector",
                           lambda e, hp=hp, qb=qb, e0=e0: e.tensor_tensor(
                               cobuf[e0:e0 + 64, hp, qb * 512:(qb + 1) * 512],
                               ctmp[:], p_lb[:], ALU.mult),
                           waits=[S.now(dve)], inc=dve)
        allv = S.now(dve)
        S.dma("sync", ctxT.rearrange("(hp e r) t -> (e r) hp t", hp=4, e=2),
              cobuf[:], waits=[allv], inc=sto)
        S.emit("sync", lambda e: e.nop(), waits=[S.now(sto)])
        S.build()
    return nc


def _l1_trimask():
    m = np.zeros((128, 4, 512), np.float32)
    for j in range(4):
        for p in range(128):
            m[p, j, 128 * j + p:] = 1.0
    return np.ascontiguousarray(m.reshape(128, 4 * 512))


def run_l1(x):
    nc = _get("l1", build_l1)
    tm = _l1_trimask()
    maps = []
    for c in range(8):
        b, hg = c // 2, c % 2
        sl = slice(hg * 512, (hg + 1) * 512)
        maps.append(dict(xT=_f(x[b].T[sl]), xtok=_f(x[b][:, sl]), trimask=tm))
    res = run_bass_kernel_spmd(nc, maps, core_ids=list(range(8)))
    ctx_ = np.empty((B, L, D), np.float32)
    for c in range(8):
        b, hg = c // 2, c % 2
        ctx_[b, :, hg * 512:(hg + 1) * 512] = np.asarray(res.results[c]["ctxT"]).T
    return ctx_



def build_l3(debug=False):
    nc = bass.Bass("TRN2", target_bir_lowering=False)
    x2T = nc.dram_tensor("x2T", [D, L], F32, kind="ExternalInput").ap()
    WqT = nc.dram_tensor("WqT", [D, 512], F32, kind="ExternalInput").ap()
    WkT = nc.dram_tensor("WkT", [D, 512], F32, kind="ExternalInput").ap()
    WvT = nc.dram_tensor("WvT", [D, 512], F32, kind="ExternalInput").ap()
    bqT = nc.dram_tensor("bqT", [128, 4], F32, kind="ExternalInput").ap()
    bkT = nc.dram_tensor("bkT", [128, 4], F32, kind="ExternalInput").ap()
    bvrow = nc.dram_tensor("bvrow", [1, 512], F32, kind="ExternalInput").ap()
    bimp = nc.dram_tensor("bimp", [1, 2048], F32, kind="ExternalInput").ap()
    id16 = nc.dram_tensor("id16", [16, 16], F32, kind="ExternalInput").ap()
    ctxT = nc.dram_tensor("ctxT", [512, L], F32, kind="ExternalOutput").ap()
    mrow = nc.dram_tensor("mrow", [2048], F32)
    if debug:
        dbg_q = nc.dram_tensor("dbg_q", [512, L], BF16, kind="ExternalOutput").ap()
        dbg_k = nc.dram_tensor("dbg_k", [512, L], BF16, kind="ExternalOutput").ap()
        dbg_v = nc.dram_tensor("dbg_v", [128, 32, 8, 65], BF16, kind="ExternalOutput").ap()
        dbg_bias = nc.dram_tensor("dbg_bias", [128, 32], F32, kind="ExternalOutput").ap()

    NQB = L // 512
    NKC = L // 128
    with ExitStack() as ctx:
        En = ctx.enter_context
        S = Sched(nc, ctx)
        xh = En(nc.sbuf_tensor("xh", [128, DC, L], BF16))
        qh = En(nc.sbuf_tensor("qh", [128, 4, L], BF16))
        kh = En(nc.sbuf_tensor("kh", [128, 4, L], BF16))
        vtok = En(nc.sbuf_tensor("vtok", [128, 32, 8, 65], BF16))
        wbuf = En(nc.sbuf_tensor("wbuf", [128, DC, 512], BF16))
        bq_s = En(nc.sbuf_tensor("bq_s", [128, 4], F32))
        bk_s = En(nc.sbuf_tensor("bk_s", [128, 4], F32))
        bvr = En(nc.sbuf_tensor("bvr", [1, 512], F32))
        bv_b = En(nc.sbuf_tensor("bv_b", [128, 512], F32))
        pt = [En(nc.sbuf_tensor(f"pt{i}", [128, 512], BF16)) for i in range(4)]
        ctile = [En(nc.sbuf_tensor(f"ctile{i}", [128, 512], F32)) for i in range(2)]
        ctmp = En(nc.sbuf_tensor("ctmp", [64, 512], F32))
        linv = En(nc.sbuf_tensor("linv", [1, 512], F32))
        ones1 = En(nc.sbuf_tensor("ones1", [1, 128], F32))
        mshift = En(nc.sbuf_tensor("mshift", [128, 1], F32))
        negbig = En(nc.sbuf_tensor("negbig", [1, 1], F32))
        brow = En(nc.sbuf_tensor("brow", [1, 2048], F32))
        wrow = En(nc.sbuf_tensor("wrow", [1, 2048], F32))
        trow = En(nc.sbuf_tensor("trow", [1, 2048], BF16))
        bstat = En(nc.sbuf_tensor("bstat", [1, 4], F32))
        mask16 = En(nc.sbuf_tensor("mask16", [16, 128], F32))
        idb = En(nc.sbuf_tensor("idb", [16, 16], F32))
        biasall = En(nc.sbuf_tensor("biasall", [128, 32], F32))
        p_sA = [En(nc.psum_tensor(f"p_sA{i}", [128, 512], F32)) for i in range(2)]
        p_sB = [En(nc.psum_tensor(f"p_sB{i}", [128, 512], F32)) for i in range(2)]
        p_cA = En(nc.psum_tensor("p_cA", [65, 512], F32))
        p_cB = En(nc.psum_tensor("p_cB", [65, 512], F32))
        p_lb = En(nc.psum_tensor("p_lb", [128, 512], F32))

        ld = S.sem("ld")
        gld = S.sem("gld")
        pe = S.sem("pe")
        act = S.sem("act")
        dve = S.sem("dve")
        sto = S.sem("sto")

        S.emit("vector", lambda e: e.memset(ones1[:], 1.0), inc=dve)
        S.emit("vector", lambda e: e.memset(mshift[:], -MSHIFT), inc=dve)
        S.emit("vector", lambda e: e.memset(negbig[:], NEGBIG), inc=dve)
        S.emit("vector", lambda e: e.memset(vtok[:], 1.0), inc=dve)
        S.emit("vector", lambda e: e.memset(biasall[:], -MSHIFT), inc=dve)
        memv = S.now(dve)
        S.dma("gpsimd", xh[:], x2T.rearrange("(c p) t -> p c t", p=128), inc=gld)
        S.dma("gpsimd", wbuf[:], WqT.rearrange("(c p) o -> p c o", p=128), inc=gld)
        xhv = S.now(gld)
        S.dma("sync", bq_s[:], bqT, inc=ld)
        S.dma("sync", bk_s[:], bkT, inc=ld)
        S.dma("sync", bvr[:], bvrow, inc=ld)
        S.dma("sync", brow[:], bimp, inc=ld)
        S.dma("sync", idb[:], id16, inc=ld)
        ldv = S.now(ld)

        m1 = bstat[0:1, 0:1]
        m2 = bstat[0:1, 1:2]
        m3 = bstat[0:1, 2:3]
        nb = negbig[0:1, 0:1]
        def vstep(fn, extra=()):
            S.emit("vector", fn, waits=[S.now(dve)] + list(extra), inc=dve)
        vstep(lambda e: e.reduce_max(m1, brow[:], axis=AX.X), extra=[ldv, memv])
        vstep(lambda e: e.tensor_single_scalar(trow[:], brow[:], m1, ALU.is_ge))
        vstep(lambda e: e.scalar_tensor_tensor(wrow[:], trow[:], nb, brow[:],
                                               ALU.mult, ALU.add))
        vstep(lambda e: e.reduce_max(m2, wrow[:], axis=AX.X))
        vstep(lambda e: e.tensor_single_scalar(trow[:], wrow[:], m2, ALU.is_ge))
        vstep(lambda e: e.scalar_tensor_tensor(wrow[:], trow[:], nb, wrow[:],
                                               ALU.mult, ALU.add))
        vstep(lambda e: e.reduce_max(m3, wrow[:], axis=AX.X))
        vstep(lambda e: e.tensor_single_scalar(trow[:], brow[:], m3, ALU.is_ge))
        vstep(lambda e: e.tensor_scalar_mul(wrow[:], trow[:], nb))
        mv = S.now(dve)
        S.dma("sync", mrow.rearrange("(a b) -> a b", a=1), wrow[:], waits=[mv], inc=ld)
        S.dma("sync", mask16[:], mrow.rearrange("(c p) -> c p", c=16),
              waits=[S.now(ld)], inc=ld)
        mk16 = S.now(ld)
        S.emit("tensor", lambda e: e.transpose(p_lb[:, 0:16], mask16[:], idb[:]),
               waits=[mk16], inc=pe)
        trv = S.now(pe)
        S.emit("vector", lambda e: e.tensor_scalar(biasall[:, 0:16], p_lb[:, 0:16],
                                                   1.0, mshift[:, 0:1], ALU.mult, ALU.add),
               waits=[trv], inc=dve)
        biasv = S.now(dve)

        pmarks = []
        for wi, (wdram, dst, bias) in enumerate(((WqT, qh, bq_s), (WkT, kh, bk_s))):
            if wi == 1:
                S.dma("gpsimd", wbuf[:], wdram.rearrange("(c p) o -> p c o", p=128),
                      waits=[S.now(pe)], inc=gld)
            wldv = S.now(gld)
            for oc in range(4):
                for ts in range(8):
                    i = len(pmarks)
                    pp = p_sA[i % 2]
                    w0 = ([xhv, ldv] if i < 2 else [pmarks[i - 2]]) + [wldv]
                    for dcx in range(DC):
                        S.emit("tensor",
                               lambda e, dcx=dcx, oc=oc, ts=ts, pp=pp: e.matmul(
                                   pp[:], lhsT=wbuf[:, dcx, oc * 128:(oc + 1) * 128],
                                   rhs=xh[:, dcx, ts * 512:(ts + 1) * 512],
                                   start=(dcx == 0), stop=(dcx == DC - 1)),
                               waits=w0 if dcx == 0 else (),
                               inc=pe if dcx == DC - 1 else None)
                    pv = S.now(pe)
                    S.emit("vector",
                           lambda e, dst=dst, bias=bias, oc=oc, ts=ts, pp=pp: e.tensor_scalar_add(
                               dst[:, oc, ts * 512:(ts + 1) * 512], pp[:], bias[:, oc:oc + 1]),
                           waits=[pv], inc=dve)
                    pmarks.append(S.now(dve))
        S.dma("gpsimd", wbuf[:], WvT.rearrange("(c p) o -> p c o", p=128),
              waits=[S.now(pe)], inc=gld)
        wvldv = S.now(gld)
        S.emit("tensor", lambda e: e.matmul(p_sB[0][:], lhsT=ones1[:], rhs=bvr[:],
                                            start=True, stop=True),
               waits=[ldv, memv], inc=pe)
        S.emit("vector", lambda e: e.tensor_copy(bv_b[:], p_sB[0][:]),
               waits=[S.now(pe)], inc=dve)
        bvv = S.now(dve)
        for tc in range(32):
            i = len(pmarks)
            pp = p_sA[i % 2]
            w0 = [pmarks[i - 2], wvldv]
            for dcx in range(DC):
                S.emit("tensor",
                       lambda e, dcx=dcx, tc=tc, pp=pp: e.matmul(
                           pp[:], lhsT=xh[:, dcx, tc * 128:(tc + 1) * 128],
                           rhs=wbuf[:, dcx, :], start=(dcx == 0), stop=(dcx == DC - 1)),
                       waits=w0 if dcx == 0 else (),
                       inc=pe if dcx == DC - 1 else None)
            pv = S.now(pe)
            S.emit("vector",
                   lambda e, tc=tc, pp=pp: e.scalar_tensor_tensor(
                       vtok[:, tc, :, 0:64], pp.ap().rearrange("p (h d) -> p h d", h=8),
                       1.0, bv_b.ap().rearrange("p (h d) -> p h d", h=8),
                       ALU.mult, ALU.add),
                   waits=[pv, bvv], inc=dve)
            pmarks.append(S.now(dve))
        projv = S.now(dve)
        if debug:
            S.dma("sync", dbg_q.rearrange("(c p) t -> p c t", p=128), qh[:],
                  waits=[projv], inc=sto)
            S.dma("sync", dbg_k.rearrange("(c p) t -> p c t", p=128), kh[:],
                  waits=[projv], inc=sto)
            S.dma("sync", dbg_v, vtok[:], waits=[projv], inc=sto)
            S.dma("sync", dbg_bias, biasall[:], waits=[projv, biasv], inc=sto)

        stomarks = {}
        for hp in range(4):
            hA, hB = 2 * hp, 2 * hp + 1
            for qb in range(NQB):
                evac_prev = S.now(dve)
                expmarks = {}
                pvmarks = {}
                for kc in range(NKC):
                    par = kc % 2
                    psA, psB = p_sA[par], p_sB[par]
                    wq0 = [expmarks[kc - 2]] if kc >= 2 else [projv, S.now(act)]
                    S.emit("tensor",
                           lambda e, hp=hp, qb=qb, kc=kc, psA=psA: e.matmul(
                               psA[:], lhsT=kh[0:64, hp, kc * 128:(kc + 1) * 128],
                               rhs=qh[0:64, hp, qb * 512:(qb + 1) * 512],
                               start=True, stop=True, tile_position=(0, 0)),
                           waits=wq0)
                    S.emit("tensor",
                           lambda e, hp=hp, qb=qb, kc=kc, psB=psB: e.matmul(
                               psB[:], lhsT=kh[64:128, hp, kc * 128:(kc + 1) * 128],
                               rhs=qh[64:128, hp, qb * 512:(qb + 1) * 512],
                               start=True, stop=True, tile_position=(64, 0)),
                           inc=pe)
                    qkm = S.now(pe)
                    we = [qkm, biasv]
                    we.append(pvmarks[kc - 2] if kc >= 2 else S.now(pe))
                    S.emit("scalar",
                           lambda e, par=par, psA=psA, kc=kc: e.activation(
                               pt[2 * par + 0][:], psA[:], AF.Exp,
                               bias=biasall[:, kc:kc + 1], scale=0.125),
                           waits=we, inc=act)
                    S.emit("scalar",
                           lambda e, par=par, psB=psB, kc=kc: e.activation(
                               pt[2 * par + 1][:], psB[:], AF.Exp,
                               bias=biasall[:, kc:kc + 1], scale=0.125),
                           inc=act)
                    expmarks[kc] = S.now(act)
                    wv0 = [expmarks[kc]] + ([evac_prev] if kc == 0 else [])
                    S.emit("tensor",
                           lambda e, hA=hA, kc=kc, par=par: e.matmul(
                               p_cA[:], lhsT=vtok[:, kc, hA, :], rhs=pt[2 * par + 0][:],
                               start=(kc == 0), stop=(kc == NKC - 1)),
                           waits=wv0)
                    S.emit("tensor",
                           lambda e, hB=hB, kc=kc, par=par: e.matmul(
                               p_cB[:], lhsT=vtok[:, kc, hB, :], rhs=pt[2 * par + 1][:],
                               start=(kc == 0), stop=(kc == NKC - 1)),
                           inc=pe)
                    pvmarks[kc] = S.now(pe)
                ctxm = S.now(pe)
                it = hp * NQB + qb
                par_c = it % 2
                wst = [stomarks[it - 2]] if it >= 2 else []
                for e_, pc in ((0, p_cA), (1, p_cB)):
                    S.emit("vector", lambda e, pc=pc: e.reciprocal(linv[:], pc[64:65, :]),
                           waits=[ctxm, S.now(pe)], inc=dve)
                    lv = S.now(dve)
                    S.emit("tensor",
                           lambda e: e.matmul(p_lb[0:64, 0:512], lhsT=ones1[0:1, 0:64],
                                              rhs=linv[:], start=True, stop=True),
                           waits=[lv, S.now(dve)], inc=pe)
                    lbv = S.now(pe)
                    S.emit("vector", lambda e, pc=pc: e.tensor_copy(ctmp[:], pc[0:64, :]),
                           waits=[lbv], inc=dve)
                    S.emit("vector",
                           lambda e, e_=e_, par_c=par_c: e.tensor_tensor(
                               ctile[par_c][e_ * 64:(e_ + 1) * 64, :],
                               ctmp[:], p_lb[0:64, 0:512], ALU.mult),
                           waits=wst + [S.now(dve)], inc=dve)
                cv = S.now(dve)
                S.dma("sync", ctxT[hp * 128:(hp + 1) * 128, qb * 512:(qb + 1) * 512],
                      ctile[par_c][:], waits=[cv], inc=sto)
                stomarks[it] = S.now(sto)
        S.emit("sync", lambda e: e.nop(), waits=[S.now(sto)])
        S.build()
    return nc


def run_l3(x2, Wq, bq, Wk, bk, Wv, bv, bimp):
    nc = _get("l3", build_l3)
    i16 = np.eye(16, dtype=np.float32)
    maps = []
    for c in range(8):
        b, hg = c // 2, c % 2
        rs = slice(hg * 512, (hg + 1) * 512)
        maps.append(dict(
            x2T=_f(x2[b].T),
            WqT=_f(np.asarray(Wq)[rs, :].T), WkT=_f(np.asarray(Wk)[rs, :].T),
            WvT=_f(np.asarray(Wv)[rs, :].T),
            bqT=_f(np.asarray(bq)[rs].reshape(4, 128).T),
            bkT=_f(np.asarray(bk)[rs].reshape(4, 128).T),
            bvrow=_f(np.asarray(bv)[rs].reshape(1, 512)),
            bimp=_f(np.asarray(bimp).reshape(1, 2048)),
            id16=i16,
        ))
    res = run_bass_kernel_spmd(nc, maps, core_ids=list(range(8)))
    ctx_ = np.empty((B, L, D), np.float32)
    for c in range(8):
        b, hg = c // 2, c % 2
        ctx_[b, :, hg * 512:(hg + 1) * 512] = np.asarray(res.results[c]["ctxT"]).T
    return ctx_



def build_l4(tw=512):
    nt = TPC // tw
    nc = bass.Bass("TRN2", target_bir_lowering=False)
    x2T = nc.dram_tensor("x2T", [D, TPC], F32, kind="ExternalInput").ap()
    ctxT = nc.dram_tensor("ctxT", [D, TPC], F32, kind="ExternalInput").ap()
    WoT = nc.dram_tensor("WoT", [D, D], F32, kind="ExternalInput").ap()
    boT = nc.dram_tensor("boT", [128, DC], F32, kind="ExternalInput").ap()
    gT = nc.dram_tensor("gT", [128, DC], F32, kind="ExternalInput").ap()
    bT = nc.dram_tensor("bT", [128, DC], F32, kind="ExternalInput").ap()
    gwT = nc.dram_tensor("gwT", [D, E], F32, kind="ExternalInput").ap()
    x3T = nc.dram_tensor("x3T", [D, TPC], F32, kind="ExternalOutput").ap()
    logits = nc.dram_tensor("logits", [TPC, E], F32, kind="ExternalOutput").ap()

    with ExitStack() as ctx:
        En = ctx.enter_context
        S = Sched(nc, ctx)
        wo = En(nc.sbuf_tensor("wo", [128, DC, D], BF16))
        gw = En(nc.sbuf_tensor("gw", [128, DC, E], F32))
        ch = En(nc.sbuf_tensor("ch", [128, DC, tw], BF16))
        x2 = En(nc.sbuf_tensor("x2", [128, DC, tw], F32))
        yy = En(nc.sbuf_tensor("yy", [128, DC, tw], F32))
        zz = En(nc.sbuf_tensor("zz", [128, DC, tw], F32))
        x3 = En(nc.sbuf_tensor("x3", [128, DC, tw], F32))
        lgt = En(nc.sbuf_tensor("lgt", [128, tw // 128, E], F32))
        sq = [En(nc.sbuf_tensor(f"sq{i}", [128, tw], F32)) for i in range(2)]
        st4 = En(nc.sbuf_tensor("st4", [1, 4 * tw], F32))
        tmp = En(nc.sbuf_tensor("tmp", [128, tw], F32))
        epsc = En(nc.sbuf_tensor("epsc", [1, 1], F32))
        ones128 = En(nc.sbuf_tensor("ones128", [128, 1], F32))
        ones1 = En(nc.sbuf_tensor("ones1", [1, 128], F32))
        bo_s = En(nc.sbuf_tensor("bo_s", [128, DC], F32))
        g_s = En(nc.sbuf_tensor("g_s", [128, DC], F32))
        b_s = En(nc.sbuf_tensor("b_s", [128, DC], F32))
        p_ping = En(nc.psum_tensor("p_ping", [128, 512], F32))
        p_pong = En(nc.psum_tensor("p_pong", [128, 512], F32))
        p_s = En(nc.psum_tensor("p_s", [1, 512], F32))
        p_q = En(nc.psum_tensor("p_q", [1, 512], F32))
        p_o = [En(nc.psum_tensor(f"p_o{i}", [128, 512], F32)) for i in range(2)]
        p_g = [En(nc.psum_tensor(f"p_g{i}", [128, E], F32)) for i in range(2)]

        ld = S.sem("ld")
        gld = S.sem("gld")
        pe = S.sem("pe")
        act = S.sem("act")
        dve = S.sem("dve")
        sto = S.sem("sto")

        cons = dict(ones128=ones128, ones1=ones1, sq=sq, st4=st4, tmp=tmp,
                    epsc=epsc[0:1, 0:1])
        psd = dict(s=p_s[0:1, 0:tw], q=p_q[0:1, 0:tw],
                   a=p_ping[:, 0:tw], b=p_pong[:, 0:tw])

        S.emit("vector", lambda e: e.memset(ones128[:], 1.0), inc=dve)
        S.emit("vector", lambda e: e.memset(ones1[:], 1.0), inc=dve)
        S.emit("vector", lambda e: e.memset(epsc[:], EPS), inc=dve)
        memv = S.now(dve)
        S.dma("gpsimd", wo[:], WoT.rearrange("(c p) o -> p c o", p=128), inc=gld)
        S.dma("sync", gw[:], gwT.rearrange("(c p) o -> p c o", p=128), inc=ld)
        S.dma("sync", bo_s[:], boT, inc=ld)
        S.dma("sync", g_s[:], gT, inc=ld)
        S.dma("sync", b_s[:], bT, inc=ld)
        wov = S.now(gld)
        ldv = S.now(ld)

        for it in range(nt):
            tsl = slice(it * tw, (it + 1) * tw)
            lw = [S.now(dve), S.now(sto), S.now(pe), S.now(act)]
            S.dma("sync", x2[:], x2T[:, tsl].rearrange("(c p) t -> p c t", p=128),
                  waits=lw, inc=ld)
            S.dma("gpsimd", ch[:], ctxT[:, tsl].rearrange("(c p) t -> p c t", p=128),
                  waits=lw, inc=gld)
            lv, gv = S.now(ld), S.now(gld)
            for oc in range(DC):
                po = p_o[oc % 2]
                wz = [wov, gv, lv, ldv, memv] if oc < 2 else [S.now(dve)]
                for dci in range(DC):
                    S.emit("tensor",
                           lambda e, dci=dci, oc=oc, po=po: e.matmul(
                               po[:, 0:tw], lhsT=wo[:, dci, oc * 128:(oc + 1) * 128],
                               rhs=ch[:, dci, :], start=(dci == 0), stop=(dci == DC - 1)),
                           waits=wz if dci == 0 else (),
                           inc=pe if dci == DC - 1 else None)
                pv = S.now(pe)
                S.emit("vector",
                       lambda e, oc=oc, po=po: e.scalar_tensor_tensor(
                           yy[:, oc, :], po[:, 0:tw], bo_s[:, oc:oc + 1], x2[:, oc, :],
                           ALU.add, ALU.add),
                       waits=[pv], inc=dve)
            yv = S.now(dve)
            zv = emit_ln(S, lambda c: yy[:, c, :], lambda c: tmp[:, 0:tw], DC, tw,
                         cons, psd, pe, act, dve, yv,
                         gamma=lambda c: g_s[:, c:c + 1], beta=lambda c: b_s[:, c:c + 1],
                         dst_cast=lambda c: zz[:, c, :])
            for c in range(DC):
                S.emit("vector",
                       lambda e, c=c: e.tensor_tensor(yy[:, c, :], x2[:, c, :], zz[:, c, :],
                                                      ALU.add),
                       waits=[zv] if c == 0 else (), inc=dve)
            y3v = S.now(dve)
            emit_ln(S, lambda c: yy[:, c, :], lambda c: x3[:, c, :], DC, tw,
                    cons, psd, pe, act, dve, y3v)
            x3v = S.now(dve)
            S.dma("sync", x3T[:, tsl].rearrange("(c p) t -> p c t", p=128), x3[:],
                  waits=[x3v], inc=sto)
            for s_ in range(tw // 128):
                pgb = p_g[s_ % 2]
                for dci in range(DC):
                    w0 = []
                    if dci == 0:
                        w0 = [x3v] + ([S.now(dve)] if s_ >= 2 else [])
                    S.emit("tensor",
                           lambda e, s_=s_, dci=dci, pgb=pgb: e.matmul(
                               pgb[:], lhsT=x3[:, dci, s_ * 128:(s_ + 1) * 128],
                               rhs=gw[:, dci, :], start=(dci == 0), stop=(dci == DC - 1)),
                           waits=w0,
                           inc=pe if dci == DC - 1 else None)
                gvv = S.now(pe)
                S.emit("vector",
                       lambda e, s_=s_, pgb=pgb: e.tensor_copy(lgt[:, s_, :], pgb[:]),
                       waits=[gvv], inc=dve)
            lgv = S.now(dve)
            S.dma("sync", logits[tsl, :].rearrange("(s p) e -> p s e", p=128), lgt[:],
                  waits=[lgv], inc=sto)
        S.emit("sync", lambda e: e.nop(), waits=[S.now(sto)])
        S.build()
    return nc


def run_l4(x2, ctx_, Wo, bo, g, b, gate_w):
    nc = _get("l4", build_l4)
    maps = []
    WoTh = _f(np.asarray(Wo).T)
    boT = _f(np.asarray(bo).reshape(DC, 128).T)
    gT = _f(np.asarray(g).reshape(DC, 128).T)
    bT = _f(np.asarray(b).reshape(DC, 128).T)
    gwTh = _f(np.asarray(gate_w).T)
    for c in range(8):
        sl = slice(c * TPC, (c + 1) * TPC)
        maps.append(dict(x2T=_f(x2[sl].T), ctxT=_f(ctx_[sl].T), WoT=WoTh, boT=boT,
                         gT=gT, bT=bT, gwT=gwTh))
    res = run_bass_kernel_spmd(nc, maps, core_ids=list(range(8)))
    x3 = np.concatenate([np.asarray(r["x3T"]).T for r in res.results], axis=0)
    lg = np.concatenate([np.asarray(r["logits"]) for r in res.results], axis=0)
    return x3, lg



def build_l5(debug=False):
    NT = TPC // 128
    EPC = E // 8
    NCT = CAP // 256
    nc = bass.Bass("TRN2", target_bir_lowering=False)
    x3tok = nc.dram_tensor("x3tok", [TPC, D], F32, kind="ExternalInput").ap()
    lgin = nc.dram_tensor("lgin", [TPC, E], F32, kind="ExternalInput").ap()
    w1d = nc.dram_tensor("w1d", [EPC, D, F], F32, kind="ExternalInput").ap()
    w2d = nc.dram_tensor("w2d", [EPC, F, D], F32, kind="ExternalInput").ap()
    UTd = nc.dram_tensor("UTd", [128, 128], F32, kind="ExternalInput").ap()
    pmaskd = nc.dram_tensor("pmaskd", [128, E], F32, kind="ExternalInput").ap()
    iota16d = nc.dram_tensor("iota16d", [128, E], F32, kind="ExternalInput").ap()
    iotaBIGd = nc.dram_tensor("iotaBIGd", [128, E], F32, kind="ExternalInput").ap()
    bigpd = nc.dram_tensor("bigpd", [128, 1], F32, kind="ExternalInput").ap()
    idd = nc.dram_tensor("idd", [128, 128], F32, kind="ExternalInput").ap()
    x4tok = nc.dram_tensor("x4tok", [TPC, D], F32, kind="ExternalOutput").ap()

    dispD = nc.dram_tensor("dispD", [E * CAP, D], BF16)
    dispRS = nc.dram_tensor("dispRS", [EPC * CAP, D], BF16)
    eoutL = nc.dram_tensor("eoutL", [EPC * CAP, D], F32)
    eoutAG = nc.dram_tensor("eoutAG", [E * CAP, D], F32, addr_space="Shared")
    agin = nc.dram_tensor("agin", [1, 16 * E], F32)
    agout = nc.dram_tensor("agout", [8, 16 * E], F32, addr_space="Shared")

    with ExitStack() as ctx:
        En = ctx.enter_context
        S = Sched(nc, ctx)
        lg = En(nc.sbuf_tensor("lg", [128, NT, E], F32))
        ohb = En(nc.sbuf_tensor("ohb", [128, NT, E], F32))
        psel = En(nc.sbuf_tensor("psel", [128, NT], F32))
        sel = En(nc.sbuf_tensor("sel", [128, NT], F32))
        lloc = En(nc.sbuf_tensor("lloc", [128, NT], F32))
        kf = En(nc.sbuf_tensor("kf", [128, NT], F32))
        kidx = En(nc.sbuf_tensor("kidx", [128, NT], U32))
        cells = En(nc.sbuf_tensor("cells", [128, 16], F32))
        t16 = En(nc.sbuf_tensor("t16", [128, E], F32))
        et = En(nc.sbuf_tensor("et", [128, E], F32))
        offloc = En(nc.sbuf_tensor("offloc", [1, E], F32))
        prefrow = En(nc.sbuf_tensor("prefrow", [1, E], F32))
        crow = En(nc.sbuf_tensor("crow", [1, 16 * E], F32))
        cg = En(nc.sbuf_tensor("cg", [128, E], F32))
        UTs = En(nc.sbuf_tensor("UTs", [128, 128], F32))
        id128f = En(nc.sbuf_tensor("id128f", [128, 128], F32))
        id128b = En(nc.sbuf_tensor("id128b", [128, 128], BF16))
        pmask = En(nc.sbuf_tensor("pmask", [128, E], F32))
        iota16 = En(nc.sbuf_tensor("iota16", [128, E], F32))
        iotaBIG = En(nc.sbuf_tensor("iotaBIG", [128, E], F32))
        bigp = En(nc.sbuf_tensor("bigp", [128, 1], F32))
        ones128 = En(nc.sbuf_tensor("ones128", [128, 1], F32))
        ones1 = En(nc.sbuf_tensor("ones1", [1, 128], F32))
        nbigf = En(nc.sbuf_tensor("nbigf", [128, 1], F32))
        capg = En(nc.sbuf_tensor("capg", [128, 1], F32))
        eps128 = En(nc.sbuf_tensor("eps128", [128, 1], F32))
        zero2k = En(nc.sbuf_tensor("zero2k", [128, D], BF16))
        xsc = [En(nc.sbuf_tensor(f"xsc{i}", [128, D], BF16)) for i in range(2)]
        w1b = En(nc.sbuf_tensor("w1b", [128, DC, F], BF16))
        w2b = En(nc.sbuf_tensor("w2b", [128, F // 128, D], BF16))
        dtok = En(nc.sbuf_tensor("dtok", [128, 2, D], BF16))
        dft = En(nc.sbuf_tensor("dft", [128, DC, 256], BF16))
        hh = En(nc.sbuf_tensor("hh", [128, F // 128, 256], BF16))
        zt = En(nc.sbuf_tensor("zt", [128, DC, 256], F32))
        otok = En(nc.sbuf_tensor("otok", [128, 2, D], F32))
        comb = En(nc.sbuf_tensor("comb", [128, D], F32))
        x3f = [En(nc.sbuf_tensor(f"x3f{i}", [128, D], F32)) for i in range(2)]
        sqt = En(nc.sbuf_tensor("sqt", [128, D], F32))
        p_ra = En(nc.psum_tensor("p_ra", [128, 128], F32))
        p_tb = [En(nc.psum_tensor(f"p_tb{i}", [128, 128], BF16)) for i in range(2)]
        p_tf = [En(nc.psum_tensor(f"p_tf{i}", [128, 128], F32)) for i in range(2)]
        p_f = [En(nc.psum_tensor(f"p_f{i}", [128, 512], F32)) for i in range(2)]

        ld = S.sem("ld")
        gld = S.sem("gld")
        pe = S.sem("pe")
        act = S.sem("act")
        dve = S.sem("dve")
        sto = S.sem("sto")
        cc = S.sem("cc")

        def vstep(fn, extra=()):
            S.emit("vector", fn, waits=[S.now(dve)] + list(extra), inc=dve)

        S.emit("vector", lambda e: e.memset(ones128[:], 1.0), inc=dve)
        S.emit("vector", lambda e: e.memset(ones1[:], 1.0), inc=dve)
        S.emit("vector", lambda e: e.memset(nbigf[:], -1e4), inc=dve)
        S.emit("vector", lambda e: e.memset(capg[:], float(CAP - 1)), inc=dve)
        S.emit("vector", lambda e: e.memset(eps128[:], EPS), inc=dve)
        S.emit("vector", lambda e: e.memset(offloc[:], -1.0), inc=dve)
        S.emit("vector", lambda e: e.memset(zero2k[:], 0.0), inc=dve)
        memv = S.now(dve)
        S.dma("sync", UTs[:], UTd, inc=ld)
        S.dma("sync", id128f[:], idd, inc=ld)
        S.dma("gpsimd", id128b[:], idd, inc=gld)
        S.dma("sync", pmask[:], pmaskd, inc=ld)
        S.dma("sync", iota16[:], iota16d, inc=ld)
        S.dma("sync", iotaBIG[:], iotaBIGd, inc=ld)
        S.dma("sync", bigp[:], bigpd, inc=ld)
        S.dma("sync", lg[:], lgin.rearrange("(c p) e -> p c e", p=128), inc=ld)
        ldv = S.now(ld)
        for r in range(E * CAP // 128):
            S.dma("scalar", dispD[r * 128:(r + 1) * 128, :], zero2k[:],
                  waits=[memv] if r == 0 else (), inc=sto)
        zfv = S.now(sto)

        mx = cells[:, 0:1]
        nmx = cells[:, 1:2]
        ssum = cells[:, 2:3]
        pft = cells[:, 3:4]
        gloc = cells[:, 4:5]
        keep = cells[:, 7:8]
        aa = cells[:, 8:9]
        bb_ = cells[:, 9:10]
        cc_ = cells[:, 10:11]
        rsn = cells[:, 11:12]
        nmu = cells[:, 12:13]
        ssq = cells[:, 13:14]
        ex2 = cells[:, 14:15]
        var = cells[:, 15:16]

        p_cum = p_ra[:, 0:16]
        p_cnt = p_ra[0:1, 16:32]
        p_pr = p_ra[0:1, 32:48]
        p_prb = p_ra[:, 48:64]
        for lt in range(NT):
            lgt = lg[:, lt, :]
            vstep(lambda e, lgt=lgt: e.reduce_max(mx, lgt, axis=AX.X), extra=[ldv])
            vstep(lambda e: e.tensor_scalar_mul(nmx, mx, -1.0))
            S.emit("scalar", lambda e, lgt=lgt: e.activation(et[:], lgt, AF.Exp,
                                                             bias=nmx, scale=1.0),
                   waits=[S.now(dve)], inc=act)
            vstep(lambda e: e.reduce_sum(ssum, et[:], axis=AX.X), extra=[S.now(act)])
            vstep(lambda e, lt=lt: e.reciprocal(psel[:, lt:lt + 1], ssum))
            vstep(lambda e, lgt=lgt: e.tensor_single_scalar(t16[:], lgt, mx, ALU.is_ge))
            vstep(lambda e: e.scalar_tensor_tensor(t16[:], t16[:], nbigf[:, 0:1],
                                                   iotaBIG[:], ALU.mult, ALU.add))
            vstep(lambda e, lt=lt: e.tensor_reduce(sel[:, lt:lt + 1], t16[:], AX.X, ALU.min))
            vstep(lambda e, lt=lt: e.tensor_single_scalar(ohb[:, lt, :], iota16[:],
                                                          sel[:, lt:lt + 1], ALU.is_equal))
            ohv = S.now(dve)
            S.emit("tensor", lambda e, lt=lt: e.matmul(p_cum, lhsT=UTs[:], rhs=ohb[:, lt, :],
                                                       start=True, stop=False),
                   waits=[ohv, ldv])
            S.emit("tensor", lambda e: e.matmul(p_cum, lhsT=ones1[:], rhs=offloc[:],
                                                start=False, stop=True), inc=pe)
            cumv = S.now(pe)
            vstep(lambda e, lt=lt: e.tensor_tensor(t16[:], p_cum, ohb[:, lt, :], ALU.mult),
                  extra=[cumv])
            vstep(lambda e, lt=lt: e.reduce_sum(lloc[:, lt:lt + 1], t16[:], axis=AX.X))
            rdv = S.now(dve)
            S.emit("tensor", lambda e, lt=lt: e.matmul(p_cnt, lhsT=ones128[:],
                                                       rhs=ohb[:, lt, :],
                                                       start=True, stop=True),
                   waits=[rdv], inc=pe)
            cntv = S.now(pe)
            vstep(lambda e: e.tensor_tensor(offloc[:], offloc[:], p_cnt, ALU.add),
                  extra=[cntv])
            vstep(lambda e, lt=lt: e.tensor_copy(crow[0:1, lt * E:(lt + 1) * E], p_cnt))
        p1v = S.now(dve)
        S.dma("sync", agin.ap(), crow[:], waits=[p1v], inc=ld)
        S.emit("gpsimd", lambda e: e.collective_compute(
            "AllGather", ALU.bypass, replica_groups=[list(range(8))],
            ins=[agin.ap()], outs=[agout.ap()]),
            waits=[S.now(ld)], inc=cc)
        S.emit("sync", lambda e: e.nop(), waits=[S.now(cc)])
        S.dma("sync", cg[:], agout.ap().rearrange("r (lt e) -> (r lt) e", e=E), inc=ld)
        cgv = S.now(ld)

        S.emit("tensor", lambda e: e.matmul(p_pr, lhsT=pmask[:, 0:1],
                                            rhs=cg[:], start=True, stop=True),
               waits=[cgv, S.now(dve)], inc=pe)
        vstep(lambda e: e.tensor_copy(prefrow[:], p_pr), extra=[S.now(pe)])
        S.emit("tensor", lambda e: e.matmul(p_prb, lhsT=ones1[:], rhs=prefrow[:],
                                            start=True, stop=True),
               waits=[S.now(dve)], inc=pe)
        prbv = S.now(pe)
        for lt in range(NT):
            vstep(lambda e, lt=lt: e.tensor_tensor(t16[:], p_prb, ohb[:, lt, :], ALU.mult),
                  extra=[prbv])
            vstep(lambda e: e.reduce_sum(pft, t16[:], axis=AX.X))
            vstep(lambda e, lt=lt: e.tensor_tensor(gloc, lloc[:, lt:lt + 1], pft, ALU.add))
            vstep(lambda e: e.tensor_tensor(keep, gloc, capg[:], ALU.is_le))
            vstep(lambda e, lt=lt: e.scalar_tensor_tensor(aa, sel[:, lt:lt + 1],
                                                          float(CAP), gloc,
                                                          ALU.mult, ALU.add))
            vstep(lambda e: e.tensor_tensor(bb_, aa, bigp[:], ALU.subtract))
            vstep(lambda e: e.tensor_tensor(cc_, bb_, keep, ALU.mult))
            vstep(lambda e, lt=lt: e.tensor_tensor(kf[:, lt:lt + 1], cc_, bigp[:], ALU.add))
        vstep(lambda e: e.tensor_copy(kidx[:], kf[:]))
        kiv = S.now(dve)
        if debug:
            dbg_sel = nc.dram_tensor("dbg_sel", [128, 16], F32, kind="ExternalOutput").ap()
            dbg_kf = nc.dram_tensor("dbg_kf", [128, 16], F32, kind="ExternalOutput").ap()
            S.dma("sync", dbg_sel, sel[:], waits=[kiv], inc=sto)
            S.dma("sync", dbg_kf, kf[:], waits=[kiv], inc=sto)

        for lt in range(NT):
            xb = xsc[lt % 2]
            S.emit("gpsimd",
                   lambda e, lt=lt, xb=xb: e.dma_start(
                       out=xb[:], in_=x3tok[lt * 128:(lt + 1) * 128, :]),
                   waits=[S.now(gld)] if lt >= 2 else [], inc=gld, inc_k=16)
            xv = S.now(gld)
            S.emit("gpsimd",
                   lambda e, lt=lt, xb=xb: e.indirect_dma_start(
                       out=dispD[0:128, :],
                       out_offset=bass.IndirectOffsetOnAxis(kidx[:, lt:lt + 1], 0),
                       in_=xb[:], in_offset=None,
                       bounds_check=E * CAP - 1, oob_is_err=False),
                   waits=[kiv, xv] + ([zfv] if lt == 0 else []), inc=gld, inc_k=16)
        scatv = S.now(gld)
        S.emit("sync", lambda e: e.nop(), waits=[scatv, zfv])
        S.emit("gpsimd", lambda e: e.collective_compute(
            "ReduceScatter", ALU.add, replica_groups=[list(range(8))],
            ins=[dispD.ap()], outs=[dispRS.ap()]),
            waits=[scatv, zfv], inc=cc)
        rsv = S.now(cc)

        for exl in range(EPC):
            S.dma("gpsimd", w1b[:], w1d[exl].rearrange("(c p) f -> p c f", p=128),
                  waits=[S.now(pe)], inc=gld)
            S.dma("gpsimd", w2b[:], w2d[exl].rearrange("(g p) dd -> p g dd", p=128),
                  waits=[S.now(pe)], inc=gld)
            wv = S.now(gld)
            for ct in range(NCT):
                base = exl * CAP + ct * 256
                S.dma("sync", dtok[:],
                      dispRS[base:base + 256, :].rearrange("(s p) dd -> p s dd", p=128),
                      waits=[(self_w := S.now(dve))] + [rsv], inc=ld)
                dtv = S.now(ld)
                tmarks = []
                for i in range(16):
                    sc, dcx = i // 8, i % 8
                    ptb = p_tb[i % 2]
                    wt = [dtv] + ([tmarks[i - 2]] if i >= 2 else [S.now(dve)])
                    S.emit("tensor",
                           lambda e, sc=sc, dcx=dcx, ptb=ptb: e.transpose(
                               ptb[:], dtok[:, sc, dcx * 128:(dcx + 1) * 128], id128b[:]),
                           waits=wt, inc=pe)
                    pv = S.now(pe)
                    S.emit("vector",
                           lambda e, sc=sc, dcx=dcx, ptb=ptb: e.tensor_copy(
                               dft[:, dcx, sc * 128:(sc + 1) * 128], ptb[:]),
                           waits=[pv], inc=dve)
                    tmarks.append(S.now(dve))
                dftv = S.now(dve)
                for fg in range(F // 128):
                    pf = p_f[fg % 2]
                    wz = [wv, dftv] if fg == 0 else ([S.now(act)] if fg >= 2 else [])
                    for dcx in range(DC):
                        S.emit("tensor",
                               lambda e, dcx=dcx, fg=fg, pf=pf: e.matmul(
                                   pf[:, 0:256], lhsT=w1b[:, dcx, fg * 128:(fg + 1) * 128],
                                   rhs=dft[:, dcx, :], start=(dcx == 0), stop=(dcx == DC - 1)),
                               waits=wz if dcx == 0 else (),
                               inc=pe if dcx == DC - 1 else None)
                    pv = S.now(pe)
                    S.emit("scalar",
                           lambda e, fg=fg, pf=pf: e.activation(
                               hh[:, fg, :], pf[:, 0:256], AF.Relu),
                           waits=[pv], inc=act)
                hv = S.now(act)
                for dcx in range(DC):
                    pf = p_f[dcx % 2]
                    for fg in range(F // 128):
                        wz = []
                        if fg == 0:
                            wz = [hv] + ([S.now(dve)] if dcx >= 2 else [])
                        S.emit("tensor",
                               lambda e, dcx=dcx, fg=fg, pf=pf: e.matmul(
                                   pf[:, 0:256], lhsT=w2b[:, fg, dcx * 128:(dcx + 1) * 128],
                                   rhs=hh[:, fg, :], start=(fg == 0), stop=(fg == F // 128 - 1)),
                               waits=wz,
                               inc=pe if fg == F // 128 - 1 else None)
                    pv = S.now(pe)
                    S.emit("vector",
                           lambda e, dcx=dcx, pf=pf: e.tensor_copy(zt[:, dcx, :], pf[:, 0:256]),
                           waits=[pv], inc=dve)
                ztv = S.now(dve)
                tmarks2 = []
                for i in range(16):
                    sc, dcx = i // 8, i % 8
                    ptf = p_tf[i % 2]
                    wt = [ztv] + ([tmarks2[i - 2]] if i >= 2 else [])
                    S.emit("tensor",
                           lambda e, sc=sc, dcx=dcx, ptf=ptf: e.transpose(
                               ptf[:], zt[:, dcx, sc * 128:(sc + 1) * 128], id128f[:]),
                           waits=wt, inc=pe)
                    pv = S.now(pe)
                    S.emit("vector",
                           lambda e, sc=sc, dcx=dcx, ptf=ptf: e.tensor_copy(
                               otok[:, sc, dcx * 128:(dcx + 1) * 128], ptf[:]),
                           waits=[pv], inc=dve)
                    tmarks2.append(S.now(dve))
                otv = S.now(dve)
                S.dma("sync", eoutL[base:base + 256, :].rearrange("(s p) dd -> p s dd", p=128),
                      otok[:], waits=[otv], inc=sto)
        S.emit("sync", lambda e: e.nop(), waits=[S.now(sto)])
        S.emit("gpsimd", lambda e: e.collective_compute(
            "AllGather", ALU.bypass, replica_groups=[list(range(8))],
            ins=[eoutL.ap()], outs=[eoutAG.ap()]),
            waits=[S.now(sto)], inc=cc)
        agv = S.now(cc)

        stomarks2 = []
        for lt in range(NT):
            par = lt % 2
            S.dma("sync", x3f[par][:], x3tok[lt * 128:(lt + 1) * 128, :],
                  waits=[S.now(dve)] if lt >= 2 else [], inc=ld)
            xv = S.now(ld)
            vstep(lambda e: e.memset(comb[:], 0.0),
                  extra=[stomarks2[lt - 1]] if lt >= 1 else [])
            S.emit("gpsimd",
                   lambda e, lt=lt: e.indirect_dma_start(
                       out=comb[:], out_offset=None,
                       in_=eoutAG[0:128, :],
                       in_offset=bass.IndirectOffsetOnAxis(kidx[:, lt:lt + 1], 0),
                       bounds_check=E * CAP - 1, oob_is_err=False),
                   waits=[agv, S.now(dve)], inc=gld, inc_k=16)
            gv = S.now(gld)
            vstep(lambda e, lt=lt: e.tensor_scalar_mul(comb[:], comb[:], psel[:, lt:lt + 1]),
                  extra=[gv])
            vstep(lambda e, par=par: e.tensor_tensor(comb[:], comb[:], x3f[par][:], ALU.add),
                  extra=[xv])
            yv = S.now(dve)
            vstep(lambda e: e.reduce_sum(rsn, comb[:], axis=AX.X, negate=True))
            vstep(lambda e: e.tensor_scalar_mul(nmu, rsn, 1.0 / D))
            S.emit("scalar", lambda e: e.activation(sqt[:], comb[:], AF.Square,
                                                    accum_out=ssq),
                   waits=[yv], inc=act)
            vstep(lambda e: e.tensor_scalar_mul(ex2, ssq, 1.0 / D), extra=[S.now(act)])
            vstep(lambda e: e.tensor_tensor(var, nmu, nmu, ALU.mult))
            vstep(lambda e: e.tensor_tensor(var, ex2, var, ALU.subtract))
            S.emit("scalar", lambda e: e.activation(var, var, AF.Ln, bias=eps128[:], scale=1.0),
                   waits=[S.now(dve)], inc=act)
            S.emit("scalar", lambda e: e.activation(var, var, AF.Exp, bias=0.0, scale=-0.5),
                   waits=[S.now(act)], inc=act)
            rv = S.now(act)
            vstep(lambda e: e.tensor_scalar(comb[:], comb[:], nmu, var, ALU.add, ALU.mult),
                  extra=[rv])
            ov = S.now(dve)
            S.dma("sync", x4tok[lt * 128:(lt + 1) * 128, :], comb[:], waits=[ov], inc=sto)
            stomarks2.append(S.now(sto))
        S.emit("sync", lambda e: e.nop(), waits=[S.now(sto)])
        S.build()
    return nc


def _l5_consts(core):
    UT = np.triu(np.ones((128, 128), np.float32))
    pm = np.zeros((128, E), np.float32)
    pm[:core * 16, :] = 1.0
    io = np.broadcast_to(np.arange(E, dtype=np.float32), (128, E)).copy()
    iob = io + 1e4
    bigp = (1e6 + np.arange(128, dtype=np.float32)).reshape(128, 1)
    return UT, pm, io, iob, bigp


def run_l5(x3, logits, w1, w2):
    nc = _get("l5", build_l5)
    w1 = np.asarray(w1)
    w2 = np.asarray(w2)
    maps = []
    for c in range(8):
        sl = slice(c * TPC, (c + 1) * TPC)
        UT, pm, io, iob, bigp = _l5_consts(c)
        maps.append(dict(
            x3tok=_f(x3[sl]), lgin=_f(logits[sl]),
            w1d=_f(w1[2 * c:2 * c + 2]), w2d=_f(w2[2 * c:2 * c + 2]),
            UTd=UT, pmaskd=pm, iota16d=io, iotaBIGd=iob, bigpd=bigp,
            idd=np.eye(128, dtype=np.float32),
        ))
    res = run_bass_kernel_spmd(nc, maps, core_ids=list(range(8)))
    return np.concatenate([np.asarray(r["x4tok"]) for r in res.results], axis=0)
























def build_fused(debug=False):
    nc = bass.Bass("TRN2", target_bir_lowering=False)
    AIn = lambda n, s, d=F32: nc.dram_tensor(n, s, d, kind="ExternalInput").ap()
    xT = AIn("xT", [D, TPC])
    xtok = AIn("xtok", [TPC, D])
    trimask = AIn("trimask", [128, 4 * 512])
    g0T = AIn("g0T", [128, DC]); b0T = AIn("b0T", [128, DC])
    fc1q = AIn("fc1q", [D, F], I32); sc1 = AIn("sc1", [DC, F]); b1T = AIn("b1T", [128, F // 128])
    fc2q = AIn("fc2q", [F, D], I32); sc2 = AIn("sc2", [F // 128, D]); b2T = AIn("b2T", [128, DC])
    WqT = AIn("WqT", [D, 512]); WkT = AIn("WkT", [D, 512]); WvT = AIn("WvT", [D, 512])
    bqT = AIn("bqT", [128, 4]); bkT = AIn("bkT", [128, 4]); bvrow = AIn("bvrow", [1, 512])
    bimp = AIn("bimp", [1, 2048]); id16 = AIn("id16", [16, 16])
    WoT = AIn("WoT", [D, D]); boT = AIn("boT", [128, DC])
    gT = AIn("gT", [128, DC]); bT = AIn("bT", [128, DC]); gwT = AIn("gwT", [D, E])
    w1d = AIn("w1d", [2, D, F]); w2d = AIn("w2d", [2, F, D])
    UTd = AIn("UTd", [128, 128]); pmaskd = AIn("pmaskd", [128, E])
    iota16d = AIn("iota16d", [128, E]); iotaBIGd = AIn("iotaBIGd", [128, E])
    bigpd = AIn("bigpd", [128, 1]); idd = AIn("idd", [128, 128])
    x4tok = nc.dram_tensor("x4tok", [TPC, D], F32, kind="ExternalOutput").ap()
    if debug:
        dbg_ctx1 = nc.dram_tensor("dbg_ctx1", [1024, 2048], F32, kind="ExternalOutput").ap()
        dbg_x2 = nc.dram_tensor("dbg_x2", [D, TPC], F32, kind="ExternalOutput").ap()
        dbg_x3 = nc.dram_tensor("dbg_x3", [TPC, D], F32, kind="ExternalOutput").ap()
        dbg_lg = nc.dram_tensor("dbg_lg", [128, 16, E], F32, kind="ExternalOutput").ap()
        dbg_kf = nc.dram_tensor("dbg_kf", [128, 16], F32, kind="ExternalOutput").ap()

    ag1in = nc.dram_tensor("ag1in", [2048, 2048], BF16)
    ag1out = nc.dram_tensor("ag1out", [8 * 2048, 2048], BF16, addr_space="Shared")
    ag2in = nc.dram_tensor("ag2in", [1024, 2048], F32)
    ag2out = nc.dram_tensor("ag2out", [8 * 1024, 2048], F32, addr_space="Shared")
    ag3in = nc.dram_tensor("ag3in", [1024, 2048], BF16)
    ag3out = nc.dram_tensor("ag3out", [8 * 1024, 2048], BF16, addr_space="Shared")
    ag4in = nc.dram_tensor("ag4in", [1024, 2048], BF16)
    ag4out = nc.dram_tensor("ag4out", [8 * 1024, 2048], BF16, addr_space="Shared")
    my1f = nc.dram_tensor("my1f", [2, 512, 2048], BF16)
    my1t = nc.dram_tensor("my1t", [2, 2048, 512], BF16)
    my2 = nc.dram_tensor("my2", [2, 512, 2048], F32)
    my3 = nc.dram_tensor("my3", [2, 1024, 2048], BF16)
    my4 = nc.dram_tensor("my4", [2, 512, 2048], BF16)
    x2loc = nc.dram_tensor("x2loc", [D, TPC], F32)
    mrow = nc.dram_tensor("mrow", [2048], F32)
    x3tokD = nc.dram_tensor("x3tokD", [TPC, D], F32)
    dispD = nc.dram_tensor("dispD", [E * CAP, D], BF16)
    dispRS1 = nc.dram_tensor("dispRS1", [2 * 512, D], BF16)
    dispRS2 = nc.dram_tensor("dispRS2", [2 * 768, D], BF16)
    eoutLa = nc.dram_tensor("eoutLa", [CAP, D], BF16)
    eoutLb = nc.dram_tensor("eoutLb", [CAP, D], BF16)
    eoutAG = nc.dram_tensor("eoutAG", [E * CAP, D], BF16, addr_space="Shared")
    ag5in = nc.dram_tensor("ag5in", [1, 16 * E], F32)
    ag5out = nc.dram_tensor("ag5out", [8, 16 * E], F32, addr_space="Shared")

    with ExitStack() as ctx:
        En = ctx.enter_context
        S = Sched(nc, ctx)
        ones128 = En(nc.sbuf_tensor("ones128", [128, 1], F32))
        ones1 = En(nc.sbuf_tensor("ones1", [1, 128], F32))
        mshift = En(nc.sbuf_tensor("mshift", [128, 1], F32))
        negbig = En(nc.sbuf_tensor("negbig", [1, 1], F32))
        nbigf = En(nc.sbuf_tensor("nbigf", [128, 1], F32))
        capg = En(nc.sbuf_tensor("capg", [128, 1], F32))
        eps128 = En(nc.sbuf_tensor("eps128", [128, 1], F32))
        epsc_t = En(nc.sbuf_tensor("epsc", [1, 1], F32))
        tri = En(nc.sbuf_tensor("tri", [128, 4 * 512], BF16))
        biasall = En(nc.sbuf_tensor("biasall", [128, 32], F32))
        cells = En(nc.sbuf_tensor("cells", [128, 16], F32))
        st4 = En(nc.sbuf_tensor("st4", [1, 4 * 256], F32))
        sqs = [En(nc.sbuf_tensor(f"sqs{i}", [128, 512], F32)) for i in range(2)]
        tmp = En(nc.sbuf_tensor("tmp", [128, 512], F32))
        fo = En(nc.sbuf_tensor("fo", [128, 256], F32))
        linvA = En(nc.sbuf_tensor("linvA", [1, 512], F32))
        linvB = En(nc.sbuf_tensor("linvB", [1, 512], F32))
        gg = En(nc.sbuf_tensor("gg", [128, DC], F32))
        bb0 = En(nc.sbuf_tensor("bb0", [128, DC], F32))
        bb1 = En(nc.sbuf_tensor("bb1", [128, F // 128], F32))
        hb2 = En(nc.sbuf_tensor("hb2", [128, DC], F32))
        bq_s = En(nc.sbuf_tensor("bq_s", [128, 4], F32))
        bk_s = En(nc.sbuf_tensor("bk_s", [128, 4], F32))
        bvr = En(nc.sbuf_tensor("bvr", [1, 512], F32))
        bv_b = En(nc.sbuf_tensor("bv_b", [128, 512], F32))
        bo_s = En(nc.sbuf_tensor("bo_s", [128, DC], F32))
        g_s = En(nc.sbuf_tensor("g_s", [128, DC], F32))
        b_s = En(nc.sbuf_tensor("b_s", [128, DC], F32))
        gw = En(nc.sbuf_tensor("gw", [128, DC, E], F32))
        UTs = En(nc.sbuf_tensor("UTs", [128, 128], F32))
        id128f = En(nc.sbuf_tensor("id128f", [128, 128], F32))
        id128b = En(nc.sbuf_tensor("id128b", [128, 128], BF16))
        pmask = En(nc.sbuf_tensor("pmask", [128, E], F32))
        iota16 = En(nc.sbuf_tensor("iota16", [128, E], F32))
        iotaBIG = En(nc.sbuf_tensor("iotaBIG", [128, E], F32))
        bigp = En(nc.sbuf_tensor("bigp", [128, 1], F32))
        idb = En(nc.sbuf_tensor("idb", [16, 16], F32))
        mask16 = En(nc.sbuf_tensor("mask16", [16, 128], F32))
        lg = En(nc.sbuf_tensor("lg", [128, 16, E], F32))
        ohb = En(nc.sbuf_tensor("ohb", [128, 16, E], F32))
        psel = En(nc.sbuf_tensor("psel", [128, 16], F32))
        sel = En(nc.sbuf_tensor("sel", [128, 16], F32))
        lloc = En(nc.sbuf_tensor("lloc", [128, 16], F32))
        kf = En(nc.sbuf_tensor("kf", [128, 16], F32))
        kidxS = [En(nc.sbuf_tensor(f"kidxS{i}", [128, 16], U32)) for i in range(2)]
        kidxC = [En(nc.sbuf_tensor(f"kidxC{i}", [128, 16], U32)) for i in range(2)]
        c512 = En(nc.sbuf_tensor("c512", [128, 1], F32))
        c511 = En(nc.sbuf_tensor("c511", [128, 1], F32))
        c8c = En(nc.sbuf_tensor("c8c", [128, 1], F32))
        c7c = En(nc.sbuf_tensor("c7c", [128, 1], F32))
        c8192 = En(nc.sbuf_tensor("c8192", [128, 1], F32))
        t16 = En(nc.sbuf_tensor("t16", [128, E], F32))
        et16 = En(nc.sbuf_tensor("et16", [128, E], F32))
        offloc = En(nc.sbuf_tensor("offloc", [1, E], F32))
        prefrow = En(nc.sbuf_tensor("prefrow", [1, E], F32))
        crow = En(nc.sbuf_tensor("crow", [1, 16 * E], F32))
        cg = En(nc.sbuf_tensor("cg", [128, E], F32))
        ARB = 177 * 1024
        arena = En(nc.sbuf_tensor("arena", [128, ARB // 2], BF16))
        abase = nc.lookup_mloc(arena).addr

        def carve(name, shape, dtype, off):
            import math
            nbytes = 1
            for s_ in shape[1:]:
                nbytes *= s_
            nbytes *= 4 if dtype in (F32, I32, U32) else 2
            assert off + nbytes <= ARB, f"{name}: {off}+{nbytes} > {ARB}"
            return nc.alloc_sbuf_tensor_at(name, shape, dtype, offset=abase + off)

        PB = [En(nc.psum_tensor(f"PB{i}", [128, 1024], F32)) for i in range(2)]
        PC = [En(nc.psum_tensor(f"PC{i}", [128, 512], F32)) for i in range(2)]
        PT = [En(nc.psum_tensor(f"PT{i}", [128, 128], BF16)) for i in range(2)]
        PF = [PB[0][:, 0:512], PB[0][:, 512:1024],
              PB[1][:, 0:512], PB[1][:, 512:1024],
              PC[0][:, :], PC[1][:, :]]

        ld = S.sem("ld")
        gld = S.sem("gld")
        sld = S.sem("sld")
        pe = S.sem("pe")
        act = S.sem("act")
        dve = S.sem("dve")
        sto = S.sem("sto")
        cc = S.sem("cc")

        ranks = {}

        def setup_rank(tag):
            def f(e):
                rb = e.alloc_register(f"rb_{tag}")
                rp = e.alloc_register(f"rp_{tag}")
                e.cc_rank_ld(rb, replica_groups=[[0, 2, 4, 6], [1, 3, 5, 7]])
                e.cc_rank_ld(rp, replica_groups=[[0, 1], [2, 3], [4, 5], [6, 7]])
                ranks[tag] = (e.compute_val(rb), e.compute_val(rp))
            return f
        S.emit("sync", setup_rank("sync"))
        S.emit("gpsimd", setup_rank("gpsimd"))

        def vstep(fn, extra=()):
            S.emit("vector", fn, waits=[S.now(dve)] + list(extra), inc=dve)

        def barrier(exclude=()):
            names = [n for n in ("ld", "gld", "sld", "pe", "act", "dve", "sto", "cc")
                     if n not in exclude]
            vals = [S.now(s) for s in names]
            vals = [v for v in vals if v[1] > 0]
            for eng in ("sync", "gpsimd", "scalar", "vector", "tensor"):
                S.emit(eng, lambda e: e.nop(), waits=vals)

        cons = dict(ones128=ones128, ones1=ones1, sq=sqs, st4=st4, tmp=tmp,
                    epsc=epsc_t[0:1, 0:1])

        for buf, val in ((ones128, 1.0), (ones1, 1.0), (mshift, -MSHIFT),
                         (negbig, NEGBIG), (nbigf, -1e4), (capg, float(CAP - 1)),
                         (eps128, EPS), (epsc_t, EPS), (biasall, -MSHIFT),
                         (offloc, -1.0), (c512, 512.0), (c511, 511.0),
                         (c8c, 8.0), (c7c, 7.0), (c8192, 8192.0)):
            S.emit("vector", lambda e, buf=buf, val=val: e.memset(buf[:], val), inc=dve)
        memv = S.now(dve)
        zrow = carve("zrow", [128, 1024], BF16, 179200)
        S.emit("vector", lambda e: e.memset(zrow[:], 0.0), inc=dve)
        memv = S.now(dve)
        for r_ in range(E * CAP // 128):
            S.dma("scalar", dispD[r_ * 128:(r_ + 1) * 128, :], zrow[:],
                  waits=[memv] if r_ == 0 else (), inc=sld)
        zfv = S.now(sld)
        trif = carve("trif", [128, 2048], F32, 0)
        xcast = carve("xcast", [128, 16, 1024], BF16, 8192)
        xfm0 = carve("xfm0", [128, 8, 2048], BF16, 8192 + 32768)
        brow = carve("brow", [1, 2048], F32, 73728)
        wrow = carve("wrow", [1, 2048], F32, 73728 + 8192)
        trow = carve("trow", [1, 2048], BF16, 73728 + 16384)
        S.dma("sync", trif[:], trimask, inc=ld)
        S.emit("vector", lambda e: e.tensor_copy(tri[:], trif[:]), waits=[S.now(ld)], inc=dve)
        for dst, srcd in ((UTs, UTd), (id128f, idd), (pmask, pmaskd), (iota16, iota16d),
                          (iotaBIG, iotaBIGd), (bigp, bigpd), (idb, id16), (gg, g0T),
                          (bb0, b0T), (bb1, b1T), (hb2, b2T), (bq_s, bqT), (bk_s, bkT),
                          (bvr, bvrow), (bo_s, boT), (g_s, gT), (b_s, bT)):
            S.dma("sync", dst[:], srcd, inc=ld)
        S.dma("sync", gw[:], gwT.rearrange("(c p) o -> p c o", p=128), inc=ld)
        S.dma("sync", brow[:], bimp, inc=ld)
        S.dma("gpsimd", id128b[:], idd, inc=gld)
        ldv0 = S.now(ld)
        S.emit("vector", lambda e: e.tensor_scalar_mul(hb2[:], hb2[:], 0.5),
               waits=[ldv0], inc=dve)
        constv = S.now(dve)
        S.dma("gpsimd", xcast[:], xtok.rearrange("(c p) d -> p c d", p=128), inc=gld)
        g1 = S.now(gld)
        S.dma("sync", ag1in[1024:2048, :].rearrange("r (b f) -> (r b) f", b=2)
              .rearrange("(c p) d -> p c d", p=128), xcast[:], waits=[g1], inc=ld)
        S.dma("gpsimd", xfm0[:], xT.rearrange("(c p) t -> p c t", p=128), inc=gld)
        g2 = S.now(gld)
        S.dma("sync", ag1in[0:1024, :].rearrange("(c p) t -> p c t", p=128), xfm0[:],
              waits=[g2], inc=ld)
        m1 = cells[0:1, 0:1]
        m2 = cells[0:1, 1:2]
        m3 = cells[0:1, 2:3]
        nb = negbig[0:1, 0:1]
        vstep(lambda e: e.reduce_max(m1, brow[:], axis=AX.X), extra=[ldv0, memv])
        vstep(lambda e: e.tensor_single_scalar(trow[:], brow[:], m1, ALU.is_ge))
        vstep(lambda e: e.scalar_tensor_tensor(wrow[:], trow[:], nb, brow[:], ALU.mult, ALU.add))
        vstep(lambda e: e.reduce_max(m2, wrow[:], axis=AX.X))
        vstep(lambda e: e.tensor_single_scalar(trow[:], wrow[:], m2, ALU.is_ge))
        vstep(lambda e: e.scalar_tensor_tensor(wrow[:], trow[:], nb, wrow[:], ALU.mult, ALU.add))
        vstep(lambda e: e.reduce_max(m3, wrow[:], axis=AX.X))
        vstep(lambda e: e.tensor_single_scalar(trow[:], brow[:], m3, ALU.is_ge))
        vstep(lambda e: e.tensor_scalar_mul(wrow[:], trow[:], nb))
        mv = S.now(dve)
        S.dma("sync", mrow.ap().rearrange("(a b) -> a b", a=1), wrow[:], waits=[mv], inc=ld)
        S.dma("sync", mask16[:], mrow.ap().rearrange("(c p) -> c p", c=16),
              waits=[S.now(ld)], inc=ld)
        S.emit("tensor", lambda e: e.transpose(PF[5][:, 0:16], mask16[:], idb[:]),
               waits=[S.now(ld)], inc=pe)
        S.emit("vector", lambda e: e.tensor_scalar(biasall[:, 0:16], PF[5][:, 0:16],
                                                   1.0, mshift[:, 0:1], ALU.mult, ALU.add),
               waits=[S.now(pe), memv], inc=dve)

        S.emit("gpsimd", lambda e: e.collective_compute(
            "AllGather", ALU.bypass, replica_groups=[list(range(8))],
            ins=[ag1in.ap()], outs=[ag1out.ap()]),
            waits=[S.now(ld), S.now(gld)], inc=cc)
        S.emit("sync", lambda e: e.nop(), waits=[S.now(cc)])
        for half in range(2):
            def cpf(e, half=half):
                vb, vp = ranks["sync"]
                v4 = ag1out.ap().rearrange("(b h s r) c -> b h s r c", b=4, h=2, s=4)
                return e.dma_start(out=my1f[half], in_=v4[vb, half, vp])
            S.emit("sync", cpf, inc=ld, inc_k=16)
            def cpt(e, half=half):
                vb, vp = ranks["sync"]
                v6 = ag1out.ap().rearrange(
                    "(b h z t2) (w q fl) -> b h z q (t2 w) fl",
                    b=4, h=2, z=2, w=2, q=2)
                return e.dma_start(out=my1t[half], in_=v6[vb, half, 1, vp])
            S.emit("sync", cpt, inc=ld, inc_k=16)
        barrier()
        xh1 = carve("xh1", [128, 4, 2, 2048], BF16, 0)
        vt1 = [carve(f"vt1_{h}", [128, 32, 65], BF16, 32768 + h * 4160) for h in range(8)]
        ptab1 = [carve(f"ptab1_{i}", [128, 1024], BF16, 66048 + i * 2048) for i in range(2)]
        ct1 = [carve(f"ct1_{i}", [128, 512], F32, 70400 + i * 2048) for i in range(2)]
        p_cA, p_cB = PF[4], PF[5]
        lbuf1 = carve("lbuf1", [128, 512], F32, 74496)
        my1fv = my1f.ap().rearrange("h2 (g p) t -> h2 g p t", p=128)
        my1tv = my1t.ap().rearrange("h2 (c p) (m d2) -> h2 m p c d2", p=128, d2=64)
        for hp in range(4):
            for half in range(2):
                S.dma("sync", xh1[:, hp, half, :], my1fv[half, hp], inc=ld)
        for h in range(8):
            S.emit("vector", lambda e, h=h: e.memset(vt1[h][:], 1.0), inc=dve)
        vmem1 = S.now(dve)
        for h in range(8):
            for half in range(2):
                def fv3(e, h=h, half=half):
                    return e.dma_start(out=vt1[h][:, 16 * half:16 * half + 16, 0:64],
                                       in_=my1tv[half, h])
                S.emit("sync", fv3, waits=[vmem1], inc=ld, inc_k=16)
        xh1v = S.now(ld)
        vt1v = xh1v
        stomarks1 = {}

        def attn_block(nkc, fqk, fexp, fmask, fpv, extra_qk0, extra_pv0):
            expmarks = {}
            pvmarks = {}

            def do_qk(kc):
                wq0 = [expmarks[kc - 2]] if kc >= 2 else list(extra_qk0)
                S.emit("tensor", fqk(kc, 0), waits=wq0)
                S.emit("tensor", fqk(kc, 1), inc=pe)
                return S.now(pe)

            def do_exp(kc, qkm):
                we = [qkm, pvmarks[kc - 2] if kc >= 2 else S.now(pe)]
                S.emit("scalar", fexp(kc), waits=we, inc=act)
                expmarks[kc] = S.now(act)

            def do_pv(kc):
                pvw = [expmarks[kc]]
                mv_ = fmask(kc, expmarks[kc])
                if mv_ is not None:
                    pvw = [mv_]
                wv0 = pvw + (list(extra_pv0) if kc == 0 else [])
                S.emit("tensor", fpv(kc, 0, nkc), waits=wv0)
                S.emit("tensor", fpv(kc, 1, nkc), inc=pe)
                pvmarks[kc] = S.now(pe)

            qkms = {0: do_qk(0)}
            for kc in range(nkc):
                if kc + 1 < nkc:
                    qkms[kc + 1] = do_qk(kc + 1)
                do_exp(kc, qkms[kc])
                do_pv(kc)
            return S.now(pe)

        for hp in range(4):
            hA, hB = 2 * hp, 2 * hp + 1
            for qb in range(8):
                nkc = 4 * (qb + 1)
                evac_prev = S.now(dve)
                qhalf, qrel = qb // 4, qb % 4

                def fqk(kc, side, hp=hp, qhalf=qhalf, qrel=qrel):
                    khalf, krel = kc // 16, kc % 16
                    par = kc % 2
                    r0 = 64 * side
                    return lambda e: e.matmul(
                        PB[par][:, 512 * side:512 * side + 512],
                        lhsT=xh1[r0:r0 + 64, hp, khalf, krel * 128:(krel + 1) * 128],
                        rhs=xh1[r0:r0 + 64, hp, qhalf, qrel * 512:(qrel + 1) * 512],
                        start=True, stop=True, tile_position=(r0, 0))

                def fexp(kc):
                    par = kc % 2
                    return lambda e: e.activation(
                        ptab1[par][:], PB[par][:, :], AF.Exp,
                        bias=mshift[:, 0:1], scale=0.125)

                def fmask(kc, em, qb=qb):
                    if kc < 4 * qb:
                        return None
                    j = kc - 4 * qb
                    par = kc % 2
                    S.emit("vector",
                           lambda e: e.tensor_tensor(
                               ptab1[par][:, 0:512], ptab1[par][:, 0:512],
                               tri[:, j * 512:(j + 1) * 512], ALU.mult),
                           waits=[em])
                    S.emit("vector",
                           lambda e: e.tensor_tensor(
                               ptab1[par][:, 512:1024], ptab1[par][:, 512:1024],
                               tri[:, j * 512:(j + 1) * 512], ALU.mult),
                           inc=dve)
                    return S.now(dve)

                def fpv(kc, side, nkc, hA=hA, hB=hB):
                    par = kc % 2
                    h = hA if side == 0 else hB
                    pc = p_cA if side == 0 else p_cB
                    return lambda e: e.matmul(
                        pc[0:65, :], lhsT=vt1[h][:, kc, :],
                        rhs=ptab1[par][:, 512 * side:512 * side + 512],
                        start=(kc == 0), stop=(kc == nkc - 1))

                ctxm = attn_block(nkc, fqk, fexp, fmask, fpv,
                                  [xh1v, S.now(act), S.now(dve)],
                                  [evac_prev, vmem1, vt1v])
                it = hp * 8 + qb
                par_c = it % 2
                wst = [stomarks1[it - 2]] if it >= 2 else []
                S.emit("vector", lambda e: e.reciprocal(linvA[:], p_cA[64:65, :]),
                       waits=[ctxm], inc=dve)
                S.emit("vector", lambda e: e.reciprocal(linvB[:], p_cB[64:65, :]),
                       inc=dve)
                lv = S.now(dve)
                S.emit("tensor",
                       lambda e: e.matmul(PB[0][0:64, 0:512], lhsT=ones1[0:1, 0:64],
                                          rhs=linvA[:], start=True, stop=True),
                       waits=[lv, S.now(act)])
                S.emit("tensor",
                       lambda e: e.matmul(PB[0][64:128, 0:512], lhsT=ones1[0:1, 0:64],
                                          rhs=linvB[:], start=True, stop=True),
                       inc=pe)
                lbv = S.now(pe)
                S.emit("vector", lambda e: e.tensor_copy(lbuf1[:], PB[0][:, 0:512]),
                       waits=[lbv], inc=dve)
                S.emit("vector",
                       lambda e, par_c=par_c: e.tensor_tensor(
                           ct1[par_c][0:64, :], p_cA[0:64, :], lbuf1[0:64, :], ALU.mult),
                       waits=wst + [S.now(dve)], inc=dve)
                S.emit("vector",
                       lambda e, par_c=par_c: e.tensor_tensor(
                           ct1[par_c][64:128, :], p_cB[0:64, :], lbuf1[64:128, :], ALU.mult),
                       inc=dve)
                cv = S.now(dve)
                qhalf, qrel = qb // 4, qb % 4
                S.dma("sync",
                      ag2in[qhalf * 512 + hp * 128: qhalf * 512 + (hp + 1) * 128,
                            qrel * 512:(qrel + 1) * 512],
                      ct1[par_c][:], waits=[cv], inc=sto)
                stomarks1[it] = S.now(sto)
        S.emit("gpsimd", lambda e: e.collective_compute(
            "AllGather", ALU.bypass, replica_groups=[list(range(8))],
            ins=[ag2in.ap()], outs=[ag2out.ap()]),
            waits=[S.now(sto)], inc=cc)
        S.emit("sync", lambda e: e.nop(), waits=[S.now(cc)])
        for half in range(2):
            def cp2(e, half=half):
                vb, vp = ranks["sync"]
                v2 = ag2out.ap().rearrange("(w hh f) c -> w hh f c", w=8, hh=2)
                return e.dma_start(out=my2[half], in_=v2[vb * 2 + half, vp])
            S.emit("sync", cp2, inc=ld, inc_k=16)
        barrier()

        tw = 256
        w1s = carve("w1s", [128, DC, F], BF16, 0)
        w2s = carve("w2s", [128, F // 128, D], BF16, 65536)
        qbuf = [carve(f"qbuf{i}", [128, 512], F32, 131072 + i * 2048) for i in range(2)]
        scrow = [carve(f"scrow{i}", [1, 512], F32, 135168 + i * 2048) for i in range(2)]
        ya2 = carve("ya2", [128, DC, tw], F32, 139264)
        x12 = carve("x12", [128, DC, tw], F32, 147456)
        hb2b = carve("hb2b", [128, DC, tw], BF16, 155648)
        yb2 = carve("yb2", [128, F // 128, tw], BF16, 159744)
        p_y = [PF[0], PF[1]]
        p_z = [PF[2], PF[3]]
        psd2 = dict(s=PF[4][0:1, 0:tw], q=PF[4][0:1, 256:256 + tw],
                    a=PF[5][:, 0:tw], b=PF[5][:, 256:256 + tw])
        agv2 = lambda: ag2out.ap().rearrange("(r f) c -> r f c", r=8)

        dmarks = []

        def dequant2(qdram, scdram, ngrp, ncol, wdst):
            nfs = ncol // 512
            for g in range(ngrp):
                for fs in range(nfs):
                    i = len(dmarks)
                    pb = i % 2
                    ppsum = p_y[0] if pb == 0 else p_y[1]
                    wprev = [dmarks[i - 2]] if i >= 2 else [S.now(act), S.now(dve)]
                    S.dma("gpsimd", qbuf[pb][:],
                          qdram[g * 128:(g + 1) * 128, fs * 512:(fs + 1) * 512],
                          waits=wprev, inc=gld)
                    S.dma("sync", scrow[pb][:],
                          scdram[g:g + 1, fs * 512:(fs + 1) * 512],
                          waits=wprev, inc=ld)
                    gl, ll = S.now(gld), S.now(ld)
                    S.emit("tensor",
                           lambda e, ppsum=ppsum, pb=pb: e.matmul(
                               ppsum[:], lhsT=ones1[:], rhs=scrow[pb][:],
                               start=True, stop=True),
                           waits=[ll, constv] + wprev, inc=pe)
                    pv = S.now(pe)
                    S.emit("vector",
                           lambda e, ppsum=ppsum, pb=pb, g=g, fs=fs: e.scalar_tensor_tensor(
                               wdst[:, g, fs * 512:(fs + 1) * 512],
                               qbuf[pb][:], -128.0, ppsum[:], ALU.add, ALU.mult),
                           waits=[pv, gl], inc=dve)
                    dmarks.append(S.now(dve))

        dequant2(fc1q, sc1, DC, F, w1s)
        dequant2(fc2q, sc2, F // 128, D, w2s)
        wready = S.now(dve)

        for it in range(TPC // tw):
            tsl = slice(it * tw, (it + 1) * tw)
            lw = [S.now(dve), S.now(sto), S.now(gld)]
            S.dma("sync", ya2[:], xT[:, tsl].rearrange("(c p) t -> p c t", p=128),
                  waits=lw, inc=ld)
            xv = S.now(ld)
            for half in range(2):
                def facc2(e, it=it, half=half):
                    view = my2.ap().rearrange("h2 (c p) t -> h2 p c t", p=128)
                    return e.dma_start(
                        out=ya2[:, 4 * half:4 * half + 4, :],
                        in_=view[half][:, :, it * tw:(it + 1) * tw],
                        accum_op=ALU.add)
                S.emit("gpsimd", facc2, waits=[xv], inc=gld, inc_k=16)
            y1v = S.now(gld)
            emit_ln(S, lambda c: ya2[:, c, :], lambda c: x12[:, c, :], DC, tw,
                    cons, psd2, pe, act, dve, y1v)
            x1v = S.now(dve)
            hv = emit_ln(S, lambda c: x12[:, c, :], lambda c: tmp[:, 0:tw], DC, tw,
                         cons, psd2, pe, act, dve, x1v,
                         gamma=lambda c: gg[:, c:c + 1], beta=lambda c: bb0[:, c:c + 1],
                         dst_cast=lambda c: hb2b[:, c, :])
            for fg in range(F // 128):
                pyb = p_y[fg % 2]
                wy = [S.now(act)] if fg >= 2 else [hv, wready]
                for g in range(DC):
                    S.emit("tensor",
                           lambda e, g=g, fg=fg, pyb=pyb: e.matmul(
                               pyb[:, 0:tw], lhsT=w1s[:, g, fg * 128:(fg + 1) * 128],
                               rhs=hb2b[:, g, :], start=(g == 0), stop=(g == DC - 1)),
                           waits=wy if g == 0 else (),
                           inc=pe if g == DC - 1 else None)
                pv = S.now(pe)
                S.emit("scalar",
                       lambda e, fg=fg, pyb=pyb: e.activation(
                           yb2[:, fg, :], pyb[:, 0:tw], AF.Silu,
                           bias=bb1[:, fg:fg + 1], scale=1.0),
                       waits=[pv], inc=act)
            yv = S.now(act)
            for dc_ in range(DC):
                pz = p_z[dc_ % 2]
                wz = [S.now(dve)] if dc_ >= 2 else [yv]
                for fg in range(F // 128):
                    S.emit("tensor",
                           lambda e, dc_=dc_, fg=fg, pz=pz: e.matmul(
                               pz[:, 0:tw], lhsT=w2s[:, fg, dc_ * 128:(dc_ + 1) * 128],
                               rhs=yb2[:, fg, :], start=(fg == 0), stop=(fg == F // 128 - 1)),
                           waits=(wz + [yv]) if fg == 0 else (),
                           inc=pe if fg == F // 128 - 1 else None)
                zv = S.now(pe)
                S.emit("vector",
                       lambda e, dc_=dc_, pz=pz: e.tensor_scalar(
                           tmp[:, 0:tw], pz[:, 0:tw], 0.5, hb2[:, dc_:dc_ + 1],
                           ALU.mult, ALU.add),
                       waits=[zv], inc=dve)
                S.emit("vector",
                       lambda e, dc_=dc_: e.tensor_tensor(fo[:, 0:tw], tmp[:, 0:tw],
                                                          x12[:, dc_, :], ALU.add),
                       waits=[S.now(dve)], inc=dve)
                S.emit("vector",
                       lambda e, dc_=dc_: e.tensor_tensor(ya2[:, dc_, :], fo[:, 0:tw],
                                                          x12[:, dc_, :], ALU.add),
                       waits=[S.now(dve)], inc=dve)
            y2v = S.now(dve)
            emit_ln(S, lambda c: ya2[:, c, :], lambda c: x12[:, c, :], DC, tw,
                    cons, psd2, pe, act, dve, y2v)
            ov = S.now(dve)
            S.dma("sync", x2loc.ap()[:, tsl].rearrange("(c p) t -> p c t", p=128), x12[:],
                  waits=[ov], inc=sto)
            S.emit("gpsimd",
                   lambda e, tsl=tsl: e.dma_start(
                       out=ag3in[:, tsl].rearrange("(c p) t -> p c t", p=128), in_=x12[:]),
                   waits=[ov], inc=gld, inc_k=16)
        S.emit("gpsimd", lambda e: e.collective_compute(
            "AllGather", ALU.bypass, replica_groups=[list(range(8))],
            ins=[ag3in.ap()], outs=[ag3out.ap()]),
            waits=[S.now(gld), S.now(sto)], inc=cc)
        S.emit("sync", lambda e: e.nop(), waits=[S.now(cc)])
        for half in range(2):
            def cp3(e, half=half):
                vb, vp = ranks["sync"]
                v3 = ag3out.ap().rearrange("(w r) c -> w r c", w=8)
                return e.dma_start(out=my3[half], in_=v3[vb * 2 + half])
            S.emit("sync", cp3, inc=ld, inc_k=16)
        barrier()

        xh3 = carve("xh3", [128, 8, 2, 2048], BF16, 0)
        qh3 = carve("qh3", [128, 4, 4096], BF16, 65536)
        kh3 = carve("kh3", [128, 4, 4096], BF16, 98304)
        vtok3 = carve("vtok3", [128, 32, 8, 65], BF16, 131072)
        wbuf3 = carve("wbuf3", [128, DC, 512], BF16, 164352)
        ptab3 = [carve(f"ptab3_{i}", [128, 1024], BF16, 172544 + i * 2048) for i in range(2)]
        ct3 = [carve(f"ct3_{i}", [128, 512], F32, 176640 + i * 2048) for i in range(2)]
        agv3 = lambda: ag3out.ap().rearrange("(r f) c -> r f c", r=8)
        my3v = my3.ap().rearrange("h2 (a p) c -> h2 p a c", p=128)
        for half in range(2):
            S.dma("sync", xh3[:, :, half, :], my3v[half], inc=ld)
        xh3v = S.now(ld)
        pmarks3 = []
        for wi, (wdram, dst, bias) in enumerate(((WqT, qh3, bq_s), (WkT, kh3, bk_s))):
            S.dma("gpsimd", wbuf3[:], wdram.rearrange("(c p) o -> p c o", p=128),
                  waits=[S.now(pe)], inc=gld)
            wldv = S.now(gld)
            for oc in range(4):
                for ts in range(8):
                    i = len(pmarks3)
                    pp = PF[i % 2]
                    w0 = ([xh3v, ldv0] if i < 2 else [pmarks3[i - 2]]) + [wldv]
                    for dcx in range(DC):
                        S.emit("tensor",
                               lambda e, dcx=dcx, oc=oc, ts=ts, pp=pp: e.matmul(
                                   pp[:], lhsT=wbuf3[:, dcx, oc * 128:(oc + 1) * 128],
                                   rhs=xh3[:, dcx, ts // 4, (ts % 4) * 512:(ts % 4 + 1) * 512],
                                   start=(dcx == 0), stop=(dcx == DC - 1)),
                               waits=w0 if dcx == 0 else (),
                               inc=pe if dcx == DC - 1 else None)
                    pv = S.now(pe)
                    S.emit("vector",
                           lambda e, dst=dst, bias=bias, oc=oc, ts=ts, pp=pp: e.tensor_scalar_add(
                               dst[:, oc, ts * 512:(ts + 1) * 512], pp[:], bias[:, oc:oc + 1]),
                           waits=[pv], inc=dve)
                    pmarks3.append(S.now(dve))
        S.dma("gpsimd", wbuf3[:], WvT.rearrange("(c p) o -> p c o", p=128),
              waits=[S.now(pe)], inc=gld)
        wvldv = S.now(gld)
        S.emit("tensor", lambda e: e.matmul(PF[2][:], lhsT=ones1[:], rhs=bvr[:],
                                            start=True, stop=True),
               waits=[ldv0, memv, S.now(act), S.now(dve)], inc=pe)
        S.emit("vector", lambda e: e.tensor_copy(bv_b[:], PF[2][:]),
               waits=[S.now(pe)], inc=dve)
        bvv = S.now(dve)
        S.emit("vector", lambda e: e.memset(vtok3[:], 1.0), inc=dve)
        vmem3 = S.now(dve)
        for tc in range(32):
            i = len(pmarks3)
            pp = PF[i % 2]
            w0 = [pmarks3[i - 2], wvldv, vmem3]
            for dcx in range(DC):
                S.emit("tensor",
                       lambda e, dcx=dcx, tc=tc, pp=pp: e.matmul(
                           pp[:], lhsT=xh3[:, dcx, tc // 16, (tc % 16) * 128:(tc % 16 + 1) * 128],
                           rhs=wbuf3[:, dcx, :], start=(dcx == 0), stop=(dcx == DC - 1)),
                       waits=w0 if dcx == 0 else (),
                       inc=pe if dcx == DC - 1 else None)
            pv = S.now(pe)
            S.emit("vector",
                   lambda e, tc=tc, pp=pp: e.scalar_tensor_tensor(
                       vtok3[:, tc, :, 0:64], pp.rearrange("p (h d2) -> p h d2", h=8),
                       1.0, bv_b.ap().rearrange("p (h d2) -> p h d2", h=8),
                       ALU.mult, ALU.add),
                   waits=[pv, bvv], inc=dve)
            pmarks3.append(S.now(dve))
        projv = S.now(dve)
        lbuf3 = carve("lbuf3", [128, 512], F32, 164352)
        wo4 = carve("wo4", [128, DC, D], BF16, 0)
        S.dma("gpsimd", wo4[:], WoT.rearrange("(c p) o -> p c o", p=128),
              waits=[S.now(pe)], inc=gld)
        wov = S.now(gld)
        stomarks3 = {}
        for hp in range(4):
            hA, hB = 2 * hp, 2 * hp + 1
            for qb in range(8):
                evac_prev = S.now(dve)

                def fqk3(kc, side, hp=hp, qb=qb):
                    par = kc % 2
                    r0 = 64 * side
                    return lambda e: e.matmul(
                        PB[par][:, 512 * side:512 * side + 512],
                        lhsT=kh3[r0:r0 + 64, hp, kc * 128:(kc + 1) * 128],
                        rhs=qh3[r0:r0 + 64, hp, qb * 512:(qb + 1) * 512],
                        start=True, stop=True, tile_position=(r0, 0))

                def fexp3(kc):
                    par = kc % 2
                    return lambda e: e.activation(
                        ptab3[par][:], PB[par][:, :], AF.Exp,
                        bias=biasall[:, kc:kc + 1], scale=0.125)

                def fmask3(kc, em):
                    return None

                def fpv3(kc, side, nkc, hA=hA, hB=hB):
                    par = kc % 2
                    h = hA if side == 0 else hB
                    pc = p_cA if side == 0 else p_cB
                    return lambda e: e.matmul(
                        pc[0:65, :], lhsT=vtok3[:, kc, h, :],
                        rhs=ptab3[par][:, 512 * side:512 * side + 512],
                        start=(kc == 0), stop=(kc == nkc - 1))

                ctxm = attn_block(32, fqk3, fexp3, fmask3, fpv3,
                                  [projv, S.now(act), S.now(dve)],
                                  [evac_prev])
                it = hp * 8 + qb
                par_c = it % 2
                wst = [stomarks3[it - 2]] if it >= 2 else []
                S.emit("vector", lambda e: e.reciprocal(linvA[:], p_cA[64:65, :]),
                       waits=[ctxm], inc=dve)
                S.emit("vector", lambda e: e.reciprocal(linvB[:], p_cB[64:65, :]),
                       inc=dve)
                lv = S.now(dve)
                S.emit("tensor",
                       lambda e: e.matmul(PB[0][0:64, 0:512], lhsT=ones1[0:1, 0:64],
                                          rhs=linvA[:], start=True, stop=True),
                       waits=[lv, S.now(act)])
                S.emit("tensor",
                       lambda e: e.matmul(PB[0][64:128, 0:512], lhsT=ones1[0:1, 0:64],
                                          rhs=linvB[:], start=True, stop=True),
                       inc=pe)
                lbv = S.now(pe)
                S.emit("vector", lambda e: e.tensor_copy(lbuf3[:], PB[0][:, 0:512]),
                       waits=[lbv], inc=dve)
                S.emit("vector",
                       lambda e, par_c=par_c: e.tensor_tensor(
                           ct3[par_c][0:64, :], p_cA[0:64, :], lbuf3[0:64, :], ALU.mult),
                       waits=wst + [S.now(dve)], inc=dve)
                S.emit("vector",
                       lambda e, par_c=par_c: e.tensor_tensor(
                           ct3[par_c][64:128, :], p_cB[0:64, :], lbuf3[64:128, :], ALU.mult),
                       inc=dve)
                cv = S.now(dve)
                qhalf, qrel = qb // 4, qb % 4
                S.emit("gpsimd",
                       lambda e, hp=hp, qhalf=qhalf, qrel=qrel, par_c=par_c: e.dma_start(
                           out=ag4in[qhalf * 512 + hp * 128: qhalf * 512 + (hp + 1) * 128,
                                     qrel * 512:(qrel + 1) * 512],
                           in_=ct3[par_c][:]),
                       waits=[cv], inc=gld, inc_k=16)
                stomarks3[it] = S.now(gld)
        S.emit("gpsimd", lambda e: e.collective_compute(
            "AllGather", ALU.bypass, replica_groups=[list(range(8))],
            ins=[ag4in.ap()], outs=[ag4out.ap()]),
            waits=[S.now(gld)], inc=cc)
        S.emit("sync", lambda e: e.nop(), waits=[S.now(cc)])
        for half in range(2):
            def cp4(e, half=half):
                vb, vp = ranks["sync"]
                v4b = ag4out.ap().rearrange("(w hh f) c -> w hh f c", w=8, hh=2)
                return e.dma_start(out=my4[half], in_=v4b[vb * 2 + half, vp])
            S.emit("sync", cp4, inc=ld, inc_k=16)
        barrier()

        p_ra = PF[0][:, 0:128]
        mx = cells[:, 0:1]
        nmx = cells[:, 1:2]
        ssum = cells[:, 2:3]
        rsn = cells[:, 11:12]
        nmu = cells[:, 12:13]
        ssq = cells[:, 13:14]
        ex2 = cells[:, 14:15]
        var = cells[:, 15:16]
        p_cum = p_ra[:, 0:16]
        p_cnt = p_ra[0:1, 16:32]
        p_pr = p_ra[0:1, 32:48]

        tw4 = 512
        ch4 = carve("ch4", [128, DC, tw4], BF16, 16384)
        x24 = carve("x24", [128, DC, tw4], F32, 24576)
        yy4 = carve("yy4", [128, DC, tw4], F32, 40960)
        zz4 = carve("zz4", [128, DC, tw4], F32, 57344)
        x34 = carve("x34", [128, DC, tw4], F32, 73728)
        tw4 = 256
        lgt4 = carve("lgt4", [128, 2, E], F32, 90112)
        xtk4 = carve("xtk4", [128, 2, D], F32, 90368)
        agv4 = lambda: ag4out.ap().rearrange("(r f) c -> r f c", r=8)
        rtm = [None]

        def route_tile(lt, first):
            lgt = lg[:, lt, :]
            vstep(lambda e, lgt=lgt: e.reduce_max(mx, lgt, axis=AX.X))
            vstep(lambda e: e.tensor_scalar_mul(nmx, mx, -1.0))
            S.emit("scalar", lambda e, lgt=lgt: e.activation(et16[:], lgt, AF.Exp,
                                                             bias=nmx, scale=1.0),
                   waits=[S.now(dve)], inc=act)
            vstep(lambda e: e.reduce_sum(ssum, et16[:], axis=AX.X), extra=[S.now(act)])
            vstep(lambda e, lt=lt: e.reciprocal(psel[:, lt:lt + 1], ssum))
            vstep(lambda e, lgt=lgt: e.tensor_single_scalar(t16[:], lgt, mx, ALU.is_ge))
            vstep(lambda e: e.scalar_tensor_tensor(t16[:], t16[:], nbigf[:, 0:1],
                                                   iotaBIG[:], ALU.mult, ALU.add))
            vstep(lambda e, lt=lt: e.tensor_reduce(sel[:, lt:lt + 1], t16[:], AX.X, ALU.min))
            vstep(lambda e, lt=lt: e.tensor_single_scalar(ohb[:, lt, :], iota16[:],
                                                          sel[:, lt:lt + 1], ALU.is_equal))
            ohv = S.now(dve)
            S.emit("tensor", lambda e, lt=lt: e.matmul(p_cum, lhsT=UTs[:], rhs=ohb[:, lt, :],
                                                       start=True, stop=False),
                   waits=[ohv] + ([S.now(act)] if first else []))
            S.emit("tensor", lambda e: e.matmul(p_cum, lhsT=ones1[:], rhs=offloc[:],
                                                start=False, stop=True), inc=pe)
            cumv = S.now(pe)
            vstep(lambda e, lt=lt: e.tensor_tensor(t16[:], p_cum, ohb[:, lt, :], ALU.mult),
                  extra=[cumv])
            vstep(lambda e, lt=lt: e.reduce_sum(lloc[:, lt:lt + 1], t16[:], axis=AX.X))
            rdv = S.now(dve)
            S.emit("tensor", lambda e, lt=lt: e.matmul(p_cnt, lhsT=ones128[:],
                                                       rhs=ohb[:, lt, :], start=True, stop=True),
                   waits=[rdv], inc=pe)
            cntv = S.now(pe)
            vstep(lambda e: e.tensor_tensor(offloc[:], offloc[:], p_cnt, ALU.add),
                  extra=[cntv])
            vstep(lambda e, lt=lt: e.tensor_copy(crow[0:1, lt * E:(lt + 1) * E], p_cnt))
        for it in range(TPC // tw4):
            tsl = slice(it * tw4, (it + 1) * tw4)
            lw = [S.now(dve), S.now(sto), S.now(gld), S.now(act)]
            S.dma("sync", x24[:, :, 0:tw4], x2loc.ap()[:, tsl].rearrange("(c p) t -> p c t", p=128),
                  waits=lw, inc=ld)
            for half in range(2):
                def fc4(e, it=it, half=half):
                    view = my4.ap().rearrange("h2 (c p) t -> h2 p c t", p=128)
                    return e.dma_start(
                        out=ch4[:, 4 * half:4 * half + 4, 0:tw4],
                        in_=view[half][:, :, it * tw4:(it + 1) * tw4])
                S.emit("gpsimd", fc4, waits=lw, inc=gld, inc_k=16)
            lv, gv = S.now(ld), S.now(gld)
            for oc in range(DC):
                po = PF[oc % 2]
                wz = ([gv, lv, ldv0, wov] + ([rtm[0]] if rtm[0] else [])) if oc < 2                     else [S.now(dve)]
                for dci in range(DC):
                    S.emit("tensor",
                           lambda e, dci=dci, oc=oc, po=po: e.matmul(
                               po[:, 0:tw4], lhsT=wo4[:, dci, oc * 128:(oc + 1) * 128],
                               rhs=ch4[:, dci, 0:tw4], start=(dci == 0), stop=(dci == DC - 1)),
                           waits=wz if dci == 0 else (),
                           inc=pe if dci == DC - 1 else None)
                pv = S.now(pe)
                S.emit("vector",
                       lambda e, oc=oc, po=po: e.scalar_tensor_tensor(
                           yy4[:, oc, 0:tw4], po[:, 0:tw4], bo_s[:, oc:oc + 1], x24[:, oc, 0:tw4],
                           ALU.add, ALU.add),
                       waits=[pv], inc=dve)
            yv = S.now(dve)
            zv = emit_ln(S, lambda c: yy4[:, c, 0:tw4], lambda c: tmp[:, 0:tw4], DC, tw4,
                         cons, psd2, pe, act, dve, yv,
                         gamma=lambda c: g_s[:, c:c + 1], beta=lambda c: b_s[:, c:c + 1],
                         dst_cast=lambda c: zz4[:, c, 0:tw4])
            for c in range(DC):
                S.emit("vector",
                       lambda e, c=c: e.tensor_tensor(yy4[:, c, 0:tw4], x24[:, c, 0:tw4],
                                                      zz4[:, c, 0:tw4], ALU.add),
                       waits=[zv] if c == 0 else (), inc=dve)
            y3v = S.now(dve)
            emit_ln(S, lambda c: yy4[:, c, 0:tw4], lambda c: x34[:, c, 0:tw4], DC, tw4,
                    cons, psd2, pe, act, dve, y3v)
            x3v = S.now(dve)
            for s_ in range(tw4 // 128):
                for dci in range(DC):
                    i = s_ * DC + dci
                    ptf = PF[2 + (i % 2)]
                    S.emit("tensor",
                           lambda e, s_=s_, dci=dci, ptf=ptf: e.transpose(
                               ptf[:, 0:128],
                               x34[:, dci, s_ * 128:(s_ + 1) * 128], id128f[:]),
                           waits=[x3v] if i == 0 else ([S.now(dve)] if i >= 2 else []),
                           inc=pe)
                    pv = S.now(pe)
                    S.emit("vector",
                           lambda e, s_=s_, dci=dci, ptf=ptf: e.tensor_copy(
                               xtk4[:, s_, dci * 128:(dci + 1) * 128], ptf[:, 0:128]),
                           waits=[pv], inc=dve)
            tkv = S.now(dve)
            S.dma("sync", x3tokD[tsl, :].rearrange("(s p) d2 -> p s d2", p=128), xtk4[:],
                  waits=[tkv], inc=sto)
            for s_ in range(tw4 // 128):
                pgb = PF[2 + (s_ % 2)]
                for dci in range(DC):
                    w0 = []
                    if dci == 0:
                        w0 = [x3v, S.now(dve)]
                    S.emit("tensor",
                           lambda e, s_=s_, dci=dci, pgb=pgb: e.matmul(
                               pgb[:, 0:E], lhsT=x34[:, dci, s_ * 128:(s_ + 1) * 128],
                               rhs=gw[:, dci, :], start=(dci == 0), stop=(dci == DC - 1)),
                           waits=w0,
                           inc=pe if dci == DC - 1 else None)
                gvv = S.now(pe)
                S.emit("vector",
                       lambda e, s_=s_, pgb=pgb: e.tensor_copy(lgt4[:, s_, :], pgb[:, 0:E]),
                       waits=[gvv], inc=dve)
            lgv = S.now(dve)
            for s_ in range(tw4 // 128):
                S.emit("vector",
                       lambda e, it=it, s_=s_: e.tensor_copy(lg[:, it * (tw4 // 128) + s_, :],
                                                             lgt4[:, s_, :]),
                       waits=[lgv], inc=dve)
                route_tile(it * (tw4 // 128) + s_, it == 0 and s_ == 0)
            rtm[0] = S.now(dve)
        p1v = S.now(dve)
        S.dma("sync", ag5in.ap(), crow[:], waits=[p1v], inc=ld)
        S.emit("gpsimd", lambda e: e.collective_compute(
            "AllGather", ALU.bypass, replica_groups=[list(range(8))],
            ins=[ag5in.ap()], outs=[ag5out.ap()]),
            waits=[S.now(ld)], inc=cc)
        S.emit("sync", lambda e: e.nop(), waits=[S.now(cc)])
        S.dma("sync", cg[:], ag5out.ap().rearrange("r (lt e2) -> (r lt) e2", e2=E), inc=ld)
        cgv = S.now(ld)
        S.emit("tensor", lambda e: e.matmul(p_pr, lhsT=pmask[:, 0:1], rhs=cg[:],
                                            start=True, stop=True),
               waits=[cgv, S.now(dve), S.now(act)], inc=pe)
        vstep(lambda e: e.tensor_copy(prefrow[:], p_pr), extra=[S.now(pe)])
        prefrow16 = carve("prefrow16", [1, 256], F32, 172032)
        r16b = carve("r16b", [128, 192], F32, 173056)
        for j in range(16):
            vstep(lambda e, j=j: e.tensor_copy(prefrow16[0:1, j * E:(j + 1) * E], prefrow[:]))
        S.emit("tensor", lambda e: e.matmul(PF[1][:, 0:256], lhsT=ones1[0:1, 0:128],
                                            rhs=prefrow16[:], start=True, stop=True),
               waits=[S.now(dve), S.now(act)], inc=pe)
        prbv = S.now(pe)
        (pftA, glocA, keepA, h8A, hsA, posA, tA, tB, aaS, aaC, nhsA, nh8A) =             [r16b[:, i * 16:(i + 1) * 16] for i in range(12)]
        ohv2 = ohb[:].rearrange("p l e -> p (l e)")
        vstep(lambda e: e.tensor_tensor(tmp[:, 0:256], PF[1][:, 0:256], ohv2, ALU.mult),
              extra=[prbv])
        vstep(lambda e: e.tensor_reduce(
            pftA, tmp[:, 0:256].rearrange("p (l e) -> p l e", l=16), AX.X, ALU.add))
        vstep(lambda e: e.tensor_tensor(glocA, lloc[:], pftA, ALU.add))
        vstep(lambda e: e.tensor_single_scalar(keepA, glocA, capg[:], ALU.is_le))
        vstep(lambda e: e.tensor_single_scalar(hsA, glocA, c512[:], ALU.is_ge))
        vstep(lambda e: e.tensor_single_scalar(nhsA, glocA, c511[:], ALU.is_le))
        vstep(lambda e: e.tensor_single_scalar(h8A, sel[:], c8c[:], ALU.is_ge))
        vstep(lambda e: e.tensor_single_scalar(nh8A, sel[:], c7c[:], ALU.is_le))
        vstep(lambda e: e.tensor_scalar_mul(tA, h8A, -15.0))
        vstep(lambda e: e.scalar_tensor_tensor(posA, sel[:], 2.0, tA, ALU.mult, ALU.add))
        vstep(lambda e: e.scalar_tensor_tensor(aaS, posA, 512.0, glocA, ALU.mult, ALU.add))
        vstep(lambda e: e.tensor_tensor(tB, hsA, posA, ALU.mult))
        vstep(lambda e: e.scalar_tensor_tensor(aaS, tB, 256.0, aaS, ALU.mult, ALU.add))
        vstep(lambda e: e.tensor_scalar_mul(tA, hsA, 7680.0))
        vstep(lambda e: e.tensor_tensor(aaS, aaS, tA, ALU.add))
        vstep(lambda e: e.tensor_single_scalar(tB, aaS, bigp[:], ALU.subtract))
        vstep(lambda e: e.tensor_tensor(tB, tB, keepA, ALU.mult))
        vstep(lambda e: e.tensor_single_scalar(tB, tB, bigp[:], ALU.add))
        vstep(lambda e: e.tensor_copy(kidxS[0][:], tB))
        vstep(lambda e: e.scalar_tensor_tensor(tB, h8A, -8.0, sel[:], ALU.mult, ALU.add))
        vstep(lambda e: e.scalar_tensor_tensor(aaC, tB, 1280.0, glocA, ALU.mult, ALU.add))
        vstep(lambda e: e.tensor_scalar_mul(tA, h8A, 10240.0))
        vstep(lambda e: e.tensor_tensor(aaC, aaC, tA, ALU.add))
        vstep(lambda e: e.tensor_single_scalar(tB, aaC, bigp[:], ALU.subtract))
        vstep(lambda e: e.tensor_tensor(tB, tB, keepA, ALU.mult))
        vstep(lambda e: e.tensor_single_scalar(tB, tB, bigp[:], ALU.add))
        vstep(lambda e: e.tensor_copy(kidxC[0][:], tB))
        kiv = S.now(dve)
        xsc5 = [carve(f"xsc5_{i}", [128, D], BF16, 167936 + i * 2048) for i in range(2)]
        for lt in range(16):
            xb = xsc5[lt % 2]
            S.emit("gpsimd",
                   lambda e, lt=lt, xb=xb: e.dma_start(
                       out=xb[:], in_=x3tokD[lt * 128:(lt + 1) * 128, :]),
                   waits=[S.now(gld)] if lt >= 2 else [S.now(sto)], inc=gld, inc_k=16)
            xv5 = S.now(gld)
            S.emit("gpsimd",
                   lambda e, lt=lt, xb=xb: e.indirect_dma_start(
                       out=dispD[0:128, :],
                       out_offset=bass.IndirectOffsetOnAxis(kidxS[0][:, lt:lt + 1], 0),
                       in_=xb[:], in_offset=None,
                       bounds_check=E * CAP - 1, oob_is_err=False),
                   waits=[kiv, xv5] + ([zfv] if lt == 0 else []), inc=gld, inc_k=16)
        scatv = S.now(gld)
        S.emit("gpsimd", lambda e: e.collective_compute(
            "ReduceScatter", ALU.add, replica_groups=[list(range(8))],
            ins=[dispD.ap()[0:E * 512, :]], outs=[dispRS1.ap()]),
            waits=[scatv, zfv], inc=cc)
        rsv1 = S.now(cc)
        S.emit("gpsimd", lambda e: e.collective_compute(
            "ReduceScatter", ALU.add, replica_groups=[list(range(8))],
            ins=[dispD.ap()[E * 512:E * CAP, :]], outs=[dispRS2.ap()]),
            inc=cc)
        rsv2 = S.now(cc)
        barrier(exclude=("cc", "gld"))

        w1h = [carve(f"w1h{i}", [128, DC, F // 2], BF16, i * 32768) for i in range(2)]
        w2h = [carve(f"w2h{i}", [128, F // 128, D // 2], BF16, 65536 + i * 32768)
               for i in range(2)]
        dtok5 = carve("dtok5", [128, 2, D], BF16, 131072)
        dft5 = carve("dft5", [128, DC, 256], BF16, 135168)
        hh5 = carve("hh5", [128, F // 128, 256], BF16, 139264)
        zt5 = carve("zt5", [128, DC, 256], F32, 155648)
        otok5 = carve("otok5", [128, 2, D], BF16, 163840)
        comb5 = carve("comb5", [128, D], F32, 0)
        combB = carve("combB", [128, D], BF16, 16384)
        x3f5 = [carve(f"x3f5_{i}", [128, D], F32, 4096 + i * 4096) for i in range(2)]
        sqt5 = carve("sqt5", [128, D], F32, 12288)
        p_tb5 = PT
        p_tf5 = [PF[3][:, 0:128], PF[4][:, 0:128]]
        p_f5 = [PF[1], PF[2]]
        wm = {}

        def load_w(exl, which, wait_marks):
            if which == "w1A":
                S.dma("gpsimd", w1h[0][:], w1d[exl][:, 0:F // 2].rearrange(
                    "(c p) f2 -> p c f2", p=128), waits=wait_marks, inc=gld)
            elif which == "w1B":
                S.dma("gpsimd", w1h[1][:], w1d[exl][:, F // 2:F].rearrange(
                    "(c p) f2 -> p c f2", p=128), waits=wait_marks, inc=gld)
            elif which == "w2A":
                S.dma("gpsimd", w2h[0][:], w2d[exl][:, 0:D // 2].rearrange(
                    "(g p) d2 -> p g d2", p=128), waits=wait_marks, inc=gld)
            else:
                S.dma("gpsimd", w2h[1][:], w2d[exl][:, D // 2:D].rearrange(
                    "(g p) d2 -> p g d2", p=128), waits=wait_marks, inc=gld)
            wm[(exl, which)] = S.now(gld)

        for which in ("w1A", "w1B", "w2A", "w2B"):
            load_w(0, which, [])
        agvA = None
        prev_ct_sto = None
        for exl in range(2):
            for ct in range(CAP // 256):
                if ct < 2:
                    srcv = dispRS1[exl * 512 + ct * 256: exl * 512 + (ct + 1) * 256, :]
                    rsw = rsv1
                else:
                    srcv = dispRS2[exl * 768 + (ct - 2) * 256: exl * 768 + (ct - 1) * 256, :]
                    rsw = rsv2
                S.dma("sync", dtok5[:], srcv.rearrange("(s p) d2 -> p s d2", p=128),
                      waits=[S.now(dve), rsw] + ([prev_ct_sto] if prev_ct_sto else []),
                      inc=ld)
                dtv = S.now(ld)
                tmarks = []
                for i in range(16):
                    sc, dcx = i // 8, i % 8
                    ptb = p_tb5[i % 2]
                    wt = [dtv] + ([tmarks[i - 2]] if i >= 2 else [S.now(dve)])
                    S.emit("tensor",
                           lambda e, sc=sc, dcx=dcx, ptb=ptb: e.transpose(
                               ptb[:], dtok5[:, sc, dcx * 128:(dcx + 1) * 128], id128b[:]),
                           waits=wt, inc=pe)
                    pv = S.now(pe)
                    S.emit("vector",
                           lambda e, sc=sc, dcx=dcx, ptb=ptb: e.tensor_copy(
                               dft5[:, dcx, sc * 128:(sc + 1) * 128], ptb[:]),
                           waits=[pv], inc=dve)
                    tmarks.append(S.now(dve))
                dftv = S.now(dve)
                for fg in range(F // 128):
                    pf = p_f5[fg % 2]
                    wz = []
                    if fg == 0:
                        wz = [wm[(exl, "w1A")], dftv]
                    elif fg == 16:
                        wz = [wm[(exl, "w1B")], S.now(act)]
                    elif fg >= 2:
                        wz = [S.now(act)]
                    wb = w1h[fg // 16]
                    fl = fg % 16
                    for dcx in range(DC):
                        S.emit("tensor",
                               lambda e, dcx=dcx, fl=fl, pf=pf, wb=wb: e.matmul(
                                   pf[:, 0:256], lhsT=wb[:, dcx, fl * 128:(fl + 1) * 128],
                                   rhs=dft5[:, dcx, :], start=(dcx == 0), stop=(dcx == DC - 1)),
                               waits=wz if dcx == 0 else (),
                               inc=pe if dcx == DC - 1 else None)
                    if exl == 0 and ct == CAP // 256 - 1 and fg in (15, 31):
                        load_w(1, "w1A" if fg == 15 else "w1B", [S.now(pe)])
                    pv = S.now(pe)
                    S.emit("scalar",
                           lambda e, fg=fg, pf=pf: e.activation(
                               hh5[:, fg, :], pf[:, 0:256], AF.Relu),
                           waits=[pv], inc=act)
                hv = S.now(act)
                for dcx in range(DC):
                    pf = p_f5[dcx % 2]
                    wb2 = w2h[dcx // 4]
                    dl = dcx % 4
                    for fg in range(F // 128):
                        wz = []
                        if fg == 0:
                            wz = [hv, wm[(exl, "w2A" if dcx < 4 else "w2B")]] +                                 ([S.now(dve)] if dcx >= 2 else [])
                        S.emit("tensor",
                               lambda e, dl=dl, fg=fg, pf=pf, wb2=wb2: e.matmul(
                                   pf[:, 0:256], lhsT=wb2[:, fg, dl * 128:(dl + 1) * 128],
                                   rhs=hh5[:, fg, :], start=(fg == 0), stop=(fg == F // 128 - 1)),
                               waits=wz,
                               inc=pe if fg == F // 128 - 1 else None)
                    if exl == 0 and ct == CAP // 256 - 1 and dcx in (3, 7):
                        load_w(1, "w2A" if dcx == 3 else "w2B", [S.now(pe)])
                    pv = S.now(pe)
                    S.emit("vector",
                           lambda e, dcx=dcx, pf=pf: e.tensor_copy(zt5[:, dcx, :], pf[:, 0:256]),
                           waits=[pv], inc=dve)
                ztv = S.now(dve)
                tmarks2 = []
                for i in range(16):
                    sc, dcx = i // 8, i % 8
                    ptf = p_tf5[i % 2]
                    wt = [ztv] + ([tmarks2[i - 2]] if i >= 2 else [])
                    S.emit("tensor",
                           lambda e, sc=sc, dcx=dcx, ptf=ptf: e.transpose(
                               ptf[:], zt5[:, dcx, sc * 128:(sc + 1) * 128], id128f[:]),
                           waits=wt, inc=pe)
                    pv = S.now(pe)
                    S.emit("vector",
                           lambda e, sc=sc, dcx=dcx, ptf=ptf: e.tensor_copy(
                               otok5[:, sc, dcx * 128:(dcx + 1) * 128], ptf[:]),
                           waits=[pv], inc=dve)
                    tmarks2.append(S.now(dve))
                otv = S.now(dve)
                eL = eoutLa if exl == 0 else eoutLb
                S.dma("sync", eL[ct * 256:(ct + 1) * 256, :].rearrange(
                    "(s p) d2 -> p s d2", p=128), otok5[:], waits=[otv], inc=sto)
                prev_ct_sto = S.now(sto)
            if exl == 0:
                S.emit("gpsimd", lambda e: e.collective_compute(
                    "AllGather", ALU.bypass, replica_groups=[list(range(8))],
                    ins=[eoutLa.ap()], outs=[eoutAG.ap()[0:8 * CAP, :]]),
                    waits=[S.now(sto)], inc=cc)
                agvA = S.now(cc)
        S.emit("gpsimd", lambda e: e.collective_compute(
            "AllGather", ALU.bypass, replica_groups=[list(range(8))],
            ins=[eoutLb.ap()], outs=[eoutAG.ap()[8 * CAP:E * CAP, :]]),
            waits=[S.now(sto)], inc=cc)
        agvB = S.now(cc)
        barrier(exclude=("cc",))
        stomarks5 = []
        for lt in range(16):
            par = lt % 2
            S.dma("sync", x3f5[par][:], x3tokD[lt * 128:(lt + 1) * 128, :],
                  waits=[S.now(dve)] if lt >= 2 else [], inc=ld)
            xv5 = S.now(ld)
            vstep(lambda e: e.memset(combB[:], 0.0))
            S.emit("gpsimd",
                   lambda e, lt=lt: e.indirect_dma_start(
                       out=combB[:], out_offset=None,
                       in_=eoutAG[0:128, :],
                       in_offset=bass.IndirectOffsetOnAxis(kidxC[0][:, lt:lt + 1], 0),
                       bounds_check=E * CAP - 1, oob_is_err=False),
                   waits=[agvA, agvB, S.now(dve)], inc=gld, inc_k=16)
            gv5 = S.now(gld)
            vstep(lambda e, lt=lt: e.tensor_scalar_mul(comb5[:], combB[:], psel[:, lt:lt + 1]),
                  extra=[gv5] + ([stomarks5[lt - 1]] if lt >= 1 else []))
            vstep(lambda e, par=par: e.tensor_tensor(comb5[:], comb5[:], x3f5[par][:], ALU.add),
                  extra=[xv5])
            yv5 = S.now(dve)
            vstep(lambda e: e.reduce_sum(rsn, comb5[:], axis=AX.X, negate=True))
            vstep(lambda e: e.tensor_scalar_mul(nmu, rsn, 1.0 / D))
            S.emit("scalar", lambda e: e.activation(sqt5[:], comb5[:], AF.Square,
                                                    accum_out=ssq),
                   waits=[yv5], inc=act)
            vstep(lambda e: e.tensor_scalar_mul(ex2, ssq, 1.0 / D), extra=[S.now(act)])
            vstep(lambda e: e.tensor_tensor(var, nmu, nmu, ALU.mult))
            vstep(lambda e: e.tensor_tensor(var, ex2, var, ALU.subtract))
            S.emit("scalar", lambda e: e.activation(var, var, AF.Ln, bias=eps128[:], scale=1.0),
                   waits=[S.now(dve)], inc=act)
            S.emit("scalar", lambda e: e.activation(var, var, AF.Exp, bias=0.0, scale=-0.5),
                   waits=[S.now(act)], inc=act)
            rv = S.now(act)
            vstep(lambda e: e.tensor_scalar(comb5[:], comb5[:], nmu, var, ALU.add, ALU.mult),
                  extra=[rv])
            ov = S.now(dve)
            S.dma("sync", x4tok[lt * 128:(lt + 1) * 128, :], comb5[:], waits=[ov], inc=sto)
            stomarks5.append(S.now(sto))
        if debug:
            S.dma("sync", dbg_ctx1, ag2in.ap(), waits=[S.now(sto)], inc=sto)
            S.dma("sync", dbg_x2, x2loc.ap(), waits=[S.now(sto)], inc=sto)
            S.dma("sync", dbg_x3, x3tokD.ap(), waits=[S.now(sto)], inc=sto)
            S.dma("sync", dbg_lg, lg[:], waits=[S.now(dve)], inc=sto)
            S.dma("sync", dbg_kf, psel[:], waits=[S.now(dve)], inc=sto)
        S.emit("sync", lambda e: e.nop(), waits=[S.now(sto)])
        S.build()
    return nc


def kernel_fused(**inputs):
    nc = _get("fused", build_fused)
    x = np.asarray(inputs["L0_attn_x"], dtype=np.float32)
    xf = x.reshape(T, D)
    g0T = _f(np.asarray(inputs["L0_ffn_ln_gamma"]).reshape(DC, 128).T)
    b0T = _f(np.asarray(inputs["L0_ffn_ln_beta"]).reshape(DC, 128).T)
    b1T = _f(np.asarray(inputs["L0_ffn_fc1_bias"]).reshape(F // 128, 128).T)
    b2T = _f(np.asarray(inputs["L0_ffn_fc2_bias"]).reshape(DC, 128).T)
    fc1q_h = np.ascontiguousarray(np.asarray(inputs["L0_ffn_fc1_q"], dtype=np.int32))
    fc2q_h = np.ascontiguousarray(np.asarray(inputs["L0_ffn_fc2_q"], dtype=np.int32))
    sc1_h = _f(inputs["L0_ffn_fc1_scales"])
    sc2_h = _f(inputs["L0_ffn_fc2_scales"])
    Wq = np.asarray(inputs["L1_attn_Wq"]); Wk = np.asarray(inputs["L1_attn_Wk"])
    Wv = np.asarray(inputs["L1_attn_Wv"]); Wo = np.asarray(inputs["L1_attn_Wo"])
    bq = np.asarray(inputs["L1_attn_bq"]); bk = np.asarray(inputs["L1_attn_bk"])
    bv = np.asarray(inputs["L1_attn_bv"]); bo = np.asarray(inputs["L1_attn_bo"])
    WoTh = _f(Wo.T)
    boT = _f(bo.reshape(DC, 128).T)
    gTh = _f(np.asarray(inputs["L1_attn_gamma"]).reshape(DC, 128).T)
    bTh = _f(np.asarray(inputs["L1_attn_beta"]).reshape(DC, 128).T)
    gwTh = _f(np.asarray(inputs["L1_ffn_gate_w"]).T)
    w1 = np.asarray(inputs["L1_ffn_w1"]); w2 = np.asarray(inputs["L1_ffn_w2"])
    tm = _l1_trimask()
    i16 = np.eye(16, dtype=np.float32)
    i128 = np.eye(128, dtype=np.float32)
    bimp_h = _f(np.asarray(inputs["L1_attn_band_importance"]).reshape(1, 2048))
    maps = []
    for c in range(8):
        sl = slice(c * TPC, (c + 1) * TPC)
        b, hg = c // 2, c % 2
        rs = slice(hg * 512, (hg + 1) * 512)
        UT, pm, io, iob, bigph = _l5_consts(c)
        maps.append(dict(
            xT=_f(xf[sl].T), xtok=_f(xf[sl]), trimask=tm,
            g0T=g0T, b0T=b0T, fc1q=fc1q_h, sc1=sc1_h, b1T=b1T,
            fc2q=fc2q_h, sc2=sc2_h, b2T=b2T,
            WqT=_f(Wq[rs, :].T), WkT=_f(Wk[rs, :].T), WvT=_f(Wv[rs, :].T),
            bqT=_f(bq[rs].reshape(4, 128).T), bkT=_f(bk[rs].reshape(4, 128).T),
            bvrow=_f(bv[rs].reshape(1, 512)), bimp=bimp_h, id16=i16,
            WoT=WoTh, boT=boT, gT=gTh, bT=bTh, gwT=gwTh,
            w1d=_f(np.stack([w1[c], w1[c + 8]])), w2d=_f(np.stack([w2[c], w2[c + 8]])),
            UTd=UT, pmaskd=pm, iota16d=io, iotaBIGd=iob, bigpd=bigph, idd=i128,
        ))
    res = run_bass_kernel_spmd(nc, maps, core_ids=list(range(8)))
    out = np.concatenate([np.asarray(r["x4tok"]) for r in res.results], axis=0)
    return np.ascontiguousarray(out.reshape(B, L, D), dtype=np.float32)


def _f(x):
    return np.ascontiguousarray(np.asarray(x), dtype=np.float32)


def _prep_l2_maps(x, ctx_, ln_g, ln_b, fc1_q, fc1_s, fc1_b, fc2_q, fc2_s, fc2_b):
    maps = []
    g0T = _f(np.asarray(ln_g).reshape(DC, 128).T)
    b0T = _f(np.asarray(ln_b).reshape(DC, 128).T)
    b1T = _f(np.asarray(fc1_b).reshape(F // 128, 128).T)
    b2T = _f(np.asarray(fc2_b).reshape(DC, 128).T)
    fc1q = np.ascontiguousarray(np.asarray(fc1_q, dtype=np.int32))
    fc2q = np.ascontiguousarray(np.asarray(fc2_q, dtype=np.int32))
    sc1 = _f(fc1_s)
    sc2 = _f(fc2_s)
    for c in range(8):
        sl = slice(c * TPC, (c + 1) * TPC)
        maps.append(dict(
            xT=_f(x[sl].T), ctxT=_f(ctx_[sl].T), g0T=g0T, b0T=b0T,
            fc1q=fc1q, sc1=sc1, b1T=b1T, fc2q=fc2q, sc2=sc2, b2T=b2T,
        ))
    return maps


_BUILT = {}


def _get(name, builder):
    if name not in _BUILT:
        _BUILT[name] = builder()
    return _BUILT[name]


def run_l2(x, ctx_, ln_g, ln_b, fc1_q, fc1_s, fc1_b, fc2_q, fc2_s, fc2_b):
    nc = _get("l2", build_l2)
    maps = _prep_l2_maps(x, ctx_, ln_g, ln_b, fc1_q, fc1_s, fc1_b, fc2_q, fc2_s, fc2_b)
    res = run_bass_kernel_spmd(nc, maps, core_ids=list(range(8)))
    return np.concatenate([r["x2T"].T for r in res.results], axis=0)


def kernel(**inputs):
    return kernel_fused(**inputs)


def kernel_unfused(**inputs):
    x = np.asarray(inputs["L0_attn_x"], dtype=np.float32)
    ctx1 = run_l1(x)
    xf = np.ascontiguousarray(x.reshape(T, D))
    x2 = run_l2(xf, ctx1.reshape(T, D),
                inputs["L0_ffn_ln_gamma"], inputs["L0_ffn_ln_beta"],
                inputs["L0_ffn_fc1_q"], inputs["L0_ffn_fc1_scales"], inputs["L0_ffn_fc1_bias"],
                inputs["L0_ffn_fc2_q"], inputs["L0_ffn_fc2_scales"], inputs["L0_ffn_fc2_bias"])
    ctx2 = run_l3(x2.reshape(B, L, D),
                  inputs["L1_attn_Wq"], inputs["L1_attn_bq"],
                  inputs["L1_attn_Wk"], inputs["L1_attn_bk"],
                  inputs["L1_attn_Wv"], inputs["L1_attn_bv"],
                  inputs["L1_attn_band_importance"])
    x3, lgts = run_l4(x2, ctx2.reshape(T, D),
                      inputs["L1_attn_Wo"], inputs["L1_attn_bo"],
                      inputs["L1_attn_gamma"], inputs["L1_attn_beta"],
                      inputs["L1_ffn_gate_w"])
    out = run_l5(x3, lgts, inputs["L1_ffn_w1"], inputs["L1_ffn_w2"])
    return np.ascontiguousarray(out.reshape(B, L, D), dtype=np.float32)


# version 10
# speedup vs baseline: 1.0191x; 1.0191x over previous
import numpy as np
import ml_dtypes
import concourse.bass as bass
import concourse.mybir as mybir
from concourse.bass_utils import run_bass_kernel_spmd
from contextlib import ExitStack

F32 = mybir.dt.float32
BF16 = mybir.dt.bfloat16
I32 = mybir.dt.int32
U32 = mybir.dt.uint32
AF = mybir.ActivationFunctionType
ALU = mybir.AluOpType
AX = mybir.AxisListType

B, L, D, H, HD = 4, 4096, 1024, 16, 64
T = B * L
TPC = T // 8
DC = D // 128
F = 4096
E = 16
CAP = 1280
CPAD = 256
MSHIFT = 32.0
EPS = 1e-5
NEGBIG = -1e30

BF = ml_dtypes.bfloat16


class Sched:

    ENGINES = ("sync", "gpsimd", "scalar", "vector", "tensor")

    def __init__(self, nc, ctx):
        self.nc = nc
        self.ctx = ctx
        self.ops = {e: [] for e in self.ENGINES}
        self.cnt = {}

    def sem(self, name):
        h = self.ctx.enter_context(self.nc.semaphore(name))
        self.cnt[name] = [h, 0]
        return name

    def now(self, sem):
        return (self.cnt[sem][0], self.cnt[sem][1])

    def emit(self, eng, fn, waits=(), inc=None, inc_k=1):
        ws = []
        for w in waits:
            if isinstance(w, str):
                h, n = self.cnt[w]
                if n > 0:
                    ws.append((h, n))
            else:
                if w[1] > 0:
                    ws.append(w)
        pair = None
        if inc is not None:
            c = self.cnt[inc]
            c[1] += inc_k
            pair = (c[0], inc_k)
        self.ops[eng].append((fn, ws, pair))

    def dma(self, eng, out, in_, waits=(), inc=None):
        self.emit(eng, lambda e, o=out, i=in_: e.dma_start(out=o, in_=i),
                  waits=waits, inc=inc, inc_k=16)

    def build(self):
        block = self.ctx.enter_context(self.nc.Block())

        def section(eng_ops):
            def run(eng):
                for fn, ws, pair in eng_ops:
                    for h, n in ws:
                        eng.wait_ge(h, n)
                    inst = fn(eng)
                    if pair is not None:
                        assert inst is not None
                        inst.then_inc(pair[0], pair[1])
            return run

        block.sync(section(self.ops["sync"]))
        block.gpsimd(section(self.ops["gpsimd"]))
        block.scalar(section(self.ops["scalar"]))
        block.vector(section(self.ops["vector"]))
        block.tensor(section(self.ops["tensor"]))


def emit_ln(S, src, dst, nchunk, tw, cons, psum, pe, act, dve, wait_src,
            gamma=None, beta=None, dst_cast=None):
    ones128, ones1 = cons["ones128"], cons["ones1"]
    sq, st4, tmp = cons["sq"], cons["st4"], cons["tmp"]
    epsc = cons["epsc"]
    p_s, p_q, p_a, p_b = psum["s"], psum["q"], psum["a"], psum["b"]
    nf = 128 * nchunk
    for c in range(nchunk):
        S.emit("tensor", lambda e, c=c: e.matmul(p_s, lhsT=ones128[:], rhs=src(c),
                                                 start=(c == 0), stop=(c == nchunk - 1)),
               waits=[wait_src] if c == 0 else (), inc=pe if c == nchunk - 1 else None)
    qmarks = []
    for c in range(nchunk):
        w = [wait_src] if c == 0 else ([qmarks[c - 2]] if c >= 2 else [])
        S.emit("scalar", lambda e, c=c: e.activation(sq[c % 2][:, 0:tw], src(c), AF.Square),
               waits=w, inc=act)
        aw = S.now(act)
        S.emit("tensor", lambda e, c=c: e.matmul(p_q, lhsT=ones128[:], rhs=sq[c % 2][:, 0:tw],
                                                 start=(c == 0), stop=(c == nchunk - 1)),
               waits=[aw], inc=pe)
        qmarks.append(S.now(pe))
    stat_done = S.now(pe)
    nmu = st4[0:1, 0 * tw:1 * tw]
    ex2 = st4[0:1, 1 * tw:2 * tw]
    var = st4[0:1, 2 * tw:3 * tw]
    bb = st4[0:1, 3 * tw:4 * tw]
    S.emit("vector", lambda e: e.tensor_scalar_mul(nmu, p_s, -1.0 / nf), waits=[stat_done],
           inc=dve)
    S.emit("vector", lambda e: e.tensor_scalar_mul(ex2, p_q, 1.0 / nf), inc=dve)
    S.emit("vector", lambda e: e.tensor_tensor(var, nmu, nmu, ALU.mult),
           waits=[S.now(dve)], inc=dve)
    S.emit("vector", lambda e: e.tensor_tensor(var, ex2, var, ALU.subtract),
           waits=[S.now(dve)], inc=dve)
    dv = S.now(dve)
    S.emit("scalar", lambda e: e.activation(var, var, AF.Ln, bias=epsc, scale=1.0), waits=[dv],
           inc=act)
    S.emit("scalar", lambda e: e.activation(var, var, AF.Exp, bias=0.0, scale=-0.5),
           waits=[S.now(act)], inc=act)
    sa = S.now(act)
    S.emit("vector", lambda e: e.tensor_tensor(bb, nmu, var, ALU.mult), waits=[sa], inc=dve)
    dv2 = S.now(dve)
    S.emit("tensor", lambda e: e.matmul(p_a, lhsT=ones1[:], rhs=var, start=True, stop=True),
           waits=[dv2])
    S.emit("tensor", lambda e: e.matmul(p_b, lhsT=ones1[:], rhs=bb, start=True, stop=True),
           inc=pe)
    ab = S.now(pe)
    for c in range(nchunk):
        w = [ab] if c == 0 else ()
        if gamma is None:
            S.emit("vector", lambda e, c=c: e.tensor_tensor(dst(c), src(c), p_a, ALU.mult), waits=w)
            S.emit("vector", lambda e, c=c: e.tensor_tensor(dst(c), dst(c), p_b, ALU.add), inc=dve)
        else:
            w = list(w) + ([S.now(act)] if c >= 1 else [])
            S.emit("vector", lambda e, c=c: e.tensor_tensor(tmp[:, 0:tw], src(c), p_a, ALU.mult), waits=w)
            S.emit("vector", lambda e, c=c: e.tensor_tensor(tmp[:, 0:tw], tmp[:, 0:tw], p_b, ALU.add),
                   inc=dve)
            S.emit("vector", lambda e, c=c: e.tensor_scalar(dst_cast(c), tmp[:, 0:tw],
                                                            gamma(c), beta(c),
                                                            ALU.mult, ALU.add),
                   inc=dve)
    return S.now(dve)


def build_l2(tw=256, debug=False):
    nt = TPC // tw
    nc = bass.Bass("TRN2", target_bir_lowering=False)
    xT = nc.dram_tensor("xT", [D, TPC], F32, kind="ExternalInput").ap()
    ctxT = nc.dram_tensor("ctxT", [D, TPC], F32, kind="ExternalInput").ap()
    g0T = nc.dram_tensor("g0T", [128, DC], F32, kind="ExternalInput").ap()
    b0T = nc.dram_tensor("b0T", [128, DC], F32, kind="ExternalInput").ap()
    fc1q = nc.dram_tensor("fc1q", [D, F], I32, kind="ExternalInput").ap()
    sc1 = nc.dram_tensor("sc1", [DC, F], F32, kind="ExternalInput").ap()
    b1T = nc.dram_tensor("b1T", [128, F // 128], F32, kind="ExternalInput").ap()
    fc2q = nc.dram_tensor("fc2q", [F, D], I32, kind="ExternalInput").ap()
    sc2 = nc.dram_tensor("sc2", [F // 128, D], F32, kind="ExternalInput").ap()
    b2T = nc.dram_tensor("b2T", [128, DC], F32, kind="ExternalInput").ap()
    x2T = nc.dram_tensor("x2T", [D, TPC], F32, kind="ExternalOutput").ap()
    if debug:
        dbg_y1 = nc.dram_tensor("dbg_y1", [D, TPC], F32, kind="ExternalOutput").ap()
        dbg_x1 = nc.dram_tensor("dbg_x1", [D, TPC], F32, kind="ExternalOutput").ap()
        dbg_h = nc.dram_tensor("dbg_h", [D, TPC], BF16, kind="ExternalOutput").ap()
        dbg_y = nc.dram_tensor("dbg_y", [F, TPC], BF16, kind="ExternalOutput").ap()
        dbg_w1 = nc.dram_tensor("dbg_w1", [D, F], BF16, kind="ExternalOutput").ap()
        dbg_w2 = nc.dram_tensor("dbg_w2", [F, D], BF16, kind="ExternalOutput").ap()

    with ExitStack() as ctx:
        En = ctx.enter_context
        S = Sched(nc, ctx)
        w1 = En(nc.sbuf_tensor("w1", [128, DC, F], BF16))
        w2 = En(nc.sbuf_tensor("w2", [128, F // 128, D], BF16))
        qbuf = [En(nc.sbuf_tensor(f"qbuf{i}", [128, 512], F32)) for i in range(2)]
        scrow = [En(nc.sbuf_tensor(f"scrow{i}", [1, 512], F32)) for i in range(2)]
        ya = En(nc.sbuf_tensor("ya", [128, DC, tw], F32))
        x1 = En(nc.sbuf_tensor("x1", [128, DC, tw], F32))
        hb = En(nc.sbuf_tensor("hb", [128, DC, tw], BF16))
        yb = En(nc.sbuf_tensor("yb", [128, F // 128, tw], BF16))
        sq = [En(nc.sbuf_tensor(f"sq{i}", [128, tw], F32)) for i in range(2)]
        st4 = En(nc.sbuf_tensor("st4", [1, 4 * tw], F32))
        tmp = En(nc.sbuf_tensor("tmp", [128, tw], F32))
        fo = En(nc.sbuf_tensor("fo", [128, tw], F32))
        ones128 = En(nc.sbuf_tensor("ones128", [128, 1], F32))
        ones1 = En(nc.sbuf_tensor("ones1", [1, 128], F32))
        gg = En(nc.sbuf_tensor("gg", [128, DC], F32))
        bb0 = En(nc.sbuf_tensor("bb0", [128, DC], F32))
        bb1 = En(nc.sbuf_tensor("bb1", [128, F // 128], F32))
        hb2 = En(nc.sbuf_tensor("hb2", [128, DC], F32))
        p_ping = En(nc.psum_tensor("p_ping", [128, 512], F32))
        p_pong = En(nc.psum_tensor("p_pong", [128, 512], F32))
        p_s = En(nc.psum_tensor("p_s", [1, 512], F32))
        p_q = En(nc.psum_tensor("p_q", [1, 512], F32))
        p_y = [En(nc.psum_tensor(f"p_y{i}", [128, 512], F32)) for i in range(2)]
        p_z = [En(nc.psum_tensor(f"p_z{i}", [128, 512], F32)) for i in range(2)]

        ld = S.sem("ld")
        gld = S.sem("gld")
        pe = S.sem("pe")
        act = S.sem("act")
        dve = S.sem("dve")
        sto = S.sem("sto")

        epsc = En(nc.sbuf_tensor("epsc", [1, 1], F32))
        cons = dict(ones128=ones128, ones1=ones1, sq=sq, st4=st4, tmp=tmp,
                    epsc=epsc[0:1, 0:1])
        psd = dict(s=p_s[0:1, 0:tw], q=p_q[0:1, 0:tw],
                   a=p_ping[:, 0:tw], b=p_pong[:, 0:tw])

        S.emit("vector", lambda e: e.memset(ones128[:], 1.0), inc=dve)
        S.emit("vector", lambda e: e.memset(epsc[:], EPS), inc=dve)
        S.emit("vector", lambda e: e.memset(ones1[:], 1.0), inc=dve)
        S.dma("sync", gg[:], g0T, inc=ld)
        S.dma("sync", bb0[:], b0T, inc=ld)
        S.dma("sync", bb1[:], b1T, inc=ld)
        S.dma("sync", hb2[:], b2T, inc=ld)
        S.emit("vector", lambda e: e.tensor_scalar_mul(hb2[:], hb2[:], 0.5),
               waits=[S.now(ld)], inc=dve)
        constv = S.now(dve)

        dmarks = []

        def dequant(qdram, scdram, ngrp, ncol, wdst):
            nfs = ncol // 512
            for g in range(ngrp):
                for fs in range(nfs):
                    i = len(dmarks)
                    pb = i % 2
                    ppsum = p_ping if pb == 0 else p_pong
                    wprev = [dmarks[i - 2]] if i >= 2 else []
                    S.dma("gpsimd", qbuf[pb][:],
                          qdram[g * 128:(g + 1) * 128, fs * 512:(fs + 1) * 512],
                          waits=wprev, inc=gld)
                    S.dma("sync", scrow[pb][:],
                          scdram[g:g + 1, fs * 512:(fs + 1) * 512],
                          waits=wprev, inc=ld)
                    gl, ll = S.now(gld), S.now(ld)
                    S.emit("tensor",
                           lambda e, ppsum=ppsum, pb=pb: e.matmul(
                               ppsum[:], lhsT=ones1[:], rhs=scrow[pb][:],
                               start=True, stop=True),
                           waits=[ll, constv] + wprev, inc=pe)
                    pv = S.now(pe)
                    S.emit("vector",
                           lambda e, ppsum=ppsum, pb=pb, g=g, fs=fs: e.scalar_tensor_tensor(
                               wdst[:, g, fs * 512:(fs + 1) * 512],
                               qbuf[pb][:], -128.0, ppsum[:], ALU.add, ALU.mult),
                           waits=[pv, gl], inc=dve)
                    dmarks.append(S.now(dve))

        dequant(fc1q, sc1, DC, F, w1)
        dequant(fc2q, sc2, F // 128, D, w2)
        wready = S.now(dve)
        if debug:
            S.dma("sync", dbg_w1.rearrange("(g p) f -> p g f", p=128), w1[:],
                  waits=[wready], inc=sto)
            S.dma("sync", dbg_w2.rearrange("(g p) f -> p g f", p=128), w2[:],
                  waits=[wready], inc=sto)

        for it in range(nt):
            tsl = slice(it * tw, (it + 1) * tw)
            lw = [S.now(dve), S.now(sto)]
            S.dma("sync", ya[:], xT[:, tsl].rearrange("(c p) t -> p c t", p=128),
                  waits=lw, inc=ld)
            xv = S.now(ld)
            S.emit("gpsimd",
                   lambda e, tsl=tsl: e.dma_start(
                       out=ya[:], in_=ctxT[:, tsl].rearrange("(c p) t -> p c t", p=128),
                       accum_op=ALU.add),
                   waits=[xv], inc=gld, inc_k=16)
            y1v = S.now(gld)
            if debug:
                S.dma("sync", dbg_y1[:, tsl].rearrange("(c p) t -> p c t", p=128), ya[:],
                      waits=[y1v], inc=sto)
            emit_ln(S, lambda c: ya[:, c, :], lambda c: x1[:, c, :], DC, tw,
                    cons, psd, pe, act, dve, y1v)
            x1v = S.now(dve)
            if debug:
                S.dma("sync", dbg_x1[:, tsl].rearrange("(c p) t -> p c t", p=128), x1[:],
                      waits=[x1v], inc=sto)
            hv = emit_ln(S, lambda c: x1[:, c, :], lambda c: tmp[:, 0:tw], DC, tw,
                         cons, psd, pe, act, dve, x1v,
                         gamma=lambda c: gg[:, c:c + 1], beta=lambda c: bb0[:, c:c + 1],
                         dst_cast=lambda c: hb[:, c, :])
            if debug:
                S.dma("sync", dbg_h[:, tsl].rearrange("(c p) t -> p c t", p=128), hb[:],
                      waits=[hv], inc=sto)
            for fg in range(F // 128):
                pyb = p_y[fg % 2]
                wy = [S.now(act)] if fg >= 2 else [hv, wready]
                for g in range(DC):
                    S.emit("tensor",
                           lambda e, g=g, fg=fg, pyb=pyb: e.matmul(
                               pyb[:, 0:tw], lhsT=w1[:, g, fg * 128:(fg + 1) * 128],
                               rhs=hb[:, g, :], start=(g == 0), stop=(g == DC - 1)),
                           waits=wy if g == 0 else (),
                           inc=pe if g == DC - 1 else None)
                pv = S.now(pe)
                S.emit("scalar",
                       lambda e, fg=fg, pyb=pyb: e.activation(
                           yb[:, fg, :], pyb[:, 0:tw], AF.Silu,
                           bias=bb1[:, fg:fg + 1], scale=1.0),
                       waits=[pv], inc=act)
            yv = S.now(act)
            if debug:
                S.dma("sync", dbg_y[:, tsl].rearrange("(c p) t -> p c t", p=128), yb[:],
                      waits=[yv], inc=sto)
            for dc_ in range(DC):
                pz = p_z[dc_ % 2]
                wz = [S.now(dve)] if dc_ >= 2 else [yv]
                for fg in range(F // 128):
                    S.emit("tensor",
                           lambda e, dc_=dc_, fg=fg, pz=pz: e.matmul(
                               pz[:, 0:tw], lhsT=w2[:, fg, dc_ * 128:(dc_ + 1) * 128],
                               rhs=yb[:, fg, :], start=(fg == 0), stop=(fg == F // 128 - 1)),
                           waits=(wz + [yv]) if fg == 0 else (),
                           inc=pe if fg == F // 128 - 1 else None)
                zv = S.now(pe)
                S.emit("vector",
                       lambda e, dc_=dc_, pz=pz: e.tensor_scalar(
                           tmp[:, 0:tw], pz[:, 0:tw], 0.5, hb2[:, dc_:dc_ + 1],
                           ALU.mult, ALU.add),
                       waits=[zv], inc=dve)
                S.emit("vector",
                       lambda e, dc_=dc_: e.tensor_tensor(fo[:, 0:tw], tmp[:, 0:tw],
                                                          x1[:, dc_, :], ALU.add),
                       waits=[S.now(dve)], inc=dve)
                S.emit("vector",
                       lambda e, dc_=dc_: e.tensor_tensor(ya[:, dc_, :], fo[:, 0:tw],
                                                          x1[:, dc_, :], ALU.add),
                       waits=[S.now(dve)], inc=dve)
            y2v = S.now(dve)
            emit_ln(S, lambda c: ya[:, c, :], lambda c: x1[:, c, :], DC, tw,
                    cons, psd, pe, act, dve, y2v)
            ov = S.now(dve)
            S.dma("sync", x2T[:, tsl].rearrange("(c p) t -> p c t", p=128), x1[:],
                  waits=[ov], inc=sto)
        S.emit("sync", lambda e: e.nop(), waits=[S.now(sto)])
        S.build()
    return nc



def build_l1():
    nc = bass.Bass("TRN2", target_bir_lowering=False)
    xT = nc.dram_tensor("xT", [512, L], F32, kind="ExternalInput").ap()
    xtok = nc.dram_tensor("xtok", [L, 512], F32, kind="ExternalInput").ap()
    trimask = nc.dram_tensor("trimask", [128, 4 * 512], F32, kind="ExternalInput").ap()
    ctxT = nc.dram_tensor("ctxT", [512, L], F32, kind="ExternalOutput").ap()

    NQB = L // 512
    with ExitStack() as ctx:
        En = ctx.enter_context
        S = Sched(nc, ctx)
        xh = En(nc.sbuf_tensor("xh", [128, 4, L], BF16))
        vt = [En(nc.sbuf_tensor(f"vt{h}", [128, 32, 65], BF16)) for h in range(8)]
        tri = En(nc.sbuf_tensor("tri", [128, 4 * 512], BF16))
        pt = [En(nc.sbuf_tensor(f"pt{i}", [128, 512], BF16)) for i in range(4)]
        cobuf = En(nc.sbuf_tensor("cobuf", [128, 4, L], F32))
        ctmp = En(nc.sbuf_tensor("ctmp", [64, 512], F32))
        linv = En(nc.sbuf_tensor("linv", [1, 512], F32))
        ones1 = En(nc.sbuf_tensor("ones1", [1, 128], F32))
        mshift = En(nc.sbuf_tensor("mshift", [128, 1], F32))
        p_sA = [En(nc.psum_tensor(f"p_sA{i}", [128, 512], F32)) for i in range(2)]
        p_sB = [En(nc.psum_tensor(f"p_sB{i}", [128, 512], F32)) for i in range(2)]
        p_cA = En(nc.psum_tensor("p_cA", [65, 512], F32))
        p_cB = En(nc.psum_tensor("p_cB", [65, 512], F32))
        p_lb = En(nc.psum_tensor("p_lb", [64, 512], F32))

        ld = S.sem("ld")
        gld = S.sem("gld")
        pe = S.sem("pe")
        act = S.sem("act")
        dve = S.sem("dve")
        sto = S.sem("sto")

        S.emit("vector", lambda e: e.memset(ones1[:], 1.0), inc=dve)
        S.emit("vector", lambda e: e.memset(mshift[:], -MSHIFT), inc=dve)
        for h in range(8):
            S.emit("vector", lambda e, h=h: e.memset(vt[h][:], 1.0), inc=dve)
        memv = S.now(dve)
        S.dma("gpsimd", tri[:], trimask, inc=gld)
        S.dma("gpsimd", xh[:], xT.rearrange("(c p) t -> p c t", p=128), inc=gld)
        xhv = S.now(gld)
        for h in range(8):
            S.emit("gpsimd",
                   lambda e, h=h: e.dma_start(
                       out=vt[h][:, :, 0:64],
                       in_=xtok[:, 64 * h:64 * h + 64].rearrange("(c p) d -> p c d", p=128)),
                   waits=[memv], inc=gld, inc_k=16)
        vtv = S.now(gld)

        for hp in range(4):
            hA, hB = 2 * hp, 2 * hp + 1
            for qb in range(NQB):
                nkc = 4 * (qb + 1)
                evac_prev = S.now(dve)
                expmarks = {}
                pvmarks = {}
                for kc in range(nkc):
                    par = kc % 2
                    psA, psB = p_sA[par], p_sB[par]
                    wq = [xhv]
                    wq.append(expmarks[kc - 2] if kc >= 2 else S.now(act))
                    S.emit("tensor",
                           lambda e, hp=hp, qb=qb, kc=kc, psA=psA: e.matmul(
                               psA[:], lhsT=xh[0:64, hp, kc * 128:(kc + 1) * 128],
                               rhs=xh[0:64, hp, qb * 512:(qb + 1) * 512],
                               start=True, stop=True, tile_position=(0, 0)),
                           waits=wq)
                    S.emit("tensor",
                           lambda e, hp=hp, qb=qb, kc=kc, psB=psB: e.matmul(
                               psB[:], lhsT=xh[64:128, hp, kc * 128:(kc + 1) * 128],
                               rhs=xh[64:128, hp, qb * 512:(qb + 1) * 512],
                               start=True, stop=True, tile_position=(64, 0)),
                           inc=pe)
                    qkm = S.now(pe)
                    we = [qkm]
                    we.append(pvmarks[kc - 2] if kc >= 2 else S.now(pe))
                    S.emit("scalar",
                           lambda e, par=par, psA=psA: e.activation(
                               pt[2 * par + 0][:], psA[:], AF.Exp, bias=mshift[:, 0:1], scale=0.125),
                           waits=we + [memv], inc=act)
                    S.emit("scalar",
                           lambda e, par=par, psB=psB: e.activation(
                               pt[2 * par + 1][:], psB[:], AF.Exp, bias=mshift[:, 0:1], scale=0.125),
                           inc=act)
                    expmarks[kc] = S.now(act)
                    pvw = [expmarks[kc]]
                    if kc >= 4 * qb:
                        j = kc - 4 * qb
                        S.emit("vector",
                               lambda e, j=j, par=par: e.tensor_tensor(
                                   pt[2 * par + 0][:], pt[2 * par + 0][:],
                                   tri[:, j * 512:(j + 1) * 512], ALU.mult),
                               waits=[expmarks[kc], xhv])
                        S.emit("vector",
                               lambda e, j=j, par=par: e.tensor_tensor(
                                   pt[2 * par + 1][:], pt[2 * par + 1][:],
                                   tri[:, j * 512:(j + 1) * 512], ALU.mult),
                               inc=dve)
                        pvw = [S.now(dve)]
                    wv = pvw + ([vtv, evac_prev] if kc == 0 else [])
                    S.emit("tensor",
                           lambda e, hA=hA, kc=kc, par=par, nkc=nkc: e.matmul(
                               p_cA[:], lhsT=vt[hA][:, kc, :], rhs=pt[2 * par + 0][:],
                               start=(kc == 0), stop=(kc == nkc - 1)),
                           waits=wv)
                    S.emit("tensor",
                           lambda e, hB=hB, kc=kc, par=par, nkc=nkc: e.matmul(
                               p_cB[:], lhsT=vt[hB][:, kc, :], rhs=pt[2 * par + 1][:],
                               start=(kc == 0), stop=(kc == nkc - 1)),
                           inc=pe)
                    pvmarks[kc] = S.now(pe)
                ctxm = S.now(pe)
                for (pc, hloc) in ((p_cA, hA), (p_cB, hB)):
                    e0 = (hloc % 2) * 64
                    S.emit("vector", lambda e, pc=pc: e.reciprocal(linv[:], pc[64:65, :]),
                           waits=[ctxm, S.now(pe)], inc=dve)
                    lv = S.now(dve)
                    S.emit("tensor",
                           lambda e: e.matmul(p_lb[:], lhsT=ones1[0:1, 0:64], rhs=linv[:],
                                              start=True, stop=True),
                           waits=[lv], inc=pe)
                    lbv = S.now(pe)
                    S.emit("vector", lambda e, pc=pc: e.tensor_copy(ctmp[:], pc[0:64, :]),
                           waits=[lbv], inc=dve)
                    S.emit("vector",
                           lambda e, hp=hp, qb=qb, e0=e0: e.tensor_tensor(
                               cobuf[e0:e0 + 64, hp, qb * 512:(qb + 1) * 512],
                               ctmp[:], p_lb[:], ALU.mult),
                           waits=[S.now(dve)], inc=dve)
        allv = S.now(dve)
        S.dma("sync", ctxT.rearrange("(hp e r) t -> (e r) hp t", hp=4, e=2),
              cobuf[:], waits=[allv], inc=sto)
        S.emit("sync", lambda e: e.nop(), waits=[S.now(sto)])
        S.build()
    return nc


def _l1_trimask():
    m = np.zeros((128, 4, 512), np.float32)
    for j in range(4):
        for p in range(128):
            m[p, j, 128 * j + p:] = 1.0
    return np.ascontiguousarray(m.reshape(128, 4 * 512))


def run_l1(x):
    nc = _get("l1", build_l1)
    tm = _l1_trimask()
    maps = []
    for c in range(8):
        b, hg = c // 2, c % 2
        sl = slice(hg * 512, (hg + 1) * 512)
        maps.append(dict(xT=_f(x[b].T[sl]), xtok=_f(x[b][:, sl]), trimask=tm))
    res = run_bass_kernel_spmd(nc, maps, core_ids=list(range(8)))
    ctx_ = np.empty((B, L, D), np.float32)
    for c in range(8):
        b, hg = c // 2, c % 2
        ctx_[b, :, hg * 512:(hg + 1) * 512] = np.asarray(res.results[c]["ctxT"]).T
    return ctx_



def build_l3(debug=False):
    nc = bass.Bass("TRN2", target_bir_lowering=False)
    x2T = nc.dram_tensor("x2T", [D, L], F32, kind="ExternalInput").ap()
    WqT = nc.dram_tensor("WqT", [D, 512], F32, kind="ExternalInput").ap()
    WkT = nc.dram_tensor("WkT", [D, 512], F32, kind="ExternalInput").ap()
    WvT = nc.dram_tensor("WvT", [D, 512], F32, kind="ExternalInput").ap()
    bqT = nc.dram_tensor("bqT", [128, 4], F32, kind="ExternalInput").ap()
    bkT = nc.dram_tensor("bkT", [128, 4], F32, kind="ExternalInput").ap()
    bvrow = nc.dram_tensor("bvrow", [1, 512], F32, kind="ExternalInput").ap()
    bimp = nc.dram_tensor("bimp", [1, 2048], F32, kind="ExternalInput").ap()
    id16 = nc.dram_tensor("id16", [16, 16], F32, kind="ExternalInput").ap()
    ctxT = nc.dram_tensor("ctxT", [512, L], F32, kind="ExternalOutput").ap()
    mrow = nc.dram_tensor("mrow", [2048], F32)
    if debug:
        dbg_q = nc.dram_tensor("dbg_q", [512, L], BF16, kind="ExternalOutput").ap()
        dbg_k = nc.dram_tensor("dbg_k", [512, L], BF16, kind="ExternalOutput").ap()
        dbg_v = nc.dram_tensor("dbg_v", [128, 32, 8, 65], BF16, kind="ExternalOutput").ap()
        dbg_bias = nc.dram_tensor("dbg_bias", [128, 32], F32, kind="ExternalOutput").ap()

    NQB = L // 512
    NKC = L // 128
    with ExitStack() as ctx:
        En = ctx.enter_context
        S = Sched(nc, ctx)
        xh = En(nc.sbuf_tensor("xh", [128, DC, L], BF16))
        qh = En(nc.sbuf_tensor("qh", [128, 4, L], BF16))
        kh = En(nc.sbuf_tensor("kh", [128, 4, L], BF16))
        vtok = En(nc.sbuf_tensor("vtok", [128, 32, 8, 65], BF16))
        wbuf = En(nc.sbuf_tensor("wbuf", [128, DC, 512], BF16))
        bq_s = En(nc.sbuf_tensor("bq_s", [128, 4], F32))
        bk_s = En(nc.sbuf_tensor("bk_s", [128, 4], F32))
        bvr = En(nc.sbuf_tensor("bvr", [1, 512], F32))
        bv_b = En(nc.sbuf_tensor("bv_b", [128, 512], F32))
        pt = [En(nc.sbuf_tensor(f"pt{i}", [128, 512], BF16)) for i in range(4)]
        ctile = [En(nc.sbuf_tensor(f"ctile{i}", [128, 512], F32)) for i in range(2)]
        ctmp = En(nc.sbuf_tensor("ctmp", [64, 512], F32))
        linv = En(nc.sbuf_tensor("linv", [1, 512], F32))
        ones1 = En(nc.sbuf_tensor("ones1", [1, 128], F32))
        mshift = En(nc.sbuf_tensor("mshift", [128, 1], F32))
        negbig = En(nc.sbuf_tensor("negbig", [1, 1], F32))
        brow = En(nc.sbuf_tensor("brow", [1, 2048], F32))
        wrow = En(nc.sbuf_tensor("wrow", [1, 2048], F32))
        trow = En(nc.sbuf_tensor("trow", [1, 2048], BF16))
        bstat = En(nc.sbuf_tensor("bstat", [1, 4], F32))
        mask16 = En(nc.sbuf_tensor("mask16", [16, 128], F32))
        idb = En(nc.sbuf_tensor("idb", [16, 16], F32))
        biasall = En(nc.sbuf_tensor("biasall", [128, 32], F32))
        p_sA = [En(nc.psum_tensor(f"p_sA{i}", [128, 512], F32)) for i in range(2)]
        p_sB = [En(nc.psum_tensor(f"p_sB{i}", [128, 512], F32)) for i in range(2)]
        p_cA = En(nc.psum_tensor("p_cA", [65, 512], F32))
        p_cB = En(nc.psum_tensor("p_cB", [65, 512], F32))
        p_lb = En(nc.psum_tensor("p_lb", [128, 512], F32))

        ld = S.sem("ld")
        gld = S.sem("gld")
        pe = S.sem("pe")
        act = S.sem("act")
        dve = S.sem("dve")
        sto = S.sem("sto")

        S.emit("vector", lambda e: e.memset(ones1[:], 1.0), inc=dve)
        S.emit("vector", lambda e: e.memset(mshift[:], -MSHIFT), inc=dve)
        S.emit("vector", lambda e: e.memset(negbig[:], NEGBIG), inc=dve)
        S.emit("vector", lambda e: e.memset(vtok[:], 1.0), inc=dve)
        S.emit("vector", lambda e: e.memset(biasall[:], -MSHIFT), inc=dve)
        memv = S.now(dve)
        S.dma("gpsimd", xh[:], x2T.rearrange("(c p) t -> p c t", p=128), inc=gld)
        S.dma("gpsimd", wbuf[:], WqT.rearrange("(c p) o -> p c o", p=128), inc=gld)
        xhv = S.now(gld)
        S.dma("sync", bq_s[:], bqT, inc=ld)
        S.dma("sync", bk_s[:], bkT, inc=ld)
        S.dma("sync", bvr[:], bvrow, inc=ld)
        S.dma("sync", brow[:], bimp, inc=ld)
        S.dma("sync", idb[:], id16, inc=ld)
        ldv = S.now(ld)

        m1 = bstat[0:1, 0:1]
        m2 = bstat[0:1, 1:2]
        m3 = bstat[0:1, 2:3]
        nb = negbig[0:1, 0:1]
        def vstep(fn, extra=()):
            S.emit("vector", fn, waits=[S.now(dve)] + list(extra), inc=dve)
        vstep(lambda e: e.reduce_max(m1, brow[:], axis=AX.X), extra=[ldv, memv])
        vstep(lambda e: e.tensor_single_scalar(trow[:], brow[:], m1, ALU.is_ge))
        vstep(lambda e: e.scalar_tensor_tensor(wrow[:], trow[:], nb, brow[:],
                                               ALU.mult, ALU.add))
        vstep(lambda e: e.reduce_max(m2, wrow[:], axis=AX.X))
        vstep(lambda e: e.tensor_single_scalar(trow[:], wrow[:], m2, ALU.is_ge))
        vstep(lambda e: e.scalar_tensor_tensor(wrow[:], trow[:], nb, wrow[:],
                                               ALU.mult, ALU.add))
        vstep(lambda e: e.reduce_max(m3, wrow[:], axis=AX.X))
        vstep(lambda e: e.tensor_single_scalar(trow[:], brow[:], m3, ALU.is_ge))
        vstep(lambda e: e.tensor_scalar_mul(wrow[:], trow[:], nb))
        mv = S.now(dve)
        S.dma("sync", mrow.rearrange("(a b) -> a b", a=1), wrow[:], waits=[mv], inc=ld)
        S.dma("sync", mask16[:], mrow.rearrange("(c p) -> c p", c=16),
              waits=[S.now(ld)], inc=ld)
        mk16 = S.now(ld)
        S.emit("tensor", lambda e: e.transpose(p_lb[:, 0:16], mask16[:], idb[:]),
               waits=[mk16], inc=pe)
        trv = S.now(pe)
        S.emit("vector", lambda e: e.tensor_scalar(biasall[:, 0:16], p_lb[:, 0:16],
                                                   1.0, mshift[:, 0:1], ALU.mult, ALU.add),
               waits=[trv], inc=dve)
        biasv = S.now(dve)

        pmarks = []
        for wi, (wdram, dst, bias) in enumerate(((WqT, qh, bq_s), (WkT, kh, bk_s))):
            if wi == 1:
                S.dma("gpsimd", wbuf[:], wdram.rearrange("(c p) o -> p c o", p=128),
                      waits=[S.now(pe)], inc=gld)
            wldv = S.now(gld)
            for oc in range(4):
                for ts in range(8):
                    i = len(pmarks)
                    pp = p_sA[i % 2]
                    w0 = ([xhv, ldv] if i < 2 else [pmarks[i - 2]]) + [wldv]
                    for dcx in range(DC):
                        S.emit("tensor",
                               lambda e, dcx=dcx, oc=oc, ts=ts, pp=pp: e.matmul(
                                   pp[:], lhsT=wbuf[:, dcx, oc * 128:(oc + 1) * 128],
                                   rhs=xh[:, dcx, ts * 512:(ts + 1) * 512],
                                   start=(dcx == 0), stop=(dcx == DC - 1)),
                               waits=w0 if dcx == 0 else (),
                               inc=pe if dcx == DC - 1 else None)
                    pv = S.now(pe)
                    S.emit("vector",
                           lambda e, dst=dst, bias=bias, oc=oc, ts=ts, pp=pp: e.tensor_scalar_add(
                               dst[:, oc, ts * 512:(ts + 1) * 512], pp[:], bias[:, oc:oc + 1]),
                           waits=[pv], inc=dve)
                    pmarks.append(S.now(dve))
        S.dma("gpsimd", wbuf[:], WvT.rearrange("(c p) o -> p c o", p=128),
              waits=[S.now(pe)], inc=gld)
        wvldv = S.now(gld)
        S.emit("tensor", lambda e: e.matmul(p_sB[0][:], lhsT=ones1[:], rhs=bvr[:],
                                            start=True, stop=True),
               waits=[ldv, memv], inc=pe)
        S.emit("vector", lambda e: e.tensor_copy(bv_b[:], p_sB[0][:]),
               waits=[S.now(pe)], inc=dve)
        bvv = S.now(dve)
        for tc in range(32):
            i = len(pmarks)
            pp = p_sA[i % 2]
            w0 = [pmarks[i - 2], wvldv]
            for dcx in range(DC):
                S.emit("tensor",
                       lambda e, dcx=dcx, tc=tc, pp=pp: e.matmul(
                           pp[:], lhsT=xh[:, dcx, tc * 128:(tc + 1) * 128],
                           rhs=wbuf[:, dcx, :], start=(dcx == 0), stop=(dcx == DC - 1)),
                       waits=w0 if dcx == 0 else (),
                       inc=pe if dcx == DC - 1 else None)
            pv = S.now(pe)
            S.emit("vector",
                   lambda e, tc=tc, pp=pp: e.scalar_tensor_tensor(
                       vtok[:, tc, :, 0:64], pp.ap().rearrange("p (h d) -> p h d", h=8),
                       1.0, bv_b.ap().rearrange("p (h d) -> p h d", h=8),
                       ALU.mult, ALU.add),
                   waits=[pv, bvv], inc=dve)
            pmarks.append(S.now(dve))
        projv = S.now(dve)
        if debug:
            S.dma("sync", dbg_q.rearrange("(c p) t -> p c t", p=128), qh[:],
                  waits=[projv], inc=sto)
            S.dma("sync", dbg_k.rearrange("(c p) t -> p c t", p=128), kh[:],
                  waits=[projv], inc=sto)
            S.dma("sync", dbg_v, vtok[:], waits=[projv], inc=sto)
            S.dma("sync", dbg_bias, biasall[:], waits=[projv, biasv], inc=sto)

        stomarks = {}
        for hp in range(4):
            hA, hB = 2 * hp, 2 * hp + 1
            for qb in range(NQB):
                evac_prev = S.now(dve)
                expmarks = {}
                pvmarks = {}
                for kc in range(NKC):
                    par = kc % 2
                    psA, psB = p_sA[par], p_sB[par]
                    wq0 = [expmarks[kc - 2]] if kc >= 2 else [projv, S.now(act)]
                    S.emit("tensor",
                           lambda e, hp=hp, qb=qb, kc=kc, psA=psA: e.matmul(
                               psA[:], lhsT=kh[0:64, hp, kc * 128:(kc + 1) * 128],
                               rhs=qh[0:64, hp, qb * 512:(qb + 1) * 512],
                               start=True, stop=True, tile_position=(0, 0)),
                           waits=wq0)
                    S.emit("tensor",
                           lambda e, hp=hp, qb=qb, kc=kc, psB=psB: e.matmul(
                               psB[:], lhsT=kh[64:128, hp, kc * 128:(kc + 1) * 128],
                               rhs=qh[64:128, hp, qb * 512:(qb + 1) * 512],
                               start=True, stop=True, tile_position=(64, 0)),
                           inc=pe)
                    qkm = S.now(pe)
                    we = [qkm, biasv]
                    we.append(pvmarks[kc - 2] if kc >= 2 else S.now(pe))
                    S.emit("scalar",
                           lambda e, par=par, psA=psA, kc=kc: e.activation(
                               pt[2 * par + 0][:], psA[:], AF.Exp,
                               bias=biasall[:, kc:kc + 1], scale=0.125),
                           waits=we, inc=act)
                    S.emit("scalar",
                           lambda e, par=par, psB=psB, kc=kc: e.activation(
                               pt[2 * par + 1][:], psB[:], AF.Exp,
                               bias=biasall[:, kc:kc + 1], scale=0.125),
                           inc=act)
                    expmarks[kc] = S.now(act)
                    wv0 = [expmarks[kc]] + ([evac_prev] if kc == 0 else [])
                    S.emit("tensor",
                           lambda e, hA=hA, kc=kc, par=par: e.matmul(
                               p_cA[:], lhsT=vtok[:, kc, hA, :], rhs=pt[2 * par + 0][:],
                               start=(kc == 0), stop=(kc == NKC - 1)),
                           waits=wv0)
                    S.emit("tensor",
                           lambda e, hB=hB, kc=kc, par=par: e.matmul(
                               p_cB[:], lhsT=vtok[:, kc, hB, :], rhs=pt[2 * par + 1][:],
                               start=(kc == 0), stop=(kc == NKC - 1)),
                           inc=pe)
                    pvmarks[kc] = S.now(pe)
                ctxm = S.now(pe)
                it = hp * NQB + qb
                par_c = it % 2
                wst = [stomarks[it - 2]] if it >= 2 else []
                for e_, pc in ((0, p_cA), (1, p_cB)):
                    S.emit("vector", lambda e, pc=pc: e.reciprocal(linv[:], pc[64:65, :]),
                           waits=[ctxm, S.now(pe)], inc=dve)
                    lv = S.now(dve)
                    S.emit("tensor",
                           lambda e: e.matmul(p_lb[0:64, 0:512], lhsT=ones1[0:1, 0:64],
                                              rhs=linv[:], start=True, stop=True),
                           waits=[lv, S.now(dve)], inc=pe)
                    lbv = S.now(pe)
                    S.emit("vector", lambda e, pc=pc: e.tensor_copy(ctmp[:], pc[0:64, :]),
                           waits=[lbv], inc=dve)
                    S.emit("vector",
                           lambda e, e_=e_, par_c=par_c: e.tensor_tensor(
                               ctile[par_c][e_ * 64:(e_ + 1) * 64, :],
                               ctmp[:], p_lb[0:64, 0:512], ALU.mult),
                           waits=wst + [S.now(dve)], inc=dve)
                cv = S.now(dve)
                S.dma("sync", ctxT[hp * 128:(hp + 1) * 128, qb * 512:(qb + 1) * 512],
                      ctile[par_c][:], waits=[cv], inc=sto)
                stomarks[it] = S.now(sto)
        S.emit("sync", lambda e: e.nop(), waits=[S.now(sto)])
        S.build()
    return nc


def run_l3(x2, Wq, bq, Wk, bk, Wv, bv, bimp):
    nc = _get("l3", build_l3)
    i16 = np.eye(16, dtype=np.float32)
    maps = []
    for c in range(8):
        b, hg = c // 2, c % 2
        rs = slice(hg * 512, (hg + 1) * 512)
        maps.append(dict(
            x2T=_f(x2[b].T),
            WqT=_f(np.asarray(Wq)[rs, :].T), WkT=_f(np.asarray(Wk)[rs, :].T),
            WvT=_f(np.asarray(Wv)[rs, :].T),
            bqT=_f(np.asarray(bq)[rs].reshape(4, 128).T),
            bkT=_f(np.asarray(bk)[rs].reshape(4, 128).T),
            bvrow=_f(np.asarray(bv)[rs].reshape(1, 512)),
            bimp=_f(np.asarray(bimp).reshape(1, 2048)),
            id16=i16,
        ))
    res = run_bass_kernel_spmd(nc, maps, core_ids=list(range(8)))
    ctx_ = np.empty((B, L, D), np.float32)
    for c in range(8):
        b, hg = c // 2, c % 2
        ctx_[b, :, hg * 512:(hg + 1) * 512] = np.asarray(res.results[c]["ctxT"]).T
    return ctx_



def build_l4(tw=512):
    nt = TPC // tw
    nc = bass.Bass("TRN2", target_bir_lowering=False)
    x2T = nc.dram_tensor("x2T", [D, TPC], F32, kind="ExternalInput").ap()
    ctxT = nc.dram_tensor("ctxT", [D, TPC], F32, kind="ExternalInput").ap()
    WoT = nc.dram_tensor("WoT", [D, D], F32, kind="ExternalInput").ap()
    boT = nc.dram_tensor("boT", [128, DC], F32, kind="ExternalInput").ap()
    gT = nc.dram_tensor("gT", [128, DC], F32, kind="ExternalInput").ap()
    bT = nc.dram_tensor("bT", [128, DC], F32, kind="ExternalInput").ap()
    gwT = nc.dram_tensor("gwT", [D, E], F32, kind="ExternalInput").ap()
    x3T = nc.dram_tensor("x3T", [D, TPC], F32, kind="ExternalOutput").ap()
    logits = nc.dram_tensor("logits", [TPC, E], F32, kind="ExternalOutput").ap()

    with ExitStack() as ctx:
        En = ctx.enter_context
        S = Sched(nc, ctx)
        wo = En(nc.sbuf_tensor("wo", [128, DC, D], BF16))
        gw = En(nc.sbuf_tensor("gw", [128, DC, E], F32))
        ch = En(nc.sbuf_tensor("ch", [128, DC, tw], BF16))
        x2 = En(nc.sbuf_tensor("x2", [128, DC, tw], F32))
        yy = En(nc.sbuf_tensor("yy", [128, DC, tw], F32))
        zz = En(nc.sbuf_tensor("zz", [128, DC, tw], F32))
        x3 = En(nc.sbuf_tensor("x3", [128, DC, tw], F32))
        lgt = En(nc.sbuf_tensor("lgt", [128, tw // 128, E], F32))
        sq = [En(nc.sbuf_tensor(f"sq{i}", [128, tw], F32)) for i in range(2)]
        st4 = En(nc.sbuf_tensor("st4", [1, 4 * tw], F32))
        tmp = En(nc.sbuf_tensor("tmp", [128, tw], F32))
        epsc = En(nc.sbuf_tensor("epsc", [1, 1], F32))
        ones128 = En(nc.sbuf_tensor("ones128", [128, 1], F32))
        ones1 = En(nc.sbuf_tensor("ones1", [1, 128], F32))
        bo_s = En(nc.sbuf_tensor("bo_s", [128, DC], F32))
        g_s = En(nc.sbuf_tensor("g_s", [128, DC], F32))
        b_s = En(nc.sbuf_tensor("b_s", [128, DC], F32))
        p_ping = En(nc.psum_tensor("p_ping", [128, 512], F32))
        p_pong = En(nc.psum_tensor("p_pong", [128, 512], F32))
        p_s = En(nc.psum_tensor("p_s", [1, 512], F32))
        p_q = En(nc.psum_tensor("p_q", [1, 512], F32))
        p_o = [En(nc.psum_tensor(f"p_o{i}", [128, 512], F32)) for i in range(2)]
        p_g = [En(nc.psum_tensor(f"p_g{i}", [128, E], F32)) for i in range(2)]

        ld = S.sem("ld")
        gld = S.sem("gld")
        pe = S.sem("pe")
        act = S.sem("act")
        dve = S.sem("dve")
        sto = S.sem("sto")

        cons = dict(ones128=ones128, ones1=ones1, sq=sq, st4=st4, tmp=tmp,
                    epsc=epsc[0:1, 0:1])
        psd = dict(s=p_s[0:1, 0:tw], q=p_q[0:1, 0:tw],
                   a=p_ping[:, 0:tw], b=p_pong[:, 0:tw])

        S.emit("vector", lambda e: e.memset(ones128[:], 1.0), inc=dve)
        S.emit("vector", lambda e: e.memset(ones1[:], 1.0), inc=dve)
        S.emit("vector", lambda e: e.memset(epsc[:], EPS), inc=dve)
        memv = S.now(dve)
        S.dma("gpsimd", wo[:], WoT.rearrange("(c p) o -> p c o", p=128), inc=gld)
        S.dma("sync", gw[:], gwT.rearrange("(c p) o -> p c o", p=128), inc=ld)
        S.dma("sync", bo_s[:], boT, inc=ld)
        S.dma("sync", g_s[:], gT, inc=ld)
        S.dma("sync", b_s[:], bT, inc=ld)
        wov = S.now(gld)
        ldv = S.now(ld)

        for it in range(nt):
            tsl = slice(it * tw, (it + 1) * tw)
            lw = [S.now(dve), S.now(sto), S.now(pe), S.now(act)]
            S.dma("sync", x2[:], x2T[:, tsl].rearrange("(c p) t -> p c t", p=128),
                  waits=lw, inc=ld)
            S.dma("gpsimd", ch[:], ctxT[:, tsl].rearrange("(c p) t -> p c t", p=128),
                  waits=lw, inc=gld)
            lv, gv = S.now(ld), S.now(gld)
            for oc in range(DC):
                po = p_o[oc % 2]
                wz = [wov, gv, lv, ldv, memv] if oc < 2 else [S.now(dve)]
                for dci in range(DC):
                    S.emit("tensor",
                           lambda e, dci=dci, oc=oc, po=po: e.matmul(
                               po[:, 0:tw], lhsT=wo[:, dci, oc * 128:(oc + 1) * 128],
                               rhs=ch[:, dci, :], start=(dci == 0), stop=(dci == DC - 1)),
                           waits=wz if dci == 0 else (),
                           inc=pe if dci == DC - 1 else None)
                pv = S.now(pe)
                S.emit("vector",
                       lambda e, oc=oc, po=po: e.scalar_tensor_tensor(
                           yy[:, oc, :], po[:, 0:tw], bo_s[:, oc:oc + 1], x2[:, oc, :],
                           ALU.add, ALU.add),
                       waits=[pv], inc=dve)
            yv = S.now(dve)
            zv = emit_ln(S, lambda c: yy[:, c, :], lambda c: tmp[:, 0:tw], DC, tw,
                         cons, psd, pe, act, dve, yv,
                         gamma=lambda c: g_s[:, c:c + 1], beta=lambda c: b_s[:, c:c + 1],
                         dst_cast=lambda c: zz[:, c, :])
            for c in range(DC):
                S.emit("vector",
                       lambda e, c=c: e.tensor_tensor(yy[:, c, :], x2[:, c, :], zz[:, c, :],
                                                      ALU.add),
                       waits=[zv] if c == 0 else (), inc=dve)
            y3v = S.now(dve)
            emit_ln(S, lambda c: yy[:, c, :], lambda c: x3[:, c, :], DC, tw,
                    cons, psd, pe, act, dve, y3v)
            x3v = S.now(dve)
            S.dma("sync", x3T[:, tsl].rearrange("(c p) t -> p c t", p=128), x3[:],
                  waits=[x3v], inc=sto)
            for s_ in range(tw // 128):
                pgb = p_g[s_ % 2]
                for dci in range(DC):
                    w0 = []
                    if dci == 0:
                        w0 = [x3v] + ([S.now(dve)] if s_ >= 2 else [])
                    S.emit("tensor",
                           lambda e, s_=s_, dci=dci, pgb=pgb: e.matmul(
                               pgb[:], lhsT=x3[:, dci, s_ * 128:(s_ + 1) * 128],
                               rhs=gw[:, dci, :], start=(dci == 0), stop=(dci == DC - 1)),
                           waits=w0,
                           inc=pe if dci == DC - 1 else None)
                gvv = S.now(pe)
                S.emit("vector",
                       lambda e, s_=s_, pgb=pgb: e.tensor_copy(lgt[:, s_, :], pgb[:]),
                       waits=[gvv], inc=dve)
            lgv = S.now(dve)
            S.dma("sync", logits[tsl, :].rearrange("(s p) e -> p s e", p=128), lgt[:],
                  waits=[lgv], inc=sto)
        S.emit("sync", lambda e: e.nop(), waits=[S.now(sto)])
        S.build()
    return nc


def run_l4(x2, ctx_, Wo, bo, g, b, gate_w):
    nc = _get("l4", build_l4)
    maps = []
    WoTh = _f(np.asarray(Wo).T)
    boT = _f(np.asarray(bo).reshape(DC, 128).T)
    gT = _f(np.asarray(g).reshape(DC, 128).T)
    bT = _f(np.asarray(b).reshape(DC, 128).T)
    gwTh = _f(np.asarray(gate_w).T)
    for c in range(8):
        sl = slice(c * TPC, (c + 1) * TPC)
        maps.append(dict(x2T=_f(x2[sl].T), ctxT=_f(ctx_[sl].T), WoT=WoTh, boT=boT,
                         gT=gT, bT=bT, gwT=gwTh))
    res = run_bass_kernel_spmd(nc, maps, core_ids=list(range(8)))
    x3 = np.concatenate([np.asarray(r["x3T"]).T for r in res.results], axis=0)
    lg = np.concatenate([np.asarray(r["logits"]) for r in res.results], axis=0)
    return x3, lg



def build_l5(debug=False):
    NT = TPC // 128
    EPC = E // 8
    NCT = CAP // 256
    nc = bass.Bass("TRN2", target_bir_lowering=False)
    x3tok = nc.dram_tensor("x3tok", [TPC, D], F32, kind="ExternalInput").ap()
    lgin = nc.dram_tensor("lgin", [TPC, E], F32, kind="ExternalInput").ap()
    w1d = nc.dram_tensor("w1d", [EPC, D, F], F32, kind="ExternalInput").ap()
    w2d = nc.dram_tensor("w2d", [EPC, F, D], F32, kind="ExternalInput").ap()
    UTd = nc.dram_tensor("UTd", [128, 128], F32, kind="ExternalInput").ap()
    pmaskd = nc.dram_tensor("pmaskd", [128, E], F32, kind="ExternalInput").ap()
    iota16d = nc.dram_tensor("iota16d", [128, E], F32, kind="ExternalInput").ap()
    iotaBIGd = nc.dram_tensor("iotaBIGd", [128, E], F32, kind="ExternalInput").ap()
    bigpd = nc.dram_tensor("bigpd", [128, 1], F32, kind="ExternalInput").ap()
    idd = nc.dram_tensor("idd", [128, 128], F32, kind="ExternalInput").ap()
    x4tok = nc.dram_tensor("x4tok", [TPC, D], F32, kind="ExternalOutput").ap()

    dispD = nc.dram_tensor("dispD", [E * CAP, D], BF16)
    dispRS = nc.dram_tensor("dispRS", [EPC * CAP, D], BF16)
    eoutL = nc.dram_tensor("eoutL", [EPC * CAP, D], F32)
    eoutAG = nc.dram_tensor("eoutAG", [E * CAP, D], F32, addr_space="Shared")
    agin = nc.dram_tensor("agin", [1, 16 * E], F32)
    agout = nc.dram_tensor("agout", [8, 16 * E], F32, addr_space="Shared")

    with ExitStack() as ctx:
        En = ctx.enter_context
        S = Sched(nc, ctx)
        lg = En(nc.sbuf_tensor("lg", [128, NT, E], F32))
        ohb = En(nc.sbuf_tensor("ohb", [128, NT, E], F32))
        psel = En(nc.sbuf_tensor("psel", [128, NT], F32))
        sel = En(nc.sbuf_tensor("sel", [128, NT], F32))
        lloc = En(nc.sbuf_tensor("lloc", [128, NT], F32))
        kf = En(nc.sbuf_tensor("kf", [128, NT], F32))
        kidx = En(nc.sbuf_tensor("kidx", [128, NT], U32))
        cells = En(nc.sbuf_tensor("cells", [128, 16], F32))
        t16 = En(nc.sbuf_tensor("t16", [128, E], F32))
        et = En(nc.sbuf_tensor("et", [128, E], F32))
        offloc = En(nc.sbuf_tensor("offloc", [1, E], F32))
        prefrow = En(nc.sbuf_tensor("prefrow", [1, E], F32))
        crow = En(nc.sbuf_tensor("crow", [1, 16 * E], F32))
        cg = En(nc.sbuf_tensor("cg", [128, E], F32))
        UTs = En(nc.sbuf_tensor("UTs", [128, 128], F32))
        id128f = En(nc.sbuf_tensor("id128f", [128, 128], F32))
        id128b = En(nc.sbuf_tensor("id128b", [128, 128], BF16))
        pmask = En(nc.sbuf_tensor("pmask", [128, E], F32))
        iota16 = En(nc.sbuf_tensor("iota16", [128, E], F32))
        iotaBIG = En(nc.sbuf_tensor("iotaBIG", [128, E], F32))
        bigp = En(nc.sbuf_tensor("bigp", [128, 1], F32))
        ones128 = En(nc.sbuf_tensor("ones128", [128, 1], F32))
        ones1 = En(nc.sbuf_tensor("ones1", [1, 128], F32))
        nbigf = En(nc.sbuf_tensor("nbigf", [128, 1], F32))
        capg = En(nc.sbuf_tensor("capg", [128, 1], F32))
        eps128 = En(nc.sbuf_tensor("eps128", [128, 1], F32))
        zero2k = En(nc.sbuf_tensor("zero2k", [128, D], BF16))
        xsc = [En(nc.sbuf_tensor(f"xsc{i}", [128, D], BF16)) for i in range(2)]
        w1b = En(nc.sbuf_tensor("w1b", [128, DC, F], BF16))
        w2b = En(nc.sbuf_tensor("w2b", [128, F // 128, D], BF16))
        dtok = En(nc.sbuf_tensor("dtok", [128, 2, D], BF16))
        dft = En(nc.sbuf_tensor("dft", [128, DC, 256], BF16))
        hh = En(nc.sbuf_tensor("hh", [128, F // 128, 256], BF16))
        zt = En(nc.sbuf_tensor("zt", [128, DC, 256], F32))
        otok = En(nc.sbuf_tensor("otok", [128, 2, D], F32))
        comb = En(nc.sbuf_tensor("comb", [128, D], F32))
        x3f = [En(nc.sbuf_tensor(f"x3f{i}", [128, D], F32)) for i in range(2)]
        sqt = En(nc.sbuf_tensor("sqt", [128, D], F32))
        p_ra = En(nc.psum_tensor("p_ra", [128, 128], F32))
        p_tb = [En(nc.psum_tensor(f"p_tb{i}", [128, 128], BF16)) for i in range(2)]
        p_tf = [En(nc.psum_tensor(f"p_tf{i}", [128, 128], F32)) for i in range(2)]
        p_f = [En(nc.psum_tensor(f"p_f{i}", [128, 512], F32)) for i in range(2)]

        ld = S.sem("ld")
        gld = S.sem("gld")
        pe = S.sem("pe")
        act = S.sem("act")
        dve = S.sem("dve")
        sto = S.sem("sto")
        cc = S.sem("cc")

        def vstep(fn, extra=()):
            S.emit("vector", fn, waits=[S.now(dve)] + list(extra), inc=dve)

        S.emit("vector", lambda e: e.memset(ones128[:], 1.0), inc=dve)
        S.emit("vector", lambda e: e.memset(ones1[:], 1.0), inc=dve)
        S.emit("vector", lambda e: e.memset(nbigf[:], -1e4), inc=dve)
        S.emit("vector", lambda e: e.memset(capg[:], float(CAP - 1)), inc=dve)
        S.emit("vector", lambda e: e.memset(eps128[:], EPS), inc=dve)
        S.emit("vector", lambda e: e.memset(offloc[:], -1.0), inc=dve)
        S.emit("vector", lambda e: e.memset(zero2k[:], 0.0), inc=dve)
        memv = S.now(dve)
        S.dma("sync", UTs[:], UTd, inc=ld)
        S.dma("sync", id128f[:], idd, inc=ld)
        S.dma("gpsimd", id128b[:], idd, inc=gld)
        S.dma("sync", pmask[:], pmaskd, inc=ld)
        S.dma("sync", iota16[:], iota16d, inc=ld)
        S.dma("sync", iotaBIG[:], iotaBIGd, inc=ld)
        S.dma("sync", bigp[:], bigpd, inc=ld)
        S.dma("sync", lg[:], lgin.rearrange("(c p) e -> p c e", p=128), inc=ld)
        ldv = S.now(ld)
        for r in range(E * CAP // 128):
            S.dma("scalar", dispD[r * 128:(r + 1) * 128, :], zero2k[:],
                  waits=[memv] if r == 0 else (), inc=sto)
        zfv = S.now(sto)

        mx = cells[:, 0:1]
        nmx = cells[:, 1:2]
        ssum = cells[:, 2:3]
        pft = cells[:, 3:4]
        gloc = cells[:, 4:5]
        keep = cells[:, 7:8]
        aa = cells[:, 8:9]
        bb_ = cells[:, 9:10]
        cc_ = cells[:, 10:11]
        rsn = cells[:, 11:12]
        nmu = cells[:, 12:13]
        ssq = cells[:, 13:14]
        ex2 = cells[:, 14:15]
        var = cells[:, 15:16]

        p_cum = p_ra[:, 0:16]
        p_cnt = p_ra[0:1, 16:32]
        p_pr = p_ra[0:1, 32:48]
        p_prb = p_ra[:, 48:64]
        for lt in range(NT):
            lgt = lg[:, lt, :]
            vstep(lambda e, lgt=lgt: e.reduce_max(mx, lgt, axis=AX.X), extra=[ldv])
            vstep(lambda e: e.tensor_scalar_mul(nmx, mx, -1.0))
            S.emit("scalar", lambda e, lgt=lgt: e.activation(et[:], lgt, AF.Exp,
                                                             bias=nmx, scale=1.0),
                   waits=[S.now(dve)], inc=act)
            vstep(lambda e: e.reduce_sum(ssum, et[:], axis=AX.X), extra=[S.now(act)])
            vstep(lambda e, lt=lt: e.reciprocal(psel[:, lt:lt + 1], ssum))
            vstep(lambda e, lgt=lgt: e.tensor_single_scalar(t16[:], lgt, mx, ALU.is_ge))
            vstep(lambda e: e.scalar_tensor_tensor(t16[:], t16[:], nbigf[:, 0:1],
                                                   iotaBIG[:], ALU.mult, ALU.add))
            vstep(lambda e, lt=lt: e.tensor_reduce(sel[:, lt:lt + 1], t16[:], AX.X, ALU.min))
            vstep(lambda e, lt=lt: e.tensor_single_scalar(ohb[:, lt, :], iota16[:],
                                                          sel[:, lt:lt + 1], ALU.is_equal))
            ohv = S.now(dve)
            S.emit("tensor", lambda e, lt=lt: e.matmul(p_cum, lhsT=UTs[:], rhs=ohb[:, lt, :],
                                                       start=True, stop=False),
                   waits=[ohv, ldv])
            S.emit("tensor", lambda e: e.matmul(p_cum, lhsT=ones1[:], rhs=offloc[:],
                                                start=False, stop=True), inc=pe)
            cumv = S.now(pe)
            vstep(lambda e, lt=lt: e.tensor_tensor(t16[:], p_cum, ohb[:, lt, :], ALU.mult),
                  extra=[cumv])
            vstep(lambda e, lt=lt: e.reduce_sum(lloc[:, lt:lt + 1], t16[:], axis=AX.X))
            rdv = S.now(dve)
            S.emit("tensor", lambda e, lt=lt: e.matmul(p_cnt, lhsT=ones128[:],
                                                       rhs=ohb[:, lt, :],
                                                       start=True, stop=True),
                   waits=[rdv], inc=pe)
            cntv = S.now(pe)
            vstep(lambda e: e.tensor_tensor(offloc[:], offloc[:], p_cnt, ALU.add),
                  extra=[cntv])
            vstep(lambda e, lt=lt: e.tensor_copy(crow[0:1, lt * E:(lt + 1) * E], p_cnt))
        p1v = S.now(dve)
        S.dma("sync", agin.ap(), crow[:], waits=[p1v], inc=ld)
        S.emit("gpsimd", lambda e: e.collective_compute(
            "AllGather", ALU.bypass, replica_groups=[list(range(8))],
            ins=[agin.ap()], outs=[agout.ap()]),
            waits=[S.now(ld)], inc=cc)
        S.emit("sync", lambda e: e.nop(), waits=[S.now(cc)])
        S.dma("sync", cg[:], agout.ap().rearrange("r (lt e) -> (r lt) e", e=E), inc=ld)
        cgv = S.now(ld)

        S.emit("tensor", lambda e: e.matmul(p_pr, lhsT=pmask[:, 0:1],
                                            rhs=cg[:], start=True, stop=True),
               waits=[cgv, S.now(dve)], inc=pe)
        vstep(lambda e: e.tensor_copy(prefrow[:], p_pr), extra=[S.now(pe)])
        S.emit("tensor", lambda e: e.matmul(p_prb, lhsT=ones1[:], rhs=prefrow[:],
                                            start=True, stop=True),
               waits=[S.now(dve)], inc=pe)
        prbv = S.now(pe)
        for lt in range(NT):
            vstep(lambda e, lt=lt: e.tensor_tensor(t16[:], p_prb, ohb[:, lt, :], ALU.mult),
                  extra=[prbv])
            vstep(lambda e: e.reduce_sum(pft, t16[:], axis=AX.X))
            vstep(lambda e, lt=lt: e.tensor_tensor(gloc, lloc[:, lt:lt + 1], pft, ALU.add))
            vstep(lambda e: e.tensor_tensor(keep, gloc, capg[:], ALU.is_le))
            vstep(lambda e, lt=lt: e.scalar_tensor_tensor(aa, sel[:, lt:lt + 1],
                                                          float(CAP), gloc,
                                                          ALU.mult, ALU.add))
            vstep(lambda e: e.tensor_tensor(bb_, aa, bigp[:], ALU.subtract))
            vstep(lambda e: e.tensor_tensor(cc_, bb_, keep, ALU.mult))
            vstep(lambda e, lt=lt: e.tensor_tensor(kf[:, lt:lt + 1], cc_, bigp[:], ALU.add))
        vstep(lambda e: e.tensor_copy(kidx[:], kf[:]))
        kiv = S.now(dve)
        if debug:
            dbg_sel = nc.dram_tensor("dbg_sel", [128, 16], F32, kind="ExternalOutput").ap()
            dbg_kf = nc.dram_tensor("dbg_kf", [128, 16], F32, kind="ExternalOutput").ap()
            S.dma("sync", dbg_sel, sel[:], waits=[kiv], inc=sto)
            S.dma("sync", dbg_kf, kf[:], waits=[kiv], inc=sto)

        for lt in range(NT):
            xb = xsc[lt % 2]
            S.emit("gpsimd",
                   lambda e, lt=lt, xb=xb: e.dma_start(
                       out=xb[:], in_=x3tok[lt * 128:(lt + 1) * 128, :]),
                   waits=[S.now(gld)] if lt >= 2 else [], inc=gld, inc_k=16)
            xv = S.now(gld)
            S.emit("gpsimd",
                   lambda e, lt=lt, xb=xb: e.indirect_dma_start(
                       out=dispD[0:128, :],
                       out_offset=bass.IndirectOffsetOnAxis(kidx[:, lt:lt + 1], 0),
                       in_=xb[:], in_offset=None,
                       bounds_check=E * CAP - 1, oob_is_err=False),
                   waits=[kiv, xv] + ([zfv] if lt == 0 else []), inc=gld, inc_k=16)
        scatv = S.now(gld)
        S.emit("sync", lambda e: e.nop(), waits=[scatv, zfv])
        S.emit("gpsimd", lambda e: e.collective_compute(
            "ReduceScatter", ALU.add, replica_groups=[list(range(8))],
            ins=[dispD.ap()], outs=[dispRS.ap()]),
            waits=[scatv, zfv], inc=cc)
        rsv = S.now(cc)

        for exl in range(EPC):
            S.dma("gpsimd", w1b[:], w1d[exl].rearrange("(c p) f -> p c f", p=128),
                  waits=[S.now(pe)], inc=gld)
            S.dma("gpsimd", w2b[:], w2d[exl].rearrange("(g p) dd -> p g dd", p=128),
                  waits=[S.now(pe)], inc=gld)
            wv = S.now(gld)
            for ct in range(NCT):
                base = exl * CAP + ct * 256
                S.dma("sync", dtok[:],
                      dispRS[base:base + 256, :].rearrange("(s p) dd -> p s dd", p=128),
                      waits=[(self_w := S.now(dve))] + [rsv], inc=ld)
                dtv = S.now(ld)
                tmarks = []
                for i in range(16):
                    sc, dcx = i // 8, i % 8
                    ptb = p_tb[i % 2]
                    wt = [dtv] + ([tmarks[i - 2]] if i >= 2 else [S.now(dve)])
                    S.emit("tensor",
                           lambda e, sc=sc, dcx=dcx, ptb=ptb: e.transpose(
                               ptb[:], dtok[:, sc, dcx * 128:(dcx + 1) * 128], id128b[:]),
                           waits=wt, inc=pe)
                    pv = S.now(pe)
                    S.emit("vector",
                           lambda e, sc=sc, dcx=dcx, ptb=ptb: e.tensor_copy(
                               dft[:, dcx, sc * 128:(sc + 1) * 128], ptb[:]),
                           waits=[pv], inc=dve)
                    tmarks.append(S.now(dve))
                dftv = S.now(dve)
                for fg in range(F // 128):
                    pf = p_f[fg % 2]
                    wz = [wv, dftv] if fg == 0 else ([S.now(act)] if fg >= 2 else [])
                    for dcx in range(DC):
                        S.emit("tensor",
                               lambda e, dcx=dcx, fg=fg, pf=pf: e.matmul(
                                   pf[:, 0:256], lhsT=w1b[:, dcx, fg * 128:(fg + 1) * 128],
                                   rhs=dft[:, dcx, :], start=(dcx == 0), stop=(dcx == DC - 1)),
                               waits=wz if dcx == 0 else (),
                               inc=pe if dcx == DC - 1 else None)
                    pv = S.now(pe)
                    S.emit("scalar",
                           lambda e, fg=fg, pf=pf: e.activation(
                               hh[:, fg, :], pf[:, 0:256], AF.Relu),
                           waits=[pv], inc=act)
                hv = S.now(act)
                for dcx in range(DC):
                    pf = p_f[dcx % 2]
                    for fg in range(F // 128):
                        wz = []
                        if fg == 0:
                            wz = [hv] + ([S.now(dve)] if dcx >= 2 else [])
                        S.emit("tensor",
                               lambda e, dcx=dcx, fg=fg, pf=pf: e.matmul(
                                   pf[:, 0:256], lhsT=w2b[:, fg, dcx * 128:(dcx + 1) * 128],
                                   rhs=hh[:, fg, :], start=(fg == 0), stop=(fg == F // 128 - 1)),
                               waits=wz,
                               inc=pe if fg == F // 128 - 1 else None)
                    pv = S.now(pe)
                    S.emit("vector",
                           lambda e, dcx=dcx, pf=pf: e.tensor_copy(zt[:, dcx, :], pf[:, 0:256]),
                           waits=[pv], inc=dve)
                ztv = S.now(dve)
                tmarks2 = []
                for i in range(16):
                    sc, dcx = i // 8, i % 8
                    ptf = p_tf[i % 2]
                    wt = [ztv] + ([tmarks2[i - 2]] if i >= 2 else [])
                    S.emit("tensor",
                           lambda e, sc=sc, dcx=dcx, ptf=ptf: e.transpose(
                               ptf[:], zt[:, dcx, sc * 128:(sc + 1) * 128], id128f[:]),
                           waits=wt, inc=pe)
                    pv = S.now(pe)
                    S.emit("vector",
                           lambda e, sc=sc, dcx=dcx, ptf=ptf: e.tensor_copy(
                               otok[:, sc, dcx * 128:(dcx + 1) * 128], ptf[:]),
                           waits=[pv], inc=dve)
                    tmarks2.append(S.now(dve))
                otv = S.now(dve)
                S.dma("sync", eoutL[base:base + 256, :].rearrange("(s p) dd -> p s dd", p=128),
                      otok[:], waits=[otv], inc=sto)
        S.emit("sync", lambda e: e.nop(), waits=[S.now(sto)])
        S.emit("gpsimd", lambda e: e.collective_compute(
            "AllGather", ALU.bypass, replica_groups=[list(range(8))],
            ins=[eoutL.ap()], outs=[eoutAG.ap()]),
            waits=[S.now(sto)], inc=cc)
        agv = S.now(cc)

        stomarks2 = []
        for lt in range(NT):
            par = lt % 2
            S.dma("sync", x3f[par][:], x3tok[lt * 128:(lt + 1) * 128, :],
                  waits=[S.now(dve)] if lt >= 2 else [], inc=ld)
            xv = S.now(ld)
            vstep(lambda e: e.memset(comb[:], 0.0),
                  extra=[stomarks2[lt - 1]] if lt >= 1 else [])
            S.emit("gpsimd",
                   lambda e, lt=lt: e.indirect_dma_start(
                       out=comb[:], out_offset=None,
                       in_=eoutAG[0:128, :],
                       in_offset=bass.IndirectOffsetOnAxis(kidx[:, lt:lt + 1], 0),
                       bounds_check=E * CAP - 1, oob_is_err=False),
                   waits=[agv, S.now(dve)], inc=gld, inc_k=16)
            gv = S.now(gld)
            vstep(lambda e, lt=lt: e.tensor_scalar_mul(comb[:], comb[:], psel[:, lt:lt + 1]),
                  extra=[gv])
            vstep(lambda e, par=par: e.tensor_tensor(comb[:], comb[:], x3f[par][:], ALU.add),
                  extra=[xv])
            yv = S.now(dve)
            vstep(lambda e: e.reduce_sum(rsn, comb[:], axis=AX.X, negate=True))
            vstep(lambda e: e.tensor_scalar_mul(nmu, rsn, 1.0 / D))
            S.emit("scalar", lambda e: e.activation(sqt[:], comb[:], AF.Square,
                                                    accum_out=ssq),
                   waits=[yv], inc=act)
            vstep(lambda e: e.tensor_scalar_mul(ex2, ssq, 1.0 / D), extra=[S.now(act)])
            vstep(lambda e: e.tensor_tensor(var, nmu, nmu, ALU.mult))
            vstep(lambda e: e.tensor_tensor(var, ex2, var, ALU.subtract))
            S.emit("scalar", lambda e: e.activation(var, var, AF.Ln, bias=eps128[:], scale=1.0),
                   waits=[S.now(dve)], inc=act)
            S.emit("scalar", lambda e: e.activation(var, var, AF.Exp, bias=0.0, scale=-0.5),
                   waits=[S.now(act)], inc=act)
            rv = S.now(act)
            vstep(lambda e: e.tensor_scalar(comb[:], comb[:], nmu, var, ALU.add, ALU.mult),
                  extra=[rv])
            ov = S.now(dve)
            S.dma("sync", x4tok[lt * 128:(lt + 1) * 128, :], comb[:], waits=[ov], inc=sto)
            stomarks2.append(S.now(sto))
        S.emit("sync", lambda e: e.nop(), waits=[S.now(sto)])
        S.build()
    return nc


def _l5_consts(core):
    UT = np.triu(np.ones((128, 128), np.float32))
    pm = np.zeros((128, E), np.float32)
    pm[:core * 16, :] = 1.0
    io = np.broadcast_to(np.arange(E, dtype=np.float32), (128, E)).copy()
    iob = io + 1e4
    bigp = (1e6 + np.arange(128, dtype=np.float32)).reshape(128, 1)
    return UT, pm, io, iob, bigp


def run_l5(x3, logits, w1, w2):
    nc = _get("l5", build_l5)
    w1 = np.asarray(w1)
    w2 = np.asarray(w2)
    maps = []
    for c in range(8):
        sl = slice(c * TPC, (c + 1) * TPC)
        UT, pm, io, iob, bigp = _l5_consts(c)
        maps.append(dict(
            x3tok=_f(x3[sl]), lgin=_f(logits[sl]),
            w1d=_f(w1[2 * c:2 * c + 2]), w2d=_f(w2[2 * c:2 * c + 2]),
            UTd=UT, pmaskd=pm, iota16d=io, iotaBIGd=iob, bigpd=bigp,
            idd=np.eye(128, dtype=np.float32),
        ))
    res = run_bass_kernel_spmd(nc, maps, core_ids=list(range(8)))
    return np.concatenate([np.asarray(r["x4tok"]) for r in res.results], axis=0)

























def build_fused(debug=False):
    nc = bass.Bass("TRN2", target_bir_lowering=False)
    AIn = lambda n, s, d=F32: nc.dram_tensor(n, s, d, kind="ExternalInput").ap()
    xT = AIn("xT", [D, TPC])
    xtok = AIn("xtok", [TPC, D])
    trimask = AIn("trimask", [128, 4 * 512])
    g0T = AIn("g0T", [128, DC]); b0T = AIn("b0T", [128, DC])
    fc1q = AIn("fc1q", [D, F], I32); sc1 = AIn("sc1", [DC, F]); b1T = AIn("b1T", [128, F // 128])
    fc2q = AIn("fc2q", [F, D], I32); sc2 = AIn("sc2", [F // 128, D]); b2T = AIn("b2T", [128, DC])
    WqT = AIn("WqT", [D, 512]); WkT = AIn("WkT", [D, 512]); WvT = AIn("WvT", [D, 512])
    bqT = AIn("bqT", [128, 4]); bkT = AIn("bkT", [128, 4]); bvrow = AIn("bvrow", [1, 512])
    bimp = AIn("bimp", [1, 2048]); id16 = AIn("id16", [16, 16])
    WoT = AIn("WoT", [D, D]); boT = AIn("boT", [128, DC])
    gT = AIn("gT", [128, DC]); bT = AIn("bT", [128, DC]); gwT = AIn("gwT", [D, E])
    w1d = AIn("w1d", [2, D, F]); w2d = AIn("w2d", [2, F, D])
    UTd = AIn("UTd", [128, 128]); pmaskd = AIn("pmaskd", [128, E])
    iota16d = AIn("iota16d", [128, E]); iotaBIGd = AIn("iotaBIGd", [128, E])
    bigpd = AIn("bigpd", [128, 1]); idd = AIn("idd", [128, 128])
    x4tok = nc.dram_tensor("x4tok", [TPC, D], F32, kind="ExternalOutput").ap()
    if debug:
        dbg_ctx1 = nc.dram_tensor("dbg_ctx1", [1024, 2048], F32, kind="ExternalOutput").ap()
        dbg_x2 = nc.dram_tensor("dbg_x2", [D, TPC], F32, kind="ExternalOutput").ap()
        dbg_x3 = nc.dram_tensor("dbg_x3", [TPC, D], F32, kind="ExternalOutput").ap()
        dbg_lg = nc.dram_tensor("dbg_lg", [128, 16, E], F32, kind="ExternalOutput").ap()
        dbg_kf = nc.dram_tensor("dbg_kf", [128, 16], F32, kind="ExternalOutput").ap()

    ag1in = nc.dram_tensor("ag1in", [2048, 2048], BF16)
    ag1out = nc.dram_tensor("ag1out", [8 * 2048, 2048], BF16, addr_space="Shared")
    ag2in = nc.dram_tensor("ag2in", [1024, 2048], F32)
    ag2out = nc.dram_tensor("ag2out", [8 * 1024, 2048], F32, addr_space="Shared")
    ag3in = nc.dram_tensor("ag3in", [1024, 2048], BF16)
    ag3out = nc.dram_tensor("ag3out", [8 * 1024, 2048], BF16, addr_space="Shared")
    ag4in = nc.dram_tensor("ag4in", [1024, 2048], BF16)
    ag4out = nc.dram_tensor("ag4out", [8 * 1024, 2048], BF16, addr_space="Shared")
    my1f = nc.dram_tensor("my1f", [2, 512, 2048], BF16)
    my1t = nc.dram_tensor("my1t", [2, 2048, 512], BF16)
    my2 = nc.dram_tensor("my2", [2, 512, 2048], F32)
    my3 = nc.dram_tensor("my3", [2, 1024, 2048], BF16)
    my4 = nc.dram_tensor("my4", [2, 512, 2048], BF16)
    x2loc = nc.dram_tensor("x2loc", [D, TPC], F32)
    mrow = nc.dram_tensor("mrow", [2048], F32)
    x3tokD = nc.dram_tensor("x3tokD", [TPC, D], F32)
    dispD = nc.dram_tensor("dispD", [E * CAP, D], BF16)
    dispRS1 = nc.dram_tensor("dispRS1", [2 * 512, D], BF16)
    dispRS2 = nc.dram_tensor("dispRS2", [2 * 768, D], BF16)
    eoutLa = nc.dram_tensor("eoutLa", [CAP, D], BF16)
    eoutLb = nc.dram_tensor("eoutLb", [CAP, D], BF16)
    eoutAG = nc.dram_tensor("eoutAG", [E * CAP, D], BF16, addr_space="Shared")
    ag5in = nc.dram_tensor("ag5in", [1, 16 * E], F32)
    ag5out = nc.dram_tensor("ag5out", [8, 16 * E], F32, addr_space="Shared")

    with ExitStack() as ctx:
        En = ctx.enter_context
        S = Sched(nc, ctx)
        ones128 = En(nc.sbuf_tensor("ones128", [128, 1], F32))
        ones1 = En(nc.sbuf_tensor("ones1", [1, 128], F32))
        mshift = En(nc.sbuf_tensor("mshift", [128, 1], F32))
        negbig = En(nc.sbuf_tensor("negbig", [1, 1], F32))
        nbigf = En(nc.sbuf_tensor("nbigf", [128, 1], F32))
        capg = En(nc.sbuf_tensor("capg", [128, 1], F32))
        eps128 = En(nc.sbuf_tensor("eps128", [128, 1], F32))
        epsc_t = En(nc.sbuf_tensor("epsc", [1, 1], F32))
        tri = En(nc.sbuf_tensor("tri", [128, 4 * 512], BF16))
        biasall = En(nc.sbuf_tensor("biasall", [128, 32], F32))
        cells = En(nc.sbuf_tensor("cells", [128, 16], F32))
        st4 = En(nc.sbuf_tensor("st4", [1, 4 * 256], F32))
        sqs = [En(nc.sbuf_tensor(f"sqs{i}", [128, 512], F32)) for i in range(2)]
        tmp = En(nc.sbuf_tensor("tmp", [128, 512], F32))
        fo = En(nc.sbuf_tensor("fo", [128, 256], F32))
        linvA = En(nc.sbuf_tensor("linvA", [1, 512], F32))
        linvB = En(nc.sbuf_tensor("linvB", [1, 512], F32))
        gg = En(nc.sbuf_tensor("gg", [128, DC], F32))
        bb0 = En(nc.sbuf_tensor("bb0", [128, DC], F32))
        bb1 = En(nc.sbuf_tensor("bb1", [128, F // 128], F32))
        hb2 = En(nc.sbuf_tensor("hb2", [128, DC], F32))
        bq_s = En(nc.sbuf_tensor("bq_s", [128, 4], F32))
        bk_s = En(nc.sbuf_tensor("bk_s", [128, 4], F32))
        bvr = En(nc.sbuf_tensor("bvr", [1, 512], F32))
        bv_b = En(nc.sbuf_tensor("bv_b", [128, 512], F32))
        bo_s = En(nc.sbuf_tensor("bo_s", [128, DC], F32))
        g_s = En(nc.sbuf_tensor("g_s", [128, DC], F32))
        b_s = En(nc.sbuf_tensor("b_s", [128, DC], F32))
        gw = En(nc.sbuf_tensor("gw", [128, DC, E], F32))
        UTs = En(nc.sbuf_tensor("UTs", [128, 128], F32))
        id128f = En(nc.sbuf_tensor("id128f", [128, 128], F32))
        id128b = En(nc.sbuf_tensor("id128b", [128, 128], BF16))
        pmask = En(nc.sbuf_tensor("pmask", [128, E], F32))
        iota16 = En(nc.sbuf_tensor("iota16", [128, E], F32))
        iotaBIG = En(nc.sbuf_tensor("iotaBIG", [128, E], F32))
        bigp = En(nc.sbuf_tensor("bigp", [128, 1], F32))
        idb = En(nc.sbuf_tensor("idb", [16, 16], F32))
        mask16 = En(nc.sbuf_tensor("mask16", [16, 128], F32))
        lg = En(nc.sbuf_tensor("lg", [128, 16, E], F32))
        ohb = En(nc.sbuf_tensor("ohb", [128, 16, E], F32))
        psel = En(nc.sbuf_tensor("psel", [128, 16], F32))
        sel = En(nc.sbuf_tensor("sel", [128, 16], F32))
        lloc = En(nc.sbuf_tensor("lloc", [128, 16], F32))
        kf = En(nc.sbuf_tensor("kf", [128, 16], F32))
        kidxS = [En(nc.sbuf_tensor(f"kidxS{i}", [128, 16], U32)) for i in range(2)]
        kidxC = [En(nc.sbuf_tensor(f"kidxC{i}", [128, 16], U32)) for i in range(2)]
        c512 = En(nc.sbuf_tensor("c512", [128, 1], F32))
        c511 = En(nc.sbuf_tensor("c511", [128, 1], F32))
        c8c = En(nc.sbuf_tensor("c8c", [128, 1], F32))
        c7c = En(nc.sbuf_tensor("c7c", [128, 1], F32))
        c8192 = En(nc.sbuf_tensor("c8192", [128, 1], F32))
        t16 = En(nc.sbuf_tensor("t16", [128, E], F32))
        et16 = En(nc.sbuf_tensor("et16", [128, E], F32))
        offloc = En(nc.sbuf_tensor("offloc", [1, E], F32))
        prefrow = En(nc.sbuf_tensor("prefrow", [1, E], F32))
        crow = En(nc.sbuf_tensor("crow", [1, 16 * E], F32))
        cg = En(nc.sbuf_tensor("cg", [128, E], F32))
        ARB = 177 * 1024
        arena = En(nc.sbuf_tensor("arena", [128, ARB // 2], BF16))
        abase = nc.lookup_mloc(arena).addr

        def carve(name, shape, dtype, off):
            import math
            nbytes = 1
            for s_ in shape[1:]:
                nbytes *= s_
            nbytes *= 4 if dtype in (F32, I32, U32) else 2
            assert off + nbytes <= ARB, f"{name}: {off}+{nbytes} > {ARB}"
            return nc.alloc_sbuf_tensor_at(name, shape, dtype, offset=abase + off)

        PB = [En(nc.psum_tensor(f"PB{i}", [128, 1024], F32)) for i in range(2)]
        PC = [En(nc.psum_tensor(f"PC{i}", [128, 512], F32)) for i in range(2)]
        PT = [En(nc.psum_tensor(f"PT{i}", [128, 128], BF16)) for i in range(2)]
        PF = [PB[0][:, 0:512], PB[0][:, 512:1024],
              PB[1][:, 0:512], PB[1][:, 512:1024],
              PC[0][:, :], PC[1][:, :]]

        ld = S.sem("ld")
        gld = S.sem("gld")
        sld = S.sem("sld")
        pe = S.sem("pe")
        act = S.sem("act")
        dve = S.sem("dve")
        sto = S.sem("sto")
        cc = S.sem("cc")

        ranks = {}

        def setup_rank(tag):
            def f(e):
                rb = e.alloc_register(f"rb_{tag}")
                rp = e.alloc_register(f"rp_{tag}")
                e.cc_rank_ld(rb, replica_groups=[[0, 2, 4, 6], [1, 3, 5, 7]])
                e.cc_rank_ld(rp, replica_groups=[[0, 1], [2, 3], [4, 5], [6, 7]])
                ranks[tag] = (e.compute_val(rb), e.compute_val(rp))
            return f
        S.emit("sync", setup_rank("sync"))
        S.emit("gpsimd", setup_rank("gpsimd"))

        def vstep(fn, extra=()):
            S.emit("vector", fn, waits=[S.now(dve)] + list(extra), inc=dve)

        def barrier(exclude=()):
            names = [n for n in ("ld", "gld", "sld", "pe", "act", "dve", "sto", "cc")
                     if n not in exclude]
            vals = [S.now(s) for s in names]
            vals = [v for v in vals if v[1] > 0]
            for eng in ("sync", "gpsimd", "scalar", "vector", "tensor"):
                S.emit(eng, lambda e: e.nop(), waits=vals)

        cons = dict(ones128=ones128, ones1=ones1, sq=sqs, st4=st4, tmp=tmp,
                    epsc=epsc_t[0:1, 0:1])

        for buf, val in ((ones128, 1.0), (ones1, 1.0), (mshift, -MSHIFT),
                         (negbig, NEGBIG), (nbigf, -1e4), (capg, float(CAP - 1)),
                         (eps128, EPS), (epsc_t, EPS), (biasall, -MSHIFT),
                         (offloc, -1.0), (c512, 512.0), (c511, 511.0),
                         (c8c, 8.0), (c7c, 7.0), (c8192, 8192.0)):
            S.emit("vector", lambda e, buf=buf, val=val: e.memset(buf[:], val), inc=dve)
        memv = S.now(dve)
        zrow = carve("zrow", [128, 1024], BF16, 179200)
        S.emit("vector", lambda e: e.memset(zrow[:], 0.0), inc=dve)
        memv = S.now(dve)
        for r_ in range(E * CAP // 128):
            S.dma("scalar", dispD[r_ * 128:(r_ + 1) * 128, :], zrow[:],
                  waits=[memv] if r_ == 0 else (), inc=sld)
        zfv = S.now(sld)
        trif = carve("trif", [128, 2048], F32, 0)
        xcast = carve("xcast", [128, 16, 1024], BF16, 8192)
        xfm0 = carve("xfm0", [128, 8, 2048], BF16, 8192 + 32768)
        brow = carve("brow", [1, 2048], F32, 73728)
        wrow = carve("wrow", [1, 2048], F32, 73728 + 8192)
        trow = carve("trow", [1, 2048], BF16, 73728 + 16384)
        S.dma("sync", trif[:], trimask, inc=ld)
        S.emit("vector", lambda e: e.tensor_copy(tri[:], trif[:]), waits=[S.now(ld)], inc=dve)
        for dst, srcd in ((UTs, UTd), (id128f, idd), (pmask, pmaskd), (iota16, iota16d),
                          (iotaBIG, iotaBIGd), (bigp, bigpd), (idb, id16), (gg, g0T),
                          (bb0, b0T), (bb1, b1T), (hb2, b2T), (bq_s, bqT), (bk_s, bkT),
                          (bvr, bvrow), (bo_s, boT), (g_s, gT), (b_s, bT)):
            S.dma("sync", dst[:], srcd, inc=ld)
        S.dma("sync", gw[:], gwT.rearrange("(c p) o -> p c o", p=128), inc=ld)
        S.dma("sync", brow[:], bimp, inc=ld)
        S.dma("gpsimd", id128b[:], idd, inc=gld)
        ldv0 = S.now(ld)
        S.emit("vector", lambda e: e.tensor_scalar_mul(hb2[:], hb2[:], 0.5),
               waits=[ldv0], inc=dve)
        constv = S.now(dve)
        S.dma("gpsimd", xcast[:], xtok.rearrange("(c p) d -> p c d", p=128), inc=gld)
        g1 = S.now(gld)
        S.dma("sync", ag1in[1024:2048, :].rearrange("r (b f) -> (r b) f", b=2)
              .rearrange("(c p) d -> p c d", p=128), xcast[:], waits=[g1], inc=ld)
        S.dma("gpsimd", xfm0[:], xT.rearrange("(c p) t -> p c t", p=128), inc=gld)
        g2 = S.now(gld)
        S.dma("sync", ag1in[0:1024, :].rearrange("(c p) t -> p c t", p=128), xfm0[:],
              waits=[g2], inc=ld)
        m1 = cells[0:1, 0:1]
        m2 = cells[0:1, 1:2]
        m3 = cells[0:1, 2:3]
        nb = negbig[0:1, 0:1]
        vstep(lambda e: e.reduce_max(m1, brow[:], axis=AX.X), extra=[ldv0, memv])
        vstep(lambda e: e.tensor_single_scalar(trow[:], brow[:], m1, ALU.is_ge))
        vstep(lambda e: e.scalar_tensor_tensor(wrow[:], trow[:], nb, brow[:], ALU.mult, ALU.add))
        vstep(lambda e: e.reduce_max(m2, wrow[:], axis=AX.X))
        vstep(lambda e: e.tensor_single_scalar(trow[:], wrow[:], m2, ALU.is_ge))
        vstep(lambda e: e.scalar_tensor_tensor(wrow[:], trow[:], nb, wrow[:], ALU.mult, ALU.add))
        vstep(lambda e: e.reduce_max(m3, wrow[:], axis=AX.X))
        vstep(lambda e: e.tensor_single_scalar(trow[:], brow[:], m3, ALU.is_ge))
        vstep(lambda e: e.tensor_scalar_mul(wrow[:], trow[:], nb))
        mv = S.now(dve)
        S.dma("sync", mrow.ap().rearrange("(a b) -> a b", a=1), wrow[:], waits=[mv], inc=ld)
        S.dma("sync", mask16[:], mrow.ap().rearrange("(c p) -> c p", c=16),
              waits=[S.now(ld)], inc=ld)
        S.emit("tensor", lambda e: e.transpose(PF[5][:, 0:16], mask16[:], idb[:]),
               waits=[S.now(ld)], inc=pe)
        S.emit("vector", lambda e: e.tensor_scalar(biasall[:, 0:16], PF[5][:, 0:16],
                                                   1.0, mshift[:, 0:1], ALU.mult, ALU.add),
               waits=[S.now(pe), memv], inc=dve)

        S.emit("gpsimd", lambda e: e.collective_compute(
            "AllGather", ALU.bypass, replica_groups=[list(range(8))],
            ins=[ag1in.ap()], outs=[ag1out.ap()]),
            waits=[S.now(ld), S.now(gld)], inc=cc)
        S.emit("sync", lambda e: e.nop(), waits=[S.now(cc)])
        for half in range(2):
            def cpf(e, half=half):
                vb, vp = ranks["sync"]
                v4 = ag1out.ap().rearrange("(b h s r) c -> b h s r c", b=4, h=2, s=4)
                return e.dma_start(out=my1f[half], in_=v4[vb, half, vp])
            S.emit("sync", cpf, inc=ld, inc_k=16)
            def cpt(e, half=half):
                vb, vp = ranks["sync"]
                v6 = ag1out.ap().rearrange(
                    "(b h z t2) (w q fl) -> b h z q (t2 w) fl",
                    b=4, h=2, z=2, w=2, q=2)
                return e.dma_start(out=my1t[half], in_=v6[vb, half, 1, vp])
            S.emit("sync", cpt, inc=ld, inc_k=16)
        barrier()
        xh1 = carve("xh1", [128, 4, 2, 2048], BF16, 0)
        vt1 = [carve(f"vt1_{h}", [128, 32, 65], BF16, 32768 + h * 4160) for h in range(8)]
        ptab1 = [carve(f"ptab1_{i}", [128, 1024], BF16, 66048 + i * 2048) for i in range(2)]
        ct1 = [carve(f"ct1_{i}", [128, 512], F32, 70400 + i * 2048) for i in range(2)]
        p_cA, p_cB = PF[4], PF[5]
        lbuf1 = carve("lbuf1", [128, 512], F32, 74496)
        my1fv = my1f.ap().rearrange("h2 (g p) t -> h2 g p t", p=128)
        my1tv = my1t.ap().rearrange("h2 (c p) (m d2) -> h2 m p c d2", p=128, d2=64)
        for hp in range(4):
            for half in range(2):
                S.dma("sync", xh1[:, hp, half, :], my1fv[half, hp], inc=ld)
        for h in range(8):
            S.emit("vector", lambda e, h=h: e.memset(vt1[h][:], 1.0), inc=dve)
        vmem1 = S.now(dve)
        for h in range(8):
            for half in range(2):
                def fv3(e, h=h, half=half):
                    return e.dma_start(out=vt1[h][:, 16 * half:16 * half + 16, 0:64],
                                       in_=my1tv[half, h])
                S.emit("sync", fv3, waits=[vmem1], inc=ld, inc_k=16)
        xh1v = S.now(ld)
        vt1v = xh1v
        stomarks1 = {}

        def attn_block(nkc, fqk, fexp, fmask, fpv, extra_qk0, extra_pv0):
            expmarks = {}
            pvmarks = {}

            def do_qk(kc):
                wq0 = [expmarks[kc - 2]] if kc >= 2 else list(extra_qk0)
                S.emit("tensor", fqk(kc, 0), waits=wq0)
                S.emit("tensor", fqk(kc, 1), inc=pe)
                return S.now(pe)

            def do_exp(kc, qkm):
                we = [qkm, pvmarks[kc - 2] if kc >= 2 else S.now(pe)]
                S.emit("scalar", fexp(kc), waits=we, inc=act)
                expmarks[kc] = S.now(act)

            def do_pv(kc):
                pvw = [expmarks[kc]]
                mv_ = fmask(kc, expmarks[kc])
                if mv_ is not None:
                    pvw = [mv_]
                wv0 = pvw + (list(extra_pv0) if kc == 0 else [])
                S.emit("tensor", fpv(kc, 0, nkc), waits=wv0)
                S.emit("tensor", fpv(kc, 1, nkc), inc=pe)
                pvmarks[kc] = S.now(pe)

            qkms = {0: do_qk(0)}
            for kc in range(nkc):
                if kc + 1 < nkc:
                    qkms[kc + 1] = do_qk(kc + 1)
                do_exp(kc, qkms[kc])
                do_pv(kc)
            return S.now(pe)

        for hp in range(4):
            hA, hB = 2 * hp, 2 * hp + 1
            for qb in range(8):
                nkc = 4 * (qb + 1)
                evac_prev = S.now(dve)
                qhalf, qrel = qb // 4, qb % 4

                def fqk(kc, side, hp=hp, qhalf=qhalf, qrel=qrel):
                    khalf, krel = kc // 16, kc % 16
                    par = kc % 2
                    r0 = 64 * side
                    return lambda e: e.matmul(
                        PB[par][:, 512 * side:512 * side + 512],
                        lhsT=xh1[r0:r0 + 64, hp, khalf, krel * 128:(krel + 1) * 128],
                        rhs=xh1[r0:r0 + 64, hp, qhalf, qrel * 512:(qrel + 1) * 512],
                        start=True, stop=True, tile_position=(r0, 0))

                def fexp(kc):
                    par = kc % 2
                    return lambda e: e.activation(
                        ptab1[par][:], PB[par][:, :], AF.Exp,
                        bias=mshift[:, 0:1], scale=0.125)

                def fmask(kc, em, qb=qb):
                    if kc < 4 * qb:
                        return None
                    j = kc - 4 * qb
                    par = kc % 2
                    S.emit("vector",
                           lambda e: e.tensor_tensor(
                               ptab1[par][:, 0:512], ptab1[par][:, 0:512],
                               tri[:, j * 512:(j + 1) * 512], ALU.mult),
                           waits=[em])
                    S.emit("vector",
                           lambda e: e.tensor_tensor(
                               ptab1[par][:, 512:1024], ptab1[par][:, 512:1024],
                               tri[:, j * 512:(j + 1) * 512], ALU.mult),
                           inc=dve)
                    return S.now(dve)

                def fpv(kc, side, nkc, hA=hA, hB=hB):
                    par = kc % 2
                    h = hA if side == 0 else hB
                    pc = p_cA if side == 0 else p_cB
                    return lambda e: e.matmul(
                        pc[0:65, :], lhsT=vt1[h][:, kc, :],
                        rhs=ptab1[par][:, 512 * side:512 * side + 512],
                        start=(kc == 0), stop=(kc == nkc - 1))

                ctxm = attn_block(nkc, fqk, fexp, fmask, fpv,
                                  [xh1v, S.now(act), S.now(dve)],
                                  [evac_prev, vmem1, vt1v])
                it = hp * 8 + qb
                par_c = it % 2
                wst = [stomarks1[it - 2]] if it >= 2 else []
                S.emit("vector", lambda e: e.reciprocal(linvA[:], p_cA[64:65, :]),
                       waits=[ctxm], inc=dve)
                S.emit("vector", lambda e: e.reciprocal(linvB[:], p_cB[64:65, :]),
                       inc=dve)
                lv = S.now(dve)
                S.emit("tensor",
                       lambda e: e.matmul(PB[0][0:64, 0:512], lhsT=ones1[0:1, 0:64],
                                          rhs=linvA[:], start=True, stop=True),
                       waits=[lv, S.now(act)])
                S.emit("tensor",
                       lambda e: e.matmul(PB[0][64:128, 0:512], lhsT=ones1[0:1, 0:64],
                                          rhs=linvB[:], start=True, stop=True),
                       inc=pe)
                lbv = S.now(pe)
                S.emit("vector", lambda e: e.tensor_copy(lbuf1[:], PB[0][:, 0:512]),
                       waits=[lbv], inc=dve)
                S.emit("vector",
                       lambda e, par_c=par_c: e.tensor_tensor(
                           ct1[par_c][0:64, :], p_cA[0:64, :], lbuf1[0:64, :], ALU.mult),
                       waits=wst + [S.now(dve)], inc=dve)
                S.emit("vector",
                       lambda e, par_c=par_c: e.tensor_tensor(
                           ct1[par_c][64:128, :], p_cB[0:64, :], lbuf1[64:128, :], ALU.mult),
                       inc=dve)
                cv = S.now(dve)
                qhalf, qrel = qb // 4, qb % 4
                S.dma("sync",
                      ag2in[qhalf * 512 + hp * 128: qhalf * 512 + (hp + 1) * 128,
                            qrel * 512:(qrel + 1) * 512],
                      ct1[par_c][:], waits=[cv], inc=sto)
                stomarks1[it] = S.now(sto)
        S.emit("gpsimd", lambda e: e.collective_compute(
            "AllGather", ALU.bypass, replica_groups=[list(range(8))],
            ins=[ag2in.ap()], outs=[ag2out.ap()]),
            waits=[S.now(sto)], inc=cc)
        S.emit("sync", lambda e: e.nop(), waits=[S.now(cc)])
        for half in range(2):
            def cp2(e, half=half):
                vb, vp = ranks["sync"]
                v2 = ag2out.ap().rearrange("(w hh f) c -> w hh f c", w=8, hh=2)
                return e.dma_start(out=my2[half], in_=v2[vb * 2 + half, vp])
            S.emit("sync", cp2, inc=ld, inc_k=16)
        barrier()

        tw = 256
        w1s = carve("w1s", [128, DC, F], BF16, 0)
        w2s = carve("w2s", [128, F // 128, D], BF16, 65536)
        qbuf = [carve(f"qbuf{i}", [128, 512], F32, 131072 + i * 2048) for i in range(2)]
        scrow = [carve(f"scrow{i}", [1, 512], F32, 135168 + i * 2048) for i in range(2)]
        ya2 = carve("ya2", [128, DC, tw], F32, 139264)
        x12 = carve("x12", [128, DC, tw], F32, 147456)
        hb2b = carve("hb2b", [128, DC, tw], BF16, 155648)
        yb2 = carve("yb2", [128, F // 128, tw], BF16, 159744)
        p_y = [PF[0], PF[1]]
        p_z = [PF[2], PF[3]]
        psd2 = dict(s=PF[4][0:1, 0:tw], q=PF[4][0:1, 256:256 + tw],
                    a=PF[5][:, 0:tw], b=PF[5][:, 256:256 + tw])
        agv2 = lambda: ag2out.ap().rearrange("(r f) c -> r f c", r=8)

        dmarks = []

        def dequant2(qdram, scdram, ngrp, ncol, wdst):
            nfs = ncol // 512
            for g in range(ngrp):
                for fs in range(nfs):
                    i = len(dmarks)
                    pb = i % 2
                    ppsum = p_y[0] if pb == 0 else p_y[1]
                    wprev = [dmarks[i - 2]] if i >= 2 else [S.now(act), S.now(dve)]
                    S.dma("gpsimd", qbuf[pb][:],
                          qdram[g * 128:(g + 1) * 128, fs * 512:(fs + 1) * 512],
                          waits=wprev, inc=gld)
                    S.dma("sync", scrow[pb][:],
                          scdram[g:g + 1, fs * 512:(fs + 1) * 512],
                          waits=wprev, inc=ld)
                    gl, ll = S.now(gld), S.now(ld)
                    S.emit("tensor",
                           lambda e, ppsum=ppsum, pb=pb: e.matmul(
                               ppsum[:], lhsT=ones1[:], rhs=scrow[pb][:],
                               start=True, stop=True),
                           waits=[ll, constv] + wprev, inc=pe)
                    pv = S.now(pe)
                    S.emit("vector",
                           lambda e, ppsum=ppsum, pb=pb, g=g, fs=fs: e.scalar_tensor_tensor(
                               wdst[:, g, fs * 512:(fs + 1) * 512],
                               qbuf[pb][:], -128.0, ppsum[:], ALU.add, ALU.mult),
                           waits=[pv, gl], inc=dve)
                    dmarks.append(S.now(dve))

        dequant2(fc1q, sc1, DC, F, w1s)
        dequant2(fc2q, sc2, F // 128, D, w2s)
        wready = S.now(dve)

        for it in range(TPC // tw):
            tsl = slice(it * tw, (it + 1) * tw)
            lw = [S.now(dve), S.now(sto), S.now(gld)]
            S.dma("sync", ya2[:], xT[:, tsl].rearrange("(c p) t -> p c t", p=128),
                  waits=lw, inc=ld)
            xv = S.now(ld)
            for half in range(2):
                def facc2(e, it=it, half=half):
                    view = my2.ap().rearrange("h2 (c p) t -> h2 p c t", p=128)
                    return e.dma_start(
                        out=ya2[:, 4 * half:4 * half + 4, :],
                        in_=view[half][:, :, it * tw:(it + 1) * tw],
                        accum_op=ALU.add)
                S.emit("gpsimd", facc2, waits=[xv], inc=gld, inc_k=16)
            y1v = S.now(gld)
            emit_ln(S, lambda c: ya2[:, c, :], lambda c: x12[:, c, :], DC, tw,
                    cons, psd2, pe, act, dve, y1v)
            x1v = S.now(dve)
            hv = emit_ln(S, lambda c: x12[:, c, :], lambda c: tmp[:, 0:tw], DC, tw,
                         cons, psd2, pe, act, dve, x1v,
                         gamma=lambda c: gg[:, c:c + 1], beta=lambda c: bb0[:, c:c + 1],
                         dst_cast=lambda c: hb2b[:, c, :])
            for fg in range(F // 128):
                pyb = p_y[fg % 2]
                wy = [S.now(act)] if fg >= 2 else [hv, wready]
                for g in range(DC):
                    S.emit("tensor",
                           lambda e, g=g, fg=fg, pyb=pyb: e.matmul(
                               pyb[:, 0:tw], lhsT=w1s[:, g, fg * 128:(fg + 1) * 128],
                               rhs=hb2b[:, g, :], start=(g == 0), stop=(g == DC - 1)),
                           waits=wy if g == 0 else (),
                           inc=pe if g == DC - 1 else None)
                pv = S.now(pe)
                S.emit("scalar",
                       lambda e, fg=fg, pyb=pyb: e.activation(
                           yb2[:, fg, :], pyb[:, 0:tw], AF.Silu,
                           bias=bb1[:, fg:fg + 1], scale=1.0),
                       waits=[pv], inc=act)
            yv = S.now(act)
            for dc_ in range(DC):
                pz = p_z[dc_ % 2]
                wz = [S.now(dve)] if dc_ >= 2 else [yv]
                for fg in range(F // 128):
                    S.emit("tensor",
                           lambda e, dc_=dc_, fg=fg, pz=pz: e.matmul(
                               pz[:, 0:tw], lhsT=w2s[:, fg, dc_ * 128:(dc_ + 1) * 128],
                               rhs=yb2[:, fg, :], start=(fg == 0), stop=(fg == F // 128 - 1)),
                           waits=(wz + [yv]) if fg == 0 else (),
                           inc=pe if fg == F // 128 - 1 else None)
                zv = S.now(pe)
                S.emit("vector",
                       lambda e, dc_=dc_, pz=pz: e.tensor_scalar(
                           tmp[:, 0:tw], pz[:, 0:tw], 0.5, hb2[:, dc_:dc_ + 1],
                           ALU.mult, ALU.add),
                       waits=[zv], inc=dve)
                S.emit("vector",
                       lambda e, dc_=dc_: e.tensor_tensor(fo[:, 0:tw], tmp[:, 0:tw],
                                                          x12[:, dc_, :], ALU.add),
                       waits=[S.now(dve)], inc=dve)
                S.emit("vector",
                       lambda e, dc_=dc_: e.tensor_tensor(ya2[:, dc_, :], fo[:, 0:tw],
                                                          x12[:, dc_, :], ALU.add),
                       waits=[S.now(dve)], inc=dve)
            y2v = S.now(dve)
            emit_ln(S, lambda c: ya2[:, c, :], lambda c: x12[:, c, :], DC, tw,
                    cons, psd2, pe, act, dve, y2v)
            ov = S.now(dve)
            S.dma("sync", x2loc.ap()[:, tsl].rearrange("(c p) t -> p c t", p=128), x12[:],
                  waits=[ov], inc=sto)
            S.emit("gpsimd",
                   lambda e, tsl=tsl: e.dma_start(
                       out=ag3in[:, tsl].rearrange("(c p) t -> p c t", p=128), in_=x12[:]),
                   waits=[ov], inc=gld, inc_k=16)
        S.emit("gpsimd", lambda e: e.collective_compute(
            "AllGather", ALU.bypass, replica_groups=[list(range(8))],
            ins=[ag3in.ap()], outs=[ag3out.ap()]),
            waits=[S.now(gld), S.now(sto)], inc=cc)
        S.emit("sync", lambda e: e.nop(), waits=[S.now(cc)])
        for half in range(2):
            def cp3(e, half=half):
                vb, vp = ranks["sync"]
                v3 = ag3out.ap().rearrange("(w r) c -> w r c", w=8)
                return e.dma_start(out=my3[half], in_=v3[vb * 2 + half])
            S.emit("sync", cp3, inc=ld, inc_k=16)
        barrier()

        xh3 = carve("xh3", [128, 8, 2, 2048], BF16, 0)
        qh3 = carve("qh3", [128, 4, 4096], BF16, 65536)
        kh3 = carve("kh3", [128, 4, 4096], BF16, 98304)
        vtok3 = carve("vtok3", [128, 32, 8, 65], BF16, 131072)
        wbuf3 = carve("wbuf3", [128, DC, 512], BF16, 164352)
        ptab3 = [carve(f"ptab3_{i}", [128, 1024], BF16, 172544 + i * 2048) for i in range(2)]
        ct3 = [carve(f"ct3_{i}", [128, 512], F32, 176640 + i * 2048) for i in range(2)]
        agv3 = lambda: ag3out.ap().rearrange("(r f) c -> r f c", r=8)
        my3v = my3.ap().rearrange("h2 (a p) c -> h2 p a c", p=128)
        for half in range(2):
            S.dma("sync", xh3[:, :, half, :], my3v[half], inc=ld)
        xh3v = S.now(ld)
        pmarks3 = []
        for wi, (wdram, dst, bias) in enumerate(((WqT, qh3, bq_s), (WkT, kh3, bk_s))):
            S.dma("gpsimd", wbuf3[:], wdram.rearrange("(c p) o -> p c o", p=128),
                  waits=[S.now(pe)], inc=gld)
            wldv = S.now(gld)
            for oc in range(4):
                for ts in range(8):
                    i = len(pmarks3)
                    pp = PF[i % 2]
                    w0 = ([xh3v, ldv0] if i < 2 else [pmarks3[i - 2]]) + [wldv]
                    for dcx in range(DC):
                        S.emit("tensor",
                               lambda e, dcx=dcx, oc=oc, ts=ts, pp=pp: e.matmul(
                                   pp[:], lhsT=wbuf3[:, dcx, oc * 128:(oc + 1) * 128],
                                   rhs=xh3[:, dcx, ts // 4, (ts % 4) * 512:(ts % 4 + 1) * 512],
                                   start=(dcx == 0), stop=(dcx == DC - 1)),
                               waits=w0 if dcx == 0 else (),
                               inc=pe if dcx == DC - 1 else None)
                    pv = S.now(pe)
                    S.emit("vector",
                           lambda e, dst=dst, bias=bias, oc=oc, ts=ts, pp=pp: e.tensor_scalar_add(
                               dst[:, oc, ts * 512:(ts + 1) * 512], pp[:], bias[:, oc:oc + 1]),
                           waits=[pv], inc=dve)
                    pmarks3.append(S.now(dve))
        S.dma("gpsimd", wbuf3[:], WvT.rearrange("(c p) o -> p c o", p=128),
              waits=[S.now(pe)], inc=gld)
        wvldv = S.now(gld)
        S.emit("tensor", lambda e: e.matmul(PF[2][:], lhsT=ones1[:], rhs=bvr[:],
                                            start=True, stop=True),
               waits=[ldv0, memv, S.now(act), S.now(dve)], inc=pe)
        S.emit("vector", lambda e: e.tensor_copy(bv_b[:], PF[2][:]),
               waits=[S.now(pe)], inc=dve)
        bvv = S.now(dve)
        S.emit("vector", lambda e: e.memset(vtok3[:], 1.0), inc=dve)
        vmem3 = S.now(dve)
        for tc in range(32):
            i = len(pmarks3)
            pp = PF[i % 2]
            w0 = [pmarks3[i - 2], wvldv, vmem3]
            for dcx in range(DC):
                S.emit("tensor",
                       lambda e, dcx=dcx, tc=tc, pp=pp: e.matmul(
                           pp[:], lhsT=xh3[:, dcx, tc // 16, (tc % 16) * 128:(tc % 16 + 1) * 128],
                           rhs=wbuf3[:, dcx, :], start=(dcx == 0), stop=(dcx == DC - 1)),
                       waits=w0 if dcx == 0 else (),
                       inc=pe if dcx == DC - 1 else None)
            pv = S.now(pe)
            S.emit("vector",
                   lambda e, tc=tc, pp=pp: e.scalar_tensor_tensor(
                       vtok3[:, tc, :, 0:64], pp.rearrange("p (h d2) -> p h d2", h=8),
                       1.0, bv_b.ap().rearrange("p (h d2) -> p h d2", h=8),
                       ALU.mult, ALU.add),
                   waits=[pv, bvv], inc=dve)
            pmarks3.append(S.now(dve))
        projv = S.now(dve)
        lbuf3 = carve("lbuf3", [128, 512], F32, 164352)
        wo4 = carve("wo4", [128, DC, D], BF16, 0)
        S.dma("gpsimd", wo4[:], WoT.rearrange("(c p) o -> p c o", p=128),
              waits=[S.now(pe)], inc=gld)
        wov = S.now(gld)
        stomarks3 = {}
        for hp in range(4):
            hA, hB = 2 * hp, 2 * hp + 1
            for qb in range(8):
                evac_prev = S.now(dve)

                def fqk3(kc, side, hp=hp, qb=qb):
                    par = kc % 2
                    r0 = 64 * side
                    return lambda e: e.matmul(
                        PB[par][:, 512 * side:512 * side + 512],
                        lhsT=kh3[r0:r0 + 64, hp, kc * 128:(kc + 1) * 128],
                        rhs=qh3[r0:r0 + 64, hp, qb * 512:(qb + 1) * 512],
                        start=True, stop=True, tile_position=(r0, 0))

                def fexp3(kc):
                    par = kc % 2
                    return lambda e: e.activation(
                        ptab3[par][:], PB[par][:, :], AF.Exp,
                        bias=biasall[:, kc:kc + 1], scale=0.125)

                def fmask3(kc, em):
                    return None

                def fpv3(kc, side, nkc, hA=hA, hB=hB):
                    par = kc % 2
                    h = hA if side == 0 else hB
                    pc = p_cA if side == 0 else p_cB
                    return lambda e: e.matmul(
                        pc[0:65, :], lhsT=vtok3[:, kc, h, :],
                        rhs=ptab3[par][:, 512 * side:512 * side + 512],
                        start=(kc == 0), stop=(kc == nkc - 1))

                ctxm = attn_block(32, fqk3, fexp3, fmask3, fpv3,
                                  [projv, S.now(act), S.now(dve)],
                                  [evac_prev])
                it = hp * 8 + qb
                par_c = it % 2
                wst = [stomarks3[it - 2]] if it >= 2 else []
                S.emit("vector", lambda e: e.reciprocal(linvA[:], p_cA[64:65, :]),
                       waits=[ctxm], inc=dve)
                S.emit("vector", lambda e: e.reciprocal(linvB[:], p_cB[64:65, :]),
                       inc=dve)
                lv = S.now(dve)
                S.emit("tensor",
                       lambda e: e.matmul(PB[0][0:64, 0:512], lhsT=ones1[0:1, 0:64],
                                          rhs=linvA[:], start=True, stop=True),
                       waits=[lv, S.now(act)])
                S.emit("tensor",
                       lambda e: e.matmul(PB[0][64:128, 0:512], lhsT=ones1[0:1, 0:64],
                                          rhs=linvB[:], start=True, stop=True),
                       inc=pe)
                lbv = S.now(pe)
                S.emit("vector", lambda e: e.tensor_copy(lbuf3[:], PB[0][:, 0:512]),
                       waits=[lbv], inc=dve)
                S.emit("vector",
                       lambda e, par_c=par_c: e.tensor_tensor(
                           ct3[par_c][0:64, :], p_cA[0:64, :], lbuf3[0:64, :], ALU.mult),
                       waits=wst + [S.now(dve)], inc=dve)
                S.emit("vector",
                       lambda e, par_c=par_c: e.tensor_tensor(
                           ct3[par_c][64:128, :], p_cB[0:64, :], lbuf3[64:128, :], ALU.mult),
                       inc=dve)
                cv = S.now(dve)
                qhalf, qrel = qb // 4, qb % 4
                S.emit("gpsimd",
                       lambda e, hp=hp, qhalf=qhalf, qrel=qrel, par_c=par_c: e.dma_start(
                           out=ag4in[qhalf * 512 + hp * 128: qhalf * 512 + (hp + 1) * 128,
                                     qrel * 512:(qrel + 1) * 512],
                           in_=ct3[par_c][:]),
                       waits=[cv], inc=gld, inc_k=16)
                stomarks3[it] = S.now(gld)
        S.emit("gpsimd", lambda e: e.collective_compute(
            "AllGather", ALU.bypass, replica_groups=[list(range(8))],
            ins=[ag4in.ap()], outs=[ag4out.ap()]),
            waits=[S.now(gld)], inc=cc)
        S.emit("sync", lambda e: e.nop(), waits=[S.now(cc)])
        for half in range(2):
            def cp4(e, half=half):
                vb, vp = ranks["sync"]
                v4b = ag4out.ap().rearrange("(w hh f) c -> w hh f c", w=8, hh=2)
                return e.dma_start(out=my4[half], in_=v4b[vb * 2 + half, vp])
            S.emit("sync", cp4, inc=ld, inc_k=16)
        barrier()

        p_ra = PF[0][:, 0:128]
        mx = cells[:, 0:1]
        nmx = cells[:, 1:2]
        ssum = cells[:, 2:3]
        rsn = cells[:, 11:12]
        nmu = cells[:, 12:13]
        ssq = cells[:, 13:14]
        ex2 = cells[:, 14:15]
        var = cells[:, 15:16]
        p_cum = p_ra[:, 0:16]
        p_cnt = p_ra[0:1, 16:32]
        p_pr = p_ra[0:1, 32:48]

        tw4 = 512
        ch4 = carve("ch4", [128, DC, tw4], BF16, 16384)
        x24 = carve("x24", [128, DC, tw4], F32, 24576)
        yy4 = carve("yy4", [128, DC, tw4], F32, 40960)
        zz4 = carve("zz4", [128, DC, tw4], F32, 57344)
        x34 = carve("x34", [128, DC, tw4], F32, 73728)
        tw4 = 256
        lgt4 = carve("lgt4", [128, 2, E], F32, 90112)
        xtk4 = carve("xtk4", [128, 2, D], F32, 90368)
        agv4 = lambda: ag4out.ap().rearrange("(r f) c -> r f c", r=8)
        rtm = [None]
        xall = carve("xall", [128, 16, D], BF16, 98560)

        def route_tile(lt, first):
            lgt = lg[:, lt, :]
            vstep(lambda e, lgt=lgt: e.reduce_max(mx, lgt, axis=AX.X))
            vstep(lambda e: e.tensor_scalar_mul(nmx, mx, -1.0))
            S.emit("scalar", lambda e, lgt=lgt: e.activation(et16[:], lgt, AF.Exp,
                                                             bias=nmx, scale=1.0),
                   waits=[S.now(dve)], inc=act)
            vstep(lambda e: e.reduce_sum(ssum, et16[:], axis=AX.X), extra=[S.now(act)])
            vstep(lambda e, lt=lt: e.reciprocal(psel[:, lt:lt + 1], ssum))
            vstep(lambda e, lgt=lgt: e.tensor_single_scalar(t16[:], lgt, mx, ALU.is_ge))
            vstep(lambda e: e.scalar_tensor_tensor(t16[:], t16[:], nbigf[:, 0:1],
                                                   iotaBIG[:], ALU.mult, ALU.add))
            vstep(lambda e, lt=lt: e.tensor_reduce(sel[:, lt:lt + 1], t16[:], AX.X, ALU.min))
            vstep(lambda e, lt=lt: e.tensor_single_scalar(ohb[:, lt, :], iota16[:],
                                                          sel[:, lt:lt + 1], ALU.is_equal))
            ohv = S.now(dve)
            S.emit("tensor", lambda e, lt=lt: e.matmul(p_cum, lhsT=UTs[:], rhs=ohb[:, lt, :],
                                                       start=True, stop=False),
                   waits=[ohv] + ([S.now(act)] if first else []))
            S.emit("tensor", lambda e: e.matmul(p_cum, lhsT=ones1[:], rhs=offloc[:],
                                                start=False, stop=True), inc=pe)
            cumv = S.now(pe)
            vstep(lambda e, lt=lt: e.tensor_tensor(t16[:], p_cum, ohb[:, lt, :], ALU.mult),
                  extra=[cumv])
            vstep(lambda e, lt=lt: e.reduce_sum(lloc[:, lt:lt + 1], t16[:], axis=AX.X))
            rdv = S.now(dve)
            S.emit("tensor", lambda e, lt=lt: e.matmul(p_cnt, lhsT=ones128[:],
                                                       rhs=ohb[:, lt, :], start=True, stop=True),
                   waits=[rdv], inc=pe)
            cntv = S.now(pe)
            vstep(lambda e: e.tensor_tensor(offloc[:], offloc[:], p_cnt, ALU.add),
                  extra=[cntv])
            vstep(lambda e, lt=lt: e.tensor_copy(crow[0:1, lt * E:(lt + 1) * E], p_cnt))
        for it in range(TPC // tw4):
            tsl = slice(it * tw4, (it + 1) * tw4)
            lw = [S.now(dve), S.now(sto), S.now(gld), S.now(act)]
            S.dma("sync", x24[:, :, 0:tw4], x2loc.ap()[:, tsl].rearrange("(c p) t -> p c t", p=128),
                  waits=lw, inc=ld)
            for half in range(2):
                def fc4(e, it=it, half=half):
                    view = my4.ap().rearrange("h2 (c p) t -> h2 p c t", p=128)
                    return e.dma_start(
                        out=ch4[:, 4 * half:4 * half + 4, 0:tw4],
                        in_=view[half][:, :, it * tw4:(it + 1) * tw4])
                S.emit("gpsimd", fc4, waits=lw, inc=gld, inc_k=16)
            lv, gv = S.now(ld), S.now(gld)
            for oc in range(DC):
                po = PF[oc % 2]
                wz = ([gv, lv, ldv0, wov] + ([rtm[0]] if rtm[0] else [])) if oc < 2                     else [S.now(dve)]
                for dci in range(DC):
                    S.emit("tensor",
                           lambda e, dci=dci, oc=oc, po=po: e.matmul(
                               po[:, 0:tw4], lhsT=wo4[:, dci, oc * 128:(oc + 1) * 128],
                               rhs=ch4[:, dci, 0:tw4], start=(dci == 0), stop=(dci == DC - 1)),
                           waits=wz if dci == 0 else (),
                           inc=pe if dci == DC - 1 else None)
                pv = S.now(pe)
                S.emit("vector",
                       lambda e, oc=oc, po=po: e.scalar_tensor_tensor(
                           yy4[:, oc, 0:tw4], po[:, 0:tw4], bo_s[:, oc:oc + 1], x24[:, oc, 0:tw4],
                           ALU.add, ALU.add),
                       waits=[pv], inc=dve)
            yv = S.now(dve)
            zv = emit_ln(S, lambda c: yy4[:, c, 0:tw4], lambda c: tmp[:, 0:tw4], DC, tw4,
                         cons, psd2, pe, act, dve, yv,
                         gamma=lambda c: g_s[:, c:c + 1], beta=lambda c: b_s[:, c:c + 1],
                         dst_cast=lambda c: zz4[:, c, 0:tw4])
            for c in range(DC):
                S.emit("vector",
                       lambda e, c=c: e.tensor_tensor(yy4[:, c, 0:tw4], x24[:, c, 0:tw4],
                                                      zz4[:, c, 0:tw4], ALU.add),
                       waits=[zv] if c == 0 else (), inc=dve)
            y3v = S.now(dve)
            emit_ln(S, lambda c: yy4[:, c, 0:tw4], lambda c: x34[:, c, 0:tw4], DC, tw4,
                    cons, psd2, pe, act, dve, y3v)
            x3v = S.now(dve)
            for s_ in range(tw4 // 128):
                for dci in range(DC):
                    i = s_ * DC + dci
                    ptf = PF[2 + (i % 2)]
                    S.emit("tensor",
                           lambda e, s_=s_, dci=dci, ptf=ptf: e.transpose(
                               ptf[:, 0:128],
                               x34[:, dci, s_ * 128:(s_ + 1) * 128], id128f[:]),
                           waits=[x3v] if i == 0 else ([S.now(dve)] if i >= 2 else []),
                           inc=pe)
                    pv = S.now(pe)
                    S.emit("vector",
                           lambda e, s_=s_, dci=dci, ptf=ptf: e.tensor_copy(
                               xtk4[:, s_, dci * 128:(dci + 1) * 128], ptf[:, 0:128]),
                           waits=[pv], inc=dve)
            tkv = S.now(dve)
            S.emit("vector",
                   lambda e, it=it: e.tensor_copy(xall[:, 2 * it:2 * it + 2, :], xtk4[:]),
                   waits=[tkv], inc=dve)
            S.dma("sync", x3tokD[tsl, :].rearrange("(s p) d2 -> p s d2", p=128), xtk4[:],
                  waits=[tkv], inc=sto)
            for s_ in range(tw4 // 128):
                pgb = PF[2 + (s_ % 2)]
                for dci in range(DC):
                    w0 = []
                    if dci == 0:
                        w0 = [x3v, S.now(dve)]
                    S.emit("tensor",
                           lambda e, s_=s_, dci=dci, pgb=pgb: e.matmul(
                               pgb[:, 0:E], lhsT=x34[:, dci, s_ * 128:(s_ + 1) * 128],
                               rhs=gw[:, dci, :], start=(dci == 0), stop=(dci == DC - 1)),
                           waits=w0,
                           inc=pe if dci == DC - 1 else None)
                gvv = S.now(pe)
                S.emit("vector",
                       lambda e, s_=s_, pgb=pgb: e.tensor_copy(lgt4[:, s_, :], pgb[:, 0:E]),
                       waits=[gvv], inc=dve)
            lgv = S.now(dve)
            for s_ in range(tw4 // 128):
                S.emit("vector",
                       lambda e, it=it, s_=s_: e.tensor_copy(lg[:, it * (tw4 // 128) + s_, :],
                                                             lgt4[:, s_, :]),
                       waits=[lgv], inc=dve)
                route_tile(it * (tw4 // 128) + s_, it == 0 and s_ == 0)
            rtm[0] = S.now(dve)
        p1v = S.now(dve)
        S.dma("sync", ag5in.ap(), crow[:], waits=[p1v], inc=ld)
        S.emit("gpsimd", lambda e: e.collective_compute(
            "AllGather", ALU.bypass, replica_groups=[list(range(8))],
            ins=[ag5in.ap()], outs=[ag5out.ap()]),
            waits=[S.now(ld)], inc=cc)
        S.emit("sync", lambda e: e.nop(), waits=[S.now(cc)])
        S.dma("sync", cg[:], ag5out.ap().rearrange("r (lt e2) -> (r lt) e2", e2=E), inc=ld)
        cgv = S.now(ld)
        S.emit("tensor", lambda e: e.matmul(p_pr, lhsT=pmask[:, 0:1], rhs=cg[:],
                                            start=True, stop=True),
               waits=[cgv, S.now(dve), S.now(act)], inc=pe)
        vstep(lambda e: e.tensor_copy(prefrow[:], p_pr), extra=[S.now(pe)])
        prefrow16 = carve("prefrow16", [1, 256], F32, 172032)
        r16b = carve("r16b", [128, 192], F32, 173056)
        for j in range(16):
            vstep(lambda e, j=j: e.tensor_copy(prefrow16[0:1, j * E:(j + 1) * E], prefrow[:]))
        S.emit("tensor", lambda e: e.matmul(PF[1][:, 0:256], lhsT=ones1[0:1, 0:128],
                                            rhs=prefrow16[:], start=True, stop=True),
               waits=[S.now(dve), S.now(act)], inc=pe)
        prbv = S.now(pe)
        (pftA, glocA, keepA, h8A, hsA, posA, tA, tB, aaS, aaC, nhsA, nh8A) =             [r16b[:, i * 16:(i + 1) * 16] for i in range(12)]
        ohv2 = ohb[:].rearrange("p l e -> p (l e)")
        vstep(lambda e: e.tensor_tensor(tmp[:, 0:256], PF[1][:, 0:256], ohv2, ALU.mult),
              extra=[prbv])
        vstep(lambda e: e.tensor_reduce(
            pftA, tmp[:, 0:256].rearrange("p (l e) -> p l e", l=16), AX.X, ALU.add))
        vstep(lambda e: e.tensor_tensor(glocA, lloc[:], pftA, ALU.add))
        vstep(lambda e: e.tensor_single_scalar(keepA, glocA, capg[:], ALU.is_le))
        vstep(lambda e: e.tensor_single_scalar(hsA, glocA, c512[:], ALU.is_ge))
        vstep(lambda e: e.tensor_single_scalar(nhsA, glocA, c511[:], ALU.is_le))
        vstep(lambda e: e.tensor_single_scalar(h8A, sel[:], c8c[:], ALU.is_ge))
        vstep(lambda e: e.tensor_single_scalar(nh8A, sel[:], c7c[:], ALU.is_le))
        vstep(lambda e: e.tensor_scalar_mul(tA, h8A, -15.0))
        vstep(lambda e: e.scalar_tensor_tensor(posA, sel[:], 2.0, tA, ALU.mult, ALU.add))
        vstep(lambda e: e.scalar_tensor_tensor(aaS, posA, 512.0, glocA, ALU.mult, ALU.add))
        vstep(lambda e: e.tensor_tensor(tB, hsA, posA, ALU.mult))
        vstep(lambda e: e.scalar_tensor_tensor(aaS, tB, 256.0, aaS, ALU.mult, ALU.add))
        vstep(lambda e: e.tensor_scalar_mul(tA, hsA, 7680.0))
        vstep(lambda e: e.tensor_tensor(aaS, aaS, tA, ALU.add))
        vstep(lambda e: e.tensor_single_scalar(tB, aaS, bigp[:], ALU.subtract))
        vstep(lambda e: e.tensor_tensor(tB, tB, keepA, ALU.mult))
        vstep(lambda e: e.tensor_single_scalar(tB, tB, bigp[:], ALU.add))
        vstep(lambda e: e.tensor_copy(kidxS[0][:], tB))
        vstep(lambda e: e.scalar_tensor_tensor(tB, h8A, -8.0, sel[:], ALU.mult, ALU.add))
        vstep(lambda e: e.scalar_tensor_tensor(aaC, tB, 1280.0, glocA, ALU.mult, ALU.add))
        vstep(lambda e: e.tensor_scalar_mul(tA, h8A, 10240.0))
        vstep(lambda e: e.tensor_tensor(aaC, aaC, tA, ALU.add))
        vstep(lambda e: e.tensor_single_scalar(tB, aaC, bigp[:], ALU.subtract))
        vstep(lambda e: e.tensor_tensor(tB, tB, keepA, ALU.mult))
        vstep(lambda e: e.tensor_single_scalar(tB, tB, bigp[:], ALU.add))
        vstep(lambda e: e.tensor_copy(kidxC[0][:], tB))
        kiv = S.now(dve)
        for lt in range(16):
            S.emit("gpsimd",
                   lambda e, lt=lt: e.indirect_dma_start(
                       out=dispD[0:128, :],
                       out_offset=bass.IndirectOffsetOnAxis(kidxS[0][:, lt:lt + 1], 0),
                       in_=xall[:, lt, :], in_offset=None,
                       bounds_check=E * CAP - 1, oob_is_err=False),
                   waits=[kiv, zfv] if lt == 0 else (), inc=gld, inc_k=16)
        scatv = S.now(gld)
        S.emit("gpsimd", lambda e: e.collective_compute(
            "ReduceScatter", ALU.add, replica_groups=[list(range(8))],
            ins=[dispD.ap()[0:E * 512, :]], outs=[dispRS1.ap()]),
            waits=[scatv, zfv], inc=cc)
        rsv1 = S.now(cc)
        S.emit("gpsimd", lambda e: e.collective_compute(
            "ReduceScatter", ALU.add, replica_groups=[list(range(8))],
            ins=[dispD.ap()[E * 512:E * CAP, :]], outs=[dispRS2.ap()]),
            inc=cc)
        rsv2 = S.now(cc)
        barrier(exclude=("cc", "gld"))

        w1h = [carve(f"w1h{i}", [128, DC, F // 2], BF16, i * 32768) for i in range(2)]
        w2h = [carve(f"w2h{i}", [128, F // 128, D // 2], BF16, 65536 + i * 32768)
               for i in range(2)]
        dtok5 = carve("dtok5", [128, 2, D], BF16, 131072)
        dft5 = carve("dft5", [128, DC, 256], BF16, 135168)
        hh5 = carve("hh5", [128, F // 128, 256], BF16, 139264)
        zt5 = carve("zt5", [128, DC, 256], F32, 155648)
        otok5 = carve("otok5", [128, 2, D], BF16, 163840)
        comb5 = carve("comb5", [128, D], F32, 0)
        combB = carve("combB", [128, D], BF16, 16384)
        x3f5 = [carve(f"x3f5_{i}", [128, D], F32, 4096 + i * 4096) for i in range(2)]
        sqt5 = carve("sqt5", [128, D], F32, 12288)
        p_tb5 = PT
        p_tf5 = [PF[3][:, 0:128], PF[4][:, 0:128]]
        p_f5 = [PF[1], PF[2]]
        wm = {}

        def load_w(exl, which, wait_marks):
            if which == "w1A":
                S.dma("gpsimd", w1h[0][:], w1d[exl][:, 0:F // 2].rearrange(
                    "(c p) f2 -> p c f2", p=128), waits=wait_marks, inc=gld)
            elif which == "w1B":
                S.dma("gpsimd", w1h[1][:], w1d[exl][:, F // 2:F].rearrange(
                    "(c p) f2 -> p c f2", p=128), waits=wait_marks, inc=gld)
            elif which == "w2A":
                S.dma("gpsimd", w2h[0][:], w2d[exl][:, 0:D // 2].rearrange(
                    "(g p) d2 -> p g d2", p=128), waits=wait_marks, inc=gld)
            else:
                S.dma("gpsimd", w2h[1][:], w2d[exl][:, D // 2:D].rearrange(
                    "(g p) d2 -> p g d2", p=128), waits=wait_marks, inc=gld)
            wm[(exl, which)] = S.now(gld)

        for which in ("w1A", "w1B", "w2A", "w2B"):
            load_w(0, which, [])
        agvA = None
        prev_ct_sto = None
        for exl in range(2):
            for ct in range(CAP // 256):
                if ct < 2:
                    srcv = dispRS1[exl * 512 + ct * 256: exl * 512 + (ct + 1) * 256, :]
                    rsw = rsv1
                else:
                    srcv = dispRS2[exl * 768 + (ct - 2) * 256: exl * 768 + (ct - 1) * 256, :]
                    rsw = rsv2
                S.dma("sync", dtok5[:], srcv.rearrange("(s p) d2 -> p s d2", p=128),
                      waits=[S.now(dve), rsw] + ([prev_ct_sto] if prev_ct_sto else []),
                      inc=ld)
                dtv = S.now(ld)
                tmarks = []
                for i in range(16):
                    sc, dcx = i // 8, i % 8
                    ptb = p_tb5[i % 2]
                    wt = [dtv] + ([tmarks[i - 2]] if i >= 2 else [S.now(dve)])
                    S.emit("tensor",
                           lambda e, sc=sc, dcx=dcx, ptb=ptb: e.transpose(
                               ptb[:], dtok5[:, sc, dcx * 128:(dcx + 1) * 128], id128b[:]),
                           waits=wt, inc=pe)
                    pv = S.now(pe)
                    S.emit("vector",
                           lambda e, sc=sc, dcx=dcx, ptb=ptb: e.tensor_copy(
                               dft5[:, dcx, sc * 128:(sc + 1) * 128], ptb[:]),
                           waits=[pv], inc=dve)
                    tmarks.append(S.now(dve))
                dftv = S.now(dve)
                for fg in range(F // 128):
                    pf = p_f5[fg % 2]
                    wz = []
                    if fg == 0:
                        wz = [wm[(exl, "w1A")], dftv]
                    elif fg == 16:
                        wz = [wm[(exl, "w1B")], S.now(act)]
                    elif fg >= 2:
                        wz = [S.now(act)]
                    wb = w1h[fg // 16]
                    fl = fg % 16
                    for dcx in range(DC):
                        S.emit("tensor",
                               lambda e, dcx=dcx, fl=fl, pf=pf, wb=wb: e.matmul(
                                   pf[:, 0:256], lhsT=wb[:, dcx, fl * 128:(fl + 1) * 128],
                                   rhs=dft5[:, dcx, :], start=(dcx == 0), stop=(dcx == DC - 1)),
                               waits=wz if dcx == 0 else (),
                               inc=pe if dcx == DC - 1 else None)
                    if exl == 0 and ct == CAP // 256 - 1 and fg in (15, 31):
                        load_w(1, "w1A" if fg == 15 else "w1B", [S.now(pe)])
                    pv = S.now(pe)
                    S.emit("scalar",
                           lambda e, fg=fg, pf=pf: e.activation(
                               hh5[:, fg, :], pf[:, 0:256], AF.Relu),
                           waits=[pv], inc=act)
                hv = S.now(act)
                for dcx in range(DC):
                    pf = p_f5[dcx % 2]
                    wb2 = w2h[dcx // 4]
                    dl = dcx % 4
                    for fg in range(F // 128):
                        wz = []
                        if fg == 0:
                            wz = [hv, wm[(exl, "w2A" if dcx < 4 else "w2B")]] +                                 ([S.now(dve)] if dcx >= 2 else [])
                        S.emit("tensor",
                               lambda e, dl=dl, fg=fg, pf=pf, wb2=wb2: e.matmul(
                                   pf[:, 0:256], lhsT=wb2[:, fg, dl * 128:(dl + 1) * 128],
                                   rhs=hh5[:, fg, :], start=(fg == 0), stop=(fg == F // 128 - 1)),
                               waits=wz,
                               inc=pe if fg == F // 128 - 1 else None)
                    if exl == 0 and ct == CAP // 256 - 1 and dcx in (3, 7):
                        load_w(1, "w2A" if dcx == 3 else "w2B", [S.now(pe)])
                    pv = S.now(pe)
                    S.emit("vector",
                           lambda e, dcx=dcx, pf=pf: e.tensor_copy(zt5[:, dcx, :], pf[:, 0:256]),
                           waits=[pv], inc=dve)
                ztv = S.now(dve)
                tmarks2 = []
                for i in range(16):
                    sc, dcx = i // 8, i % 8
                    ptf = p_tf5[i % 2]
                    wt = [ztv] + ([tmarks2[i - 2]] if i >= 2 else [])
                    S.emit("tensor",
                           lambda e, sc=sc, dcx=dcx, ptf=ptf: e.transpose(
                               ptf[:], zt5[:, dcx, sc * 128:(sc + 1) * 128], id128f[:]),
                           waits=wt, inc=pe)
                    pv = S.now(pe)
                    S.emit("vector",
                           lambda e, sc=sc, dcx=dcx, ptf=ptf: e.tensor_copy(
                               otok5[:, sc, dcx * 128:(dcx + 1) * 128], ptf[:]),
                           waits=[pv], inc=dve)
                    tmarks2.append(S.now(dve))
                otv = S.now(dve)
                eL = eoutLa if exl == 0 else eoutLb
                S.dma("sync", eL[ct * 256:(ct + 1) * 256, :].rearrange(
                    "(s p) d2 -> p s d2", p=128), otok5[:], waits=[otv], inc=sto)
                prev_ct_sto = S.now(sto)
            if exl == 0:
                S.emit("gpsimd", lambda e: e.collective_compute(
                    "AllGather", ALU.bypass, replica_groups=[list(range(8))],
                    ins=[eoutLa.ap()], outs=[eoutAG.ap()[0:8 * CAP, :]]),
                    waits=[S.now(sto)], inc=cc)
                agvA = S.now(cc)
        S.emit("gpsimd", lambda e: e.collective_compute(
            "AllGather", ALU.bypass, replica_groups=[list(range(8))],
            ins=[eoutLb.ap()], outs=[eoutAG.ap()[8 * CAP:E * CAP, :]]),
            waits=[S.now(sto)], inc=cc)
        agvB = S.now(cc)
        barrier(exclude=("cc",))
        stomarks5 = []
        for lt in range(16):
            par = lt % 2
            S.dma("sync", x3f5[par][:], x3tokD[lt * 128:(lt + 1) * 128, :],
                  waits=[S.now(dve)] if lt >= 2 else [], inc=ld)
            xv5 = S.now(ld)
            vstep(lambda e: e.memset(combB[:], 0.0))
            S.emit("gpsimd",
                   lambda e, lt=lt: e.indirect_dma_start(
                       out=combB[:], out_offset=None,
                       in_=eoutAG[0:128, :],
                       in_offset=bass.IndirectOffsetOnAxis(kidxC[0][:, lt:lt + 1], 0),
                       bounds_check=E * CAP - 1, oob_is_err=False),
                   waits=[agvA, agvB, S.now(dve)], inc=gld, inc_k=16)
            gv5 = S.now(gld)
            vstep(lambda e, lt=lt: e.tensor_scalar_mul(comb5[:], combB[:], psel[:, lt:lt + 1]),
                  extra=[gv5] + ([stomarks5[lt - 1]] if lt >= 1 else []))
            vstep(lambda e, par=par: e.tensor_tensor(comb5[:], comb5[:], x3f5[par][:], ALU.add),
                  extra=[xv5])
            yv5 = S.now(dve)
            vstep(lambda e: e.reduce_sum(rsn, comb5[:], axis=AX.X, negate=True))
            vstep(lambda e: e.tensor_scalar_mul(nmu, rsn, 1.0 / D))
            S.emit("scalar", lambda e: e.activation(sqt5[:], comb5[:], AF.Square,
                                                    accum_out=ssq),
                   waits=[yv5], inc=act)
            vstep(lambda e: e.tensor_scalar_mul(ex2, ssq, 1.0 / D), extra=[S.now(act)])
            vstep(lambda e: e.tensor_tensor(var, nmu, nmu, ALU.mult))
            vstep(lambda e: e.tensor_tensor(var, ex2, var, ALU.subtract))
            S.emit("scalar", lambda e: e.activation(var, var, AF.Ln, bias=eps128[:], scale=1.0),
                   waits=[S.now(dve)], inc=act)
            S.emit("scalar", lambda e: e.activation(var, var, AF.Exp, bias=0.0, scale=-0.5),
                   waits=[S.now(act)], inc=act)
            rv = S.now(act)
            vstep(lambda e: e.tensor_scalar(comb5[:], comb5[:], nmu, var, ALU.add, ALU.mult),
                  extra=[rv])
            ov = S.now(dve)
            S.dma("sync", x4tok[lt * 128:(lt + 1) * 128, :], comb5[:], waits=[ov], inc=sto)
            stomarks5.append(S.now(sto))
        if debug:
            S.dma("sync", dbg_ctx1, ag2in.ap(), waits=[S.now(sto)], inc=sto)
            S.dma("sync", dbg_x2, x2loc.ap(), waits=[S.now(sto)], inc=sto)
            S.dma("sync", dbg_x3, x3tokD.ap(), waits=[S.now(sto)], inc=sto)
            S.dma("sync", dbg_lg, lg[:], waits=[S.now(dve)], inc=sto)
            S.dma("sync", dbg_kf, psel[:], waits=[S.now(dve)], inc=sto)
        S.emit("sync", lambda e: e.nop(), waits=[S.now(sto)])
        S.build()
    return nc


def kernel_fused(**inputs):
    nc = _get("fused", build_fused)
    x = np.asarray(inputs["L0_attn_x"], dtype=np.float32)
    xf = x.reshape(T, D)
    g0T = _f(np.asarray(inputs["L0_ffn_ln_gamma"]).reshape(DC, 128).T)
    b0T = _f(np.asarray(inputs["L0_ffn_ln_beta"]).reshape(DC, 128).T)
    b1T = _f(np.asarray(inputs["L0_ffn_fc1_bias"]).reshape(F // 128, 128).T)
    b2T = _f(np.asarray(inputs["L0_ffn_fc2_bias"]).reshape(DC, 128).T)
    fc1q_h = np.ascontiguousarray(np.asarray(inputs["L0_ffn_fc1_q"], dtype=np.int32))
    fc2q_h = np.ascontiguousarray(np.asarray(inputs["L0_ffn_fc2_q"], dtype=np.int32))
    sc1_h = _f(inputs["L0_ffn_fc1_scales"])
    sc2_h = _f(inputs["L0_ffn_fc2_scales"])
    Wq = np.asarray(inputs["L1_attn_Wq"]); Wk = np.asarray(inputs["L1_attn_Wk"])
    Wv = np.asarray(inputs["L1_attn_Wv"]); Wo = np.asarray(inputs["L1_attn_Wo"])
    bq = np.asarray(inputs["L1_attn_bq"]); bk = np.asarray(inputs["L1_attn_bk"])
    bv = np.asarray(inputs["L1_attn_bv"]); bo = np.asarray(inputs["L1_attn_bo"])
    WoTh = _f(Wo.T)
    boT = _f(bo.reshape(DC, 128).T)
    gTh = _f(np.asarray(inputs["L1_attn_gamma"]).reshape(DC, 128).T)
    bTh = _f(np.asarray(inputs["L1_attn_beta"]).reshape(DC, 128).T)
    gwTh = _f(np.asarray(inputs["L1_ffn_gate_w"]).T)
    w1 = np.asarray(inputs["L1_ffn_w1"]); w2 = np.asarray(inputs["L1_ffn_w2"])
    tm = _l1_trimask()
    i16 = np.eye(16, dtype=np.float32)
    i128 = np.eye(128, dtype=np.float32)
    bimp_h = _f(np.asarray(inputs["L1_attn_band_importance"]).reshape(1, 2048))
    maps = []
    for c in range(8):
        sl = slice(c * TPC, (c + 1) * TPC)
        b, hg = c // 2, c % 2
        rs = slice(hg * 512, (hg + 1) * 512)
        UT, pm, io, iob, bigph = _l5_consts(c)
        maps.append(dict(
            xT=_f(xf[sl].T), xtok=_f(xf[sl]), trimask=tm,
            g0T=g0T, b0T=b0T, fc1q=fc1q_h, sc1=sc1_h, b1T=b1T,
            fc2q=fc2q_h, sc2=sc2_h, b2T=b2T,
            WqT=_f(Wq[rs, :].T), WkT=_f(Wk[rs, :].T), WvT=_f(Wv[rs, :].T),
            bqT=_f(bq[rs].reshape(4, 128).T), bkT=_f(bk[rs].reshape(4, 128).T),
            bvrow=_f(bv[rs].reshape(1, 512)), bimp=bimp_h, id16=i16,
            WoT=WoTh, boT=boT, gT=gTh, bT=bTh, gwT=gwTh,
            w1d=_f(np.stack([w1[c], w1[c + 8]])), w2d=_f(np.stack([w2[c], w2[c + 8]])),
            UTd=UT, pmaskd=pm, iota16d=io, iotaBIGd=iob, bigpd=bigph, idd=i128,
        ))
    res = run_bass_kernel_spmd(nc, maps, core_ids=list(range(8)))
    out = np.concatenate([np.asarray(r["x4tok"]) for r in res.results], axis=0)
    return np.ascontiguousarray(out.reshape(B, L, D), dtype=np.float32)


def _f(x):
    return np.ascontiguousarray(np.asarray(x), dtype=np.float32)


def _prep_l2_maps(x, ctx_, ln_g, ln_b, fc1_q, fc1_s, fc1_b, fc2_q, fc2_s, fc2_b):
    maps = []
    g0T = _f(np.asarray(ln_g).reshape(DC, 128).T)
    b0T = _f(np.asarray(ln_b).reshape(DC, 128).T)
    b1T = _f(np.asarray(fc1_b).reshape(F // 128, 128).T)
    b2T = _f(np.asarray(fc2_b).reshape(DC, 128).T)
    fc1q = np.ascontiguousarray(np.asarray(fc1_q, dtype=np.int32))
    fc2q = np.ascontiguousarray(np.asarray(fc2_q, dtype=np.int32))
    sc1 = _f(fc1_s)
    sc2 = _f(fc2_s)
    for c in range(8):
        sl = slice(c * TPC, (c + 1) * TPC)
        maps.append(dict(
            xT=_f(x[sl].T), ctxT=_f(ctx_[sl].T), g0T=g0T, b0T=b0T,
            fc1q=fc1q, sc1=sc1, b1T=b1T, fc2q=fc2q, sc2=sc2, b2T=b2T,
        ))
    return maps


_BUILT = {}


def _get(name, builder):
    if name not in _BUILT:
        _BUILT[name] = builder()
    return _BUILT[name]


def run_l2(x, ctx_, ln_g, ln_b, fc1_q, fc1_s, fc1_b, fc2_q, fc2_s, fc2_b):
    nc = _get("l2", build_l2)
    maps = _prep_l2_maps(x, ctx_, ln_g, ln_b, fc1_q, fc1_s, fc1_b, fc2_q, fc2_s, fc2_b)
    res = run_bass_kernel_spmd(nc, maps, core_ids=list(range(8)))
    return np.concatenate([r["x2T"].T for r in res.results], axis=0)


def kernel(**inputs):
    return kernel_fused(**inputs)


def kernel_unfused(**inputs):
    x = np.asarray(inputs["L0_attn_x"], dtype=np.float32)
    ctx1 = run_l1(x)
    xf = np.ascontiguousarray(x.reshape(T, D))
    x2 = run_l2(xf, ctx1.reshape(T, D),
                inputs["L0_ffn_ln_gamma"], inputs["L0_ffn_ln_beta"],
                inputs["L0_ffn_fc1_q"], inputs["L0_ffn_fc1_scales"], inputs["L0_ffn_fc1_bias"],
                inputs["L0_ffn_fc2_q"], inputs["L0_ffn_fc2_scales"], inputs["L0_ffn_fc2_bias"])
    ctx2 = run_l3(x2.reshape(B, L, D),
                  inputs["L1_attn_Wq"], inputs["L1_attn_bq"],
                  inputs["L1_attn_Wk"], inputs["L1_attn_bk"],
                  inputs["L1_attn_Wv"], inputs["L1_attn_bv"],
                  inputs["L1_attn_band_importance"])
    x3, lgts = run_l4(x2, ctx2.reshape(T, D),
                      inputs["L1_attn_Wo"], inputs["L1_attn_bo"],
                      inputs["L1_attn_gamma"], inputs["L1_attn_beta"],
                      inputs["L1_ffn_gate_w"])
    out = run_l5(x3, lgts, inputs["L1_ffn_w1"], inputs["L1_ffn_w2"])
    return np.ascontiguousarray(out.reshape(B, L, D), dtype=np.float32)
